# version 07 (final)
# speedup vs baseline: 1.1671x; 1.0188x over previous
import contextlib
import numpy as np
import concourse.bass as bass
import concourse.mybir as mybir
from concourse.bass_utils import run_bass_kernel_spmd

F32 = mybir.dt.float32
BF16 = mybir.dt.bfloat16
I32 = mybir.dt.int32
AF = mybir.ActivationFunctionType
ALU = mybir.AluOpType
AXX = mybir.AxisListType.X

NCORE = 8
D = 1024
NT = 16
TOK = NT * 128
EPS32 = float(np.finfo(np.float32).eps)
PAIRS = [[0, 1], [2, 3], [4, 5], [6, 7]]
FULLG = [list(range(NCORE))]

PHASES = ("l0", "moe", "l1a", "l1f")
TAPS = ()
DEBUG = ()
BANKED = "AB"

ENGS = ("tensor", "vector", "scalar", "gpsimd", "sync")
ESEM = {"tensor": "T", "vector": "V", "scalar": "S", "gpsimd": "G", "sync": "Y"}
ENG_SEMS = set(ESEM.values())


class Buf:
    __slots__ = ("t", "w", "readers", "name", "dsem", "shared")

    def __init__(self, t, name="", g=None):
        self.t = t
        self.name = name
        self.dsem = g or ("d_" + name)
        self.shared = g is not None
        self.w = None
        self.readers = []

    def __getitem__(self, idx):
        return self.t[idx]


class Emitter:
    def __init__(self):
        self.streams = {e: [] for e in ENGS}
        self.cnt = {s: 0 for s in ESEM.values()}
        self.hwm = {}

    def _prep(self, eng, reads, writes, dma, asyncamt=None):
        if dma or asyncamt:
            sem = writes[0].dsem + "@" + eng
            amt = asyncamt or 16
            self.cnt.setdefault(sem, 0)
        else:
            sem = ESEM[eng]
            amt = 1
        deps = []
        for b in reads:
            if b.w is not None:
                deps.append(b.w)
        for b in writes:
            if b.w is not None:
                if dma and not b.readers and b.w[0] == sem:
                    pass
                else:
                    deps.append(b.w)
            deps.extend(b.readers)
        waits = {}
        for (s, v) in deps:
            if s.startswith("#"):
                self.frozen = getattr(self, "frozen", set())
                self.frozen.add(s[1:])
                for q in ENGS:
                    nm = s[1:] + "@" + q
                    vv = self.cnt.get(nm, 0)
                    if vv and waits.get(nm, 0) < vv and self.hwm.get((eng, nm), 0) < vv:
                        waits[nm] = vv
                continue
            if self.hwm.get((eng, s), 0) >= v:
                continue
            if waits.get(s, 0) < v:
                waits[s] = v
        for s, v in waits.items():
            self.hwm[(eng, s)] = v
        self.dirty = True
        self.cnt[sem] += amt
        if dma or asyncamt:
            if writes[0].shared:
                after = ("#" + writes[0].dsem, 0)
            else:
                after = (sem, self.cnt[sem])
        else:
            after = (sem, self.cnt[sem])
        for b in writes:
            b.w = after
            b.readers = []
        for b in reads:
            b.readers.append(after)
        return list(waits.items()), sem, amt

    def emit(self, eng, fn, reads=(), writes=(), dma=False, asyncamt=None):
        waits, sem, amt = self._prep(eng, reads, writes, dma, asyncamt)
        self.streams[eng].append(("op", waits, fn, None, sem, amt))

    def emit2(self, eng, fnE, fnO, reads=(), writes=(), dma=False):
        waits, sem, amt = self._prep(eng, reads, writes, dma)
        self.streams[eng].append(("par", waits, fnE, fnO, sem, amt))

    def regload(self, eng, pfs_buf, ap):
        waits, sem, amt = self._prep(eng, (pfs_buf,), (), False)
        self.streams[eng].append(("reg", waits, ap, None, sem, amt))

    def barrier(self):
        if not getattr(self, "dirty", True):
            return
        self.dirty = False
        self.nbar = getattr(self, "nbar", 0) + 1
        snap = {s: v for s, v in self.cnt.items() if s != "B"}
        self.cnt.setdefault("B", 0)
        for e in ENGS:
            waits = []
            for s, v in snap.items():
                if s == ESEM[e] or v == 0:
                    continue
                if self.hwm.get((e, s), 0) < v:
                    waits.append((s, v))
                    self.hwm[(e, s)] = v
            self.streams[e].append(("op", waits, (lambda eng: eng.nop()), None, "B", 1))
        self.cnt["B"] += 5
        tgt = self.cnt["B"]
        for e in ENGS:
            self.streams[e].append(("op", [("B", tgt)], (lambda eng: eng.nop()), None,
                                    "B", 1))
            self.hwm[(e, "B")] = tgt
        self.cnt["B"] += 5

    def finalize_into(self, block, sems):
        for e in ENGS:
            dec = getattr(block, e)

            def body(eng, _ops=tuple(self.streams[e])):
                preg = None
                for kind, waits, a, b, sem, amt in _ops:
                    for (s, v) in waits:
                        eng.wait_ge(sems[s], v)
                    if kind == "reg":
                        preg = eng.alloc_register("par")
                        eng.reg_load(preg, a)
                        eng.nop().then_inc(sems[sem], amt)
                    elif kind == "op":
                        instr = a(eng)
                        if instr is None:
                            instr = eng.nop()
                        instr.then_inc(sems[sem], amt)
                    else:
                        with eng.If_eq(preg, 0):
                            a(eng).then_inc(sems[sem], amt)
                        with eng.If_eq(preg, 1):
                            b(eng).then_inc(sems[sem], amt)

            dec(body)


def lhsT_tiles(w):
    Kd, Md = w.shape
    nK, nM = Kd // 128, Md // 128
    return np.ascontiguousarray(
        w.reshape(nK, 128, nM, 128).transpose(2, 1, 0, 3).reshape(nM, 128, nK * 128))


def col_tiles(v):
    return np.ascontiguousarray(v.reshape(-1, 128).T)


SLOT_H = [4096, 2048, 3072, 4096, 2048, 3072, 4096, 2048]
SLOT_G = ["e0", "e1", "e2", "e0", "e1", "e2", "e0", "e1"]
SLOT_IDX = [0, 0, 0, 1, 1, 1, 2, 2]


def build_host_inputs(ins):
    d = {}
    wqkv = ins["L0_attn_w_qkv"]
    wq, wk, wv = wqkv[0:1024], wqkv[1024:2048], wqkv[2048:3072]
    perm = []
    for h in range(16):
        perm.extend(64 * h + np.arange(0, 64, 2))
    for h in range(16):
        perm.extend(64 * h + np.arange(1, 64, 2))
    perm = np.array(perm)
    d["w_q"] = lhsT_tiles(np.ascontiguousarray(wq[perm].T))
    d["w_k"] = lhsT_tiles(np.ascontiguousarray(wk[perm].T))
    d["w_v"] = lhsT_tiles(np.ascontiguousarray(wv.T))
    d["w_out_r"] = np.ascontiguousarray(ins["L0_attn_w_out"].T.reshape(8, 128, 1024))
    d["gam"] = col_tiles(ins["L0_attn_norm_weight"])
    pm = ins["L0_attn_persist_mem"]
    hperm = np.concatenate([np.arange(0, 64, 2), np.arange(1, 64, 2)])
    pmk = np.zeros((16, 65, 4), np.float32)
    pmv = np.zeros((16, 4, 65), np.float32)
    for h in range(16):
        pmk[h, 0:64] = pm[h][:, hperm].T
        pmk[h, 64] = 1.0
        pmv[h, :, 0:64] = pm[h]
        pmv[h, :, 64] = 1.0
    d["_pmk"] = pmk
    d["_pmv"] = pmv
    d["gw1"] = np.ascontiguousarray(ins["L0_ffn_gate_w1"].reshape(8, 128, 1024))
    d["gb1r"] = ins["L0_ffn_gate_b1"].reshape(1, 1024)
    d["lnsc"] = col_tiles(ins["L0_ffn_ln_scale"])
    d["lnoc"] = col_tiles(ins["L0_ffn_ln_offset"])
    d["gw2"] = lhsT_tiles(ins["L0_ffn_gate_w2"])
    d["gb2c"] = col_tiles(ins["L0_ffn_gate_b2"])
    d["gw3"] = np.ascontiguousarray(ins["L0_ffn_gate_w3"].reshape(4, 128, 8))
    d["gb3"] = ins["L0_ffn_gate_b3"].reshape(1, 8)
    d["aff_w"] = np.ascontiguousarray(ins["L0_ffn_aff_w"].reshape(8, 128, 8))
    d["aff_b"] = ins["L0_ffn_aff_b"].reshape(1, 8)
    d["usage"] = ins["L0_ffn_expert_usage"].reshape(1, 8)
    b2all = np.zeros((8, 1024), np.float32)
    for s in range(8):
        g, i = SLOT_G[s], SLOT_IDX[s]
        if g == "e0":
            d[f"s{s}_w1"] = lhsT_tiles(ins["L0_ffn_e0_w1"][i])
            d[f"s{s}_b1"] = col_tiles(ins["L0_ffn_e0_b1"][i])
            d[f"s{s}_wo"] = np.ascontiguousarray(ins["L0_ffn_e0_w2"][i].reshape(32, 128, 1024))
            b2all[s] = ins["L0_ffn_e0_b2"][i]
        elif g == "e1":
            d[f"s{s}_w1"] = lhsT_tiles(ins["L0_ffn_e1_w1"][i])
            d[f"s{s}_b1"] = col_tiles(ins["L0_ffn_e1_b1"][i])
            d[f"s{s}_w2"] = lhsT_tiles(ins["L0_ffn_e1_w2"][i])
            d[f"s{s}_b2"] = col_tiles(ins["L0_ffn_e1_b2"][i])
            d[f"s{s}_wo"] = np.ascontiguousarray(ins["L0_ffn_e1_w3"][i].reshape(16, 128, 1024))
            b2all[s] = ins["L0_ffn_e1_b3"][i]
        else:
            d[f"s{s}_w1"] = lhsT_tiles(ins["L0_ffn_e2_w1"][i])
            d[f"s{s}_b1"] = col_tiles(ins["L0_ffn_e2_b1"][i])
            d[f"s{s}_wo"] = np.ascontiguousarray(ins["L0_ffn_e2_w2"][i].reshape(24, 128, 1024))
            b2all[s] = ins["L0_ffn_e2_b2"][i]
    d["b2all"] = b2all
    d["l1_wq"] = lhsT_tiles(ins["L1_attn_Wq"])
    d["l1_wk"] = lhsT_tiles(ins["L1_attn_Wk"])
    d["l1_wv"] = lhsT_tiles(ins["L1_attn_Wv"])
    d["l1_wo"] = np.ascontiguousarray(ins["L1_attn_Wo"].reshape(8, 128, 1024))
    kqs = np.zeros((64, 16 * 64), np.float32)
    kks = np.zeros((64, 16 * 64), np.float32)
    for h in range(16):
        kqs[:, 64 * h:64 * (h + 1)] = ins["L1_attn_Kq"][h]
        kks[:, 64 * h:64 * (h + 1)] = ins["L1_attn_Kk"][h]
    d["l1_kq"] = kqs
    d["l1_kk"] = kks
    d["f_w1"] = lhsT_tiles(np.ascontiguousarray(ins["L1_ffn_w1_w"].T))
    d["f_w2"] = lhsT_tiles(np.ascontiguousarray(ins["L1_ffn_w2_w"].T))
    d["f_h1"] = lhsT_tiles(np.ascontiguousarray(ins["L1_ffn_h1_w"].T))
    d["f_h2"] = lhsT_tiles(np.ascontiguousarray(ins["L1_ffn_h2_w"].T))
    d["f_w3"] = lhsT_tiles(np.ascontiguousarray(ins["L1_ffn_w3_w"].T))
    d["f_s1"] = col_tiles(ins["L1_ffn_w1_s"])
    d["f_s2"] = col_tiles(ins["L1_ffn_w2_s"])
    d["f_sh1"] = col_tiles(ins["L1_ffn_h1_s"])
    d["f_sh2"] = col_tiles(ins["L1_ffn_h2_s"])
    d["f_s3"] = col_tiles(ins["L1_ffn_w3_s"])
    d["ident"] = np.eye(128, dtype=np.float32)
    m1 = np.zeros((4, 128, 512), np.float32)
    q = np.arange(128)[:, None]
    k = np.arange(512)[None, :]
    for dlt in range(4):
        m1[dlt] = np.where(k <= q + 128 * dlt, 0.0, -1e30)
    d["mask1"] = m1
    kk = np.arange(128)[:, None]
    qq = np.arange(128)[None, :]
    d["mask2"] = np.where(kk <= qq, 0.0, -1e30).astype(np.float32)
    d["mask01"] = np.where(kk <= qq, 1.0, 0.0).astype(np.float32)
    wfix = np.zeros((128, 8), np.float32)
    wfix[:, 0] = 0.125
    wfix[:, 1] = 0.125
    d["wfix"] = wfix
    return d


_NOISE = None


def moe_noise():
    global _NOISE
    if _NOISE is None:
        import jax
        import jax.numpy as jnp
        with jax.default_device(jax.devices("cpu")[0]):
            _NOISE = np.asarray(jax.random.normal(jax.random.key(1), (4, 4096, 8),
                                                  dtype=jnp.float32))
    return _NOISE


def rope_tables(pos0):
    inv = (1.0 / 10000.0 ** (np.arange(0, 64, 2, dtype=np.float32) / 64.0))
    pos = np.arange(pos0, pos0 + TOK, dtype=np.float32)
    fr = inv[:, None] * pos[None, :]
    return (np.tile(np.cos(fr), (4, 1)).astype(np.float32),
            np.tile(np.sin(fr), (4, 1)).astype(np.float32))


def build_nc():
    nc = bass.Bass("TRN2", target_bir_lowering=False)
    K = Emitter()

    X = Buf(nc.dram_tensor("x", [TOK, D], F32, kind="ExternalInput").ap(), "x")
    OUT = Buf(nc.dram_tensor("out", [TOK, D], F32, kind="ExternalOutput").ap(), "out",
              g="gout")
    PF = Buf(nc.dram_tensor("pflag", [1, 1], I32, kind="ExternalInput").ap(), "pflag")
    dr = {}

    def din(name, shape):
        dr[name] = Buf(nc.dram_tensor(name, list(shape), F32,
                                      kind="ExternalInput").ap(), name)

    din("w_q", (8, 128, 1024)); din("w_k", (8, 128, 1024)); din("w_v", (8, 128, 1024))
    din("w_out_r", (8, 128, 1024)); din("gam", (128, 8))
    din("pmk", (8, 65, 4)); din("pmv", (8, 4, 65))
    din("gw1", (8, 128, 1024)); din("gb1r", (1, 1024)); din("lnsc", (128, 8)); din("lnoc", (128, 8))
    din("gw2", (4, 128, 1024)); din("gb2c", (128, 4)); din("gw3", (4, 128, 8)); din("gb3", (1, 8))
    din("aff_w", (8, 128, 8)); din("aff_b", (1, 8)); din("usage", (1, 8))
    for s in range(8):
        H = SLOT_H[s]
        din(f"s{s}_w1", (H // 128, 128, 1024))
        din(f"s{s}_b1", (128, H // 128))
        if SLOT_G[s] == "e1":
            din(f"s{s}_w2", (16, 128, 2048)); din(f"s{s}_b2", (128, 16))
        din(f"s{s}_wo", (H // 128, 128, 1024))
    din("b2all", (8, 1024))
    din("l1_wq", (8, 128, 1024)); din("l1_wk", (8, 128, 1024)); din("l1_wv", (8, 128, 1024))
    din("l1_wo", (8, 128, 1024)); din("l1_kq", (64, 1024)); din("l1_kk", (64, 1024))
    din("f_w1", (32, 128, 1024)); din("f_w2", (32, 128, 1024))
    din("f_h1", (32, 128, 4096)); din("f_h2", (32, 128, 4096)); din("f_w3", (8, 128, 4096))
    din("f_s1", (128, 32)); din("f_s2", (128, 32)); din("f_sh1", (128, 32))
    din("f_sh2", (128, 32)); din("f_s3", (128, 8))
    din("ident", (128, 128)); din("mask1", (4, 128, 512)); din("mask2", (128, 128))
    din("mask01", (128, 128)); din("wfix", (128, 8)); din("noise", (TOK, 8))
    din("ropecos", (128, TOK)); din("ropesin", (128, TOK))

    def dint(name, shape, dt=F32, g=None):
        return Buf(nc.dram_tensor(name, list(shape), dt).ap(), name, g=g)

    AG1K = 16 * 65 * TOK
    AG1V = NT * 128 * 1024
    AG1Q = 16 * 64 * TOK
    AG1 = AG1K + AG1V + AG1Q
    ag1i = dint("ag1i", [AG1], BF16)
    ag1o = dint("ag1o", [2 * AG1], BF16)
    ag2i = dint("ag2i", [8, 4096, 64], BF16)
    ag2o = dint("ag2o", [2, 8, 4096, 64], BF16)
    ag3i = dint("ag3i", [1, TOK])
    ag3o = dint("ag3o", [NCORE, TOK])
    ag4i = dint("ag4i", [128, 8])
    ag4o = dint("ag4o", [128, 8])
    ag5i = dint("ag5i", [16, 8, 128, 65])
    ag5o = dint("ag5o", [2, 16, 8, 128, 65])
    yspill = dint("yspill", [NT, 128, 1024], BF16)
    stash = {}
    for s in range(8):
        H = SLOT_H[s]
        stash[f"s{s}_w1"] = dint(f"st_s{s}w1", [H // 128, 128, 1024], BF16, g="gstash")
        if SLOT_G[s] == "e1":
            stash[f"s{s}_w2"] = dint(f"st_s{s}w2", [16, 128, 2048], BF16, g="gstash")
        stash[f"s{s}_wo"] = dint(f"st_s{s}wo", [H // 128, 128, 1024], BF16, g="gstash")
    for nm, shp in (("f_w1", (32, 128, 1024)), ("f_w2", (32, 128, 1024)),
                    ("f_h1", (32, 128, 4096)), ("f_h2", (32, 128, 4096)),
                    ("f_w3", (8, 128, 4096))):
        stash[nm] = dint("st_" + nm, list(shp), BF16, g="gstash")
    taps = {}
    for t in TAPS:
        taps[t] = Buf(nc.dram_tensor("tap_" + t, [TOK, D], F32,
                                     kind="ExternalOutput").ap(), "tap" + t, g="gout")
    dbg = {}
    for nm, shp, dt in (("ag1i", [AG1], BF16), ("ag2i", [8 * 4096 * 64], BF16),
                        ("mall", [8, 128, 32], F32)):
        if nm in DEBUG:
            dbg[nm] = Buf(nc.dram_tensor("dbg_" + nm, shp, dt,
                                         kind="ExternalOutput").ap(), "dbg" + nm)

    es = contextlib.ExitStack()
    with es:
        def sbt(name, shape, dt, stack=None, g=None):
            return Buf((stack or es).enter_context(
                nc.sbuf_tensor(name, list(shape), dt)), name, g=g)

        resid = [sbt(f"res{t}", (128, D), F32, g="gx") for t in range(NT)]
        fmA = sbt("fmA", (128, 16384), BF16)
        tmA = sbt("tmA", (128, 16384), BF16)
        fs0 = sbt("fs0", (128, 2048), F32)
        fs1 = sbt("fs1", (128, 1024), F32)
        fs2 = sbt("fs2", (128, 1024), F32)
        fs3 = sbt("fs3", (128, 1024), F32)
        sbf0 = sbt("sbf0", (128, 4096), BF16)
        sbf1 = sbt("sbf1", (128, 2048), BF16)
        dump = sbt("dump", (128, 1024), BF16)
        stat = sbt("stat", (128, 80), F32)
        cst = sbt("cst", (128, 8), F32)
        idb = sbt("idb", (128, 128), BF16)
        idf = sbt("idf", (128, 128), F32)
        ones_b = sbt("ones_b", (1, 128), BF16)
        ones_bc = sbt("ones_bc", (128, 1), BF16)
        ones_f = sbt("ones_f", (128, 1), F32)
        ones_fr = sbt("ones_fr", (1, 128), F32)
        gamm = sbt("gamm", (128, 8), F32)
        pfs = sbt("pfs", (1, 1), I32)
        for _b in (idb, idf, gamm, pfs):
            _b.dsem = "gx"
            _b.shared = True
        psA = Buf(es.enter_context(nc.psum_tensor("psA", [128, 2048], F32)), "psA")
        psB = Buf(es.enter_context(nc.psum_tensor("psB", [128, 1536], F32)), "psB")
        psT = Buf(es.enter_context(nc.psum_tensor("psT", [128, 1024], BF16)), "psT")
        psAb = ([Buf(psA.t, f"psA_{i}") for i in range(4)]
                if "A" in BANKED else [psA] * 4)
        psBb = ([Buf(psB.t, f"psB_{i}") for i in range(3)]
                if "B" in BANKED else [psB] * 3)
        psTb = ([Buf(psT.t, f"psT_{i}") for i in range(2)]
                if "T" in BANKED else [psT] * 2)

        def bA(lo, hi):
            return psAb[lo // 512:(hi + 511) // 512]

        def bB(lo, hi):
            return psBb[lo // 512:(hi + 511) // 512]

        def bT(lo, hi):
            return psTb[lo // 512:(hi + 511) // 512]

        def dma(q, out, in_, reads=(), writes=()):
            K.emit(q, (lambda e, o=out, i=in_: e.dma_start(out=o, in_=i)),
                   reads=reads, writes=writes, dma=True)

        def dma2(q, outE, inE, outO, inO, reads=(), writes=()):
            K.emit2(q, (lambda e, o=outE, i=inE: e.dma_start(out=o, in_=i)),
                    (lambda e, o=outO, i=inO: e.dma_start(out=o, in_=i)),
                    reads=reads, writes=writes, dma=True)

        vop = lambda fn, reads=(), writes=(): K.emit("vector", fn, reads, writes)
        sop = lambda fn, reads=(), writes=(): K.emit("scalar", fn, reads, writes)
        top = lambda fn, reads=(), writes=(): K.emit("tensor", fn, reads, writes)
        gop = lambda fn, reads=(), writes=(): K.emit("gpsimd", fn, reads, writes)

        dma("sync", pfs[:, :], PF[:, :], reads=[PF], writes=[pfs])
        for e in ENGS:
            K.regload(e, pfs, pfs[0:1, 0:1])

        dma("gpsimd", idb[:, :], dr["ident"][:, :], reads=[dr["ident"]], writes=[idb])
        dma("sync", idf[:, :], dr["ident"][:, :], reads=[dr["ident"]], writes=[idf])
        dma("sync", gamm[:, :], dr["gam"][:, :], reads=[dr["gam"]], writes=[gamm])
        vop(lambda e: e.memset(ones_b[:, :], 1.0), writes=[ones_b])
        vop(lambda e: e.memset(ones_bc[:, :], 1.0), writes=[ones_bc])
        vop(lambda e: e.memset(ones_f[:, :], 1.0), writes=[ones_f])
        vop(lambda e: e.memset(ones_fr[:, :], 1.0), writes=[ones_fr])
        vop(lambda e: e.memset(cst[:, :], 0.0), writes=[cst])
        vop(lambda e: e.memset(cst[:, 1:2], 1e-5), writes=[cst])
        vop(lambda e: e.memset(cst[:, 2:3], EPS32 * 1e-5), writes=[cst])
        for t in range(NT):
            dma("sync", resid[t][:, :], X[128 * t:128 * (t + 1), :],
                reads=[X], writes=[resid[t]])

        def ln_stats(src_buf, src_ap, c0, rms_fuse=False):
            negm = stat[:, c0:c0 + 1]
            rr = stat[:, c0 + 1:c0 + 2]
            vop(lambda e: e.tensor_reduce(out=negm, in_=src_ap, axis=AXX, op=ALU.add,
                                          negate=True),
                reads=[src_buf], writes=[stat])
            vop(lambda e: e.tensor_scalar_mul(out=negm, in0=negm, scalar1=1.0 / D),
                reads=[stat], writes=[stat])
            sop(lambda e: e.activation(out=dump[:, :], in_=src_ap, func=AF.Square,
                                       scale=1.0, bias=negm, accum_out=rr),
                reads=[src_buf, stat], writes=[dump, stat])
            if rms_fuse:
                sc, bc = (1.0 + EPS32) / D, cst[:, 2:3]
            else:
                sc, bc = 1.0 / D, cst[:, 1:2]
            sop(lambda e, sc=sc, bc=bc: e.activation(out=rr, in_=rr, func=AF.Sqrt,
                                                     scale=sc, bias=bc),
                reads=[stat, cst], writes=[stat])
            vop(lambda e: e.reciprocal(out=rr, in_=rr), reads=[stat], writes=[stat])
            return negm, rr

        def mk_bias(negm, rr, c2):
            bu = stat[:, c2:c2 + 1]
            vop(lambda e: e.tensor_tensor(out=bu, in0=negm, in1=rr, op=ALU.mult),
                reads=[stat], writes=[stat])
            return bu

        def ln_add_resid(src_buf, src_ap, t):
            negm, rr = ln_stats(src_buf, src_ap, 64)
            bu = mk_bias(negm, rr, 66)
            sop(lambda e: e.activation(out=fs1[:, :], in_=src_ap, func=AF.Identity,
                                       scale=rr, bias=bu),
                reads=[src_buf, stat], writes=[fs1])
            vop(lambda e, t=t: e.tensor_tensor(out=resid[t][:, :], in0=resid[t][:, :],
                                               in1=fs1[:, :], op=ALU.add),
                reads=[fs1, resid[t]], writes=[resid[t]])

        def tp_tm_to_fm(src_buf, dst_buf, n_tiles=NT, scale_buf=None, dst_fm_stride=2048,
                        src_col=0, dst_col=0, reg_base=1024):
            for p in range(8):
                for g0 in range(0, n_tiles, 4):
                    n = min(4, n_tiles - g0)
                    rb = 512 * ((g0 // 4) % 2)
                    reg = psT[0:128, rb:rb + 512]
                    for j in range(n):
                        t = g0 + j
                        top(lambda e, t=t, p=p, j=j, reg=reg: e.transpose(
                            reg[:, 128 * j:128 * (j + 1)],
                            src_buf[:, src_col + 1024 * t + 128 * p:
                                    src_col + 1024 * t + 128 * (p + 1)], idb[:, :]),
                            reads=[src_buf, idb], writes=[psTb[rb // 512]])
                    c0 = dst_col + dst_fm_stride * p + 128 * g0
                    if scale_buf is not None:
                        sop(lambda e, c0=c0, n=n, reg=reg, p=p: e.activation(
                            out=dst_buf[:, c0:c0 + 128 * n], in_=reg[:, 0:128 * n],
                            func=AF.Identity, scale=scale_buf[:, p:p + 1],
                            bias=cst[:, 0:1]),
                            reads=[psTb[rb // 512], scale_buf, cst], writes=[dst_buf])
                    else:
                        sop(lambda e, c0=c0, n=n, reg=reg: e.copy(
                            out=dst_buf[:, c0:c0 + 128 * n], in_=reg[:, 0:128 * n]),
                            reads=[psTb[rb // 512]], writes=[dst_buf])

        def bcast(src_ap, dst_ap, reads, writes, scale=1.0):
            reg = psB[0:128, 1500:1501]
            top(lambda e: e.matmul(reg, ones_fr[0:1, :], src_ap, start=True, stop=True),
                reads=list(reads) + [ones_fr], writes=[psBb[2]])
            sop(lambda e: e.activation(out=dst_ap, in_=reg, func=AF.Identity,
                                       scale=scale, bias=cst[:, 0:1]),
                reads=[psBb[2], cst], writes=list(writes))

        if "l0" in PHASES:
            p0 = contextlib.ExitStack()
            pmk_s = sbt("pmk_s", (65, 32), BF16, p0)
            pmv_s = sbt("pmv_s", (4, 520), BF16, p0)
            m1s = sbt("m1s", (128, 2048), BF16, p0)
            m2s = sbt("m2s", (128, 128), BF16, p0)
            p0a = contextlib.ExitStack()
            ropc = sbt("ropc", (128, TOK), F32, p0a)
            rops = sbt("rops", (128, TOK), F32, p0a)
            wsl = [sbt(f"wsl{i}", (128, 1024), BF16, p0a) for i in range(4)]
            for _b in (ropc, rops, pmk_s, pmv_s, m1s, m2s):
                _b.dsem = "gp0"
                _b.shared = True
            dma("sync", ropc[:, :], dr["ropecos"][:, :], reads=[dr["ropecos"]], writes=[ropc])
            dma("sync", rops[:, :], dr["ropesin"][:, :], reads=[dr["ropesin"]], writes=[rops])
            for u in range(8):
                dma("gpsimd", pmk_s[0:65, 4 * u:4 * (u + 1)], dr["pmk"][u],
                    reads=[dr["pmk"]], writes=[pmk_s])
                dma("gpsimd", pmv_s[0:4, 65 * u:65 * (u + 1)], dr["pmv"][u],
                    reads=[dr["pmv"]], writes=[pmv_s])
            for dlt in range(4):
                dma("gpsimd", m1s[:, 512 * dlt:512 * (dlt + 1)], dr["mask1"][dlt],
                    reads=[dr["mask1"]], writes=[m1s])
            dma("gpsimd", m2s[:, :], dr["mask2"][:, :], reads=[dr["mask2"]], writes=[m2s])

            for t in range(NT):
                negm, rr = ln_stats(resid[t], resid[t][:, :], 4 * t, rms_fuse=True)
                bu = mk_bias(negm, rr, 4 * t + 2)
                sop(lambda e, t=t, rr=rr, bu=bu: e.activation(
                    out=tmA[:, 1024 * t:1024 * (t + 1)], in_=resid[t][:, :],
                    func=AF.Identity, scale=rr, bias=bu),
                    reads=[resid[t], stat], writes=[tmA])
            tp_tm_to_fm(tmA, fmA, scale_buf=gamm)

            def ufm(p):
                return fmA[:, 2048 * p:2048 * (p + 1)]

            vop(lambda e: e.memset(sbf0[0:1, 0:2048], 1.0), writes=[sbf0])
            ag1k = ag1i[0:AG1K].rearrange("(h r c) -> h r c", h=16, r=65, c=TOK)
            ag1v = ag1i[AG1K:AG1K + AG1V].rearrange("(t p d) -> t p d", t=NT, p=128, d=1024)
            ag1q = ag1i[AG1K + AG1V:AG1].rearrange("(h r c) -> h r c", h=16, r=64, c=TOK)
            for h in range(16):
                dma("sync", ag1k[h, 64:65, :], sbf0[0:1, 0:TOK],
                    reads=[sbf0], writes=[ag1i])

            for m in range(8):
                sl = wsl[m % 2]
                dma("gpsimd", sl[:, :], dr["w_v"][m], reads=[dr["w_v"]], writes=[sl])
                for kk in range(8):
                    for tb in range(4):
                        top(lambda e, sl=sl, kk=kk, tb=tb: e.matmul(
                            psA[0:128, 512 * tb:512 * (tb + 1)],
                            sl[:, 128 * kk:128 * (kk + 1)],
                            ufm(kk)[:, 512 * tb:512 * (tb + 1)],
                            start=(kk == 0), stop=(kk == 7)),
                            reads=[sl, fmA], writes=[psAb[tb]])
                for tb in range(4):
                    sop(lambda e, tb=tb: e.copy(
                        out=sbf1[:, 512 * tb:512 * (tb + 1)],
                        in_=psA[0:128, 512 * tb:512 * (tb + 1)]),
                        reads=[psAb[tb]], writes=[sbf1])
                for g0 in range(0, 16, 4):
                    rb = 512 * ((g0 // 4) % 2)
                    reg = psT[0:128, rb:rb + 512]
                    for j in range(4):
                        t = g0 + j
                        top(lambda e, t=t, j=j, reg=reg: e.transpose(
                            reg[:, 128 * j:128 * (j + 1)],
                            sbf1[:, 128 * t:128 * (t + 1)], idb[:, :]),
                            reads=[sbf1, idb], writes=[psTb[rb // 512]])
                    sop(lambda e, g0=g0, reg=reg: e.copy(
                        out=sbf0[:, 2048 + 512 * (g0 // 4):2048 + 512 * (g0 // 4) + 512],
                        in_=reg),
                        reads=[psTb[rb // 512]], writes=[sbf0])
                for t in range(NT):
                    dma("sync", ag1v[t, :, 128 * m:128 * (m + 1)],
                        sbf0[:, 2048 + 128 * t:2048 + 128 * (t + 1)],
                        reads=[sbf0], writes=[ag1i])

            for qk, wname, agsec, scl in (("q", "w_q", ag1q, 0.125), ("k", "w_k", ag1k, 1.0)):
                for pr in range(4):
                    sle, slo = wsl[2], wsl[3]
                    dma("gpsimd", sle[:, :], dr[wname][pr], reads=[dr[wname]], writes=[sle])
                    dma("gpsimd", slo[:, :], dr[wname][pr + 4], reads=[dr[wname]], writes=[slo])
                    for hh in range(2):
                        for par_i, sl, dstf in ((0, sle, fs2), (1, slo, fs3)):
                            for tb in range(2):
                                rb = 512 * ((2 * par_i + tb) % 3)
                                reg = psB[0:128, rb:rb + 512]
                                for kk in range(8):
                                    top(lambda e, sl=sl, kk=kk, reg=reg, hh=hh, tb=tb: e.matmul(
                                        reg, sl[:, 128 * kk:128 * (kk + 1)],
                                        ufm(kk)[:, 1024 * hh + 512 * tb:1024 * hh + 512 * (tb + 1)],
                                        start=(kk == 0), stop=(kk == 7)),
                                        reads=[sl, fmA], writes=[psBb[rb // 512]])
                                sop(lambda e, reg=reg, dstf=dstf, tb=tb, scl=scl: e.activation(
                                    out=dstf[:, 512 * tb:512 * (tb + 1)], in_=reg,
                                    func=AF.Identity, scale=scl, bias=cst[:, 0:1]),
                                    reads=[psBb[rb // 512], cst], writes=[dstf])
                        cs_ = ropc[:, 1024 * hh:1024 * (hh + 1)]
                        sn_ = rops[:, 1024 * hh:1024 * (hh + 1)]
                        vop(lambda e, cs_=cs_: e.tensor_tensor(out=fs0[:, 0:1024], in0=fs2[:, :], in1=cs_, op=ALU.mult),
                            reads=[fs2, ropc], writes=[fs0])
                        vop(lambda e, sn_=sn_: e.tensor_tensor(out=fs1[:, :], in0=fs3[:, :], in1=sn_, op=ALU.mult),
                            reads=[fs3, rops], writes=[fs1])
                        vop(lambda e: e.tensor_tensor(out=sbf1[:, 0:1024], in0=fs0[:, 0:1024], in1=fs1[:, :], op=ALU.subtract),
                            reads=[fs0, fs1], writes=[sbf1])
                        vop(lambda e, cs_=cs_: e.tensor_tensor(out=fs0[:, 1024:2048], in0=fs3[:, :], in1=cs_, op=ALU.mult),
                            reads=[fs3, ropc], writes=[fs0])
                        vop(lambda e, sn_=sn_: e.tensor_tensor(out=fs1[:, :], in0=fs2[:, :], in1=sn_, op=ALU.mult),
                            reads=[fs2, rops], writes=[fs1])
                        vop(lambda e: e.tensor_tensor(out=sbf1[:, 1024:2048], in0=fs0[:, 1024:2048], in1=fs1[:, :], op=ALU.add),
                            reads=[fs0, fs1], writes=[sbf1])
                        for i in range(4):
                            h = 4 * pr + i
                            dma("sync", agsec[h, 0:32, 1024 * hh:1024 * (hh + 1)],
                                sbf1[32 * i:32 * (i + 1), 0:1024],
                                reads=[sbf1], writes=[ag1i])
                            dma("sync", agsec[h, 32:64, 1024 * hh:1024 * (hh + 1)],
                                sbf1[32 * i:32 * (i + 1), 1024:2048],
                                reads=[sbf1], writes=[ag1i])

            if "ag1i" in DEBUG:
                dma("sync", dbg["ag1i"][:], ag1i[:], reads=[ag1i], writes=[dbg["ag1i"]])
            K.emit("gpsimd", lambda e: e.collective_compute(
                "AllGather", ALU.bypass, replica_groups=PAIRS,
                ins=[ag1i[:]], outs=[ag1o[:]]),
                reads=[ag1i], writes=[ag1o], asyncamt=1)

            K.barrier()
            p0a.close()
            go1 = ag1o[:].rearrange("(s f) -> s f", s=2, f=AG1)
            KOF, QOF, VOF, PT0, PT1, YOF = 0, 4096, 8192, 10272, 12320, 14368
            khat = Buf(fmA.t, "khat")
            qhat = Buf(fmA.t, "qhat")
            vhat = Buf(fmA.t, "vhat")
            pT = [Buf(fmA.t, "pT0"), Buf(fmA.t, "pT1")]
            yst = Buf(fmA.t, "yst")
            for u in range(8):
                for s_ in range(2):
                    def ksrc(h, s_=s_):
                        return go1[s_, 65 * TOK * h:65 * TOK * (h + 1)].rearrange(
                            "(a b) -> a b", a=65, b=TOK)
                    dma2("sync", fmA[0:65, KOF + 2048 * s_:KOF + 2048 * (s_ + 1)], ksrc(u),
                         fmA[0:65, KOF + 2048 * s_:KOF + 2048 * (s_ + 1)], ksrc(u + 8),
                         reads=[ag1o], writes=[khat])

                    def qsrc(h, s_=s_):
                        return go1[s_, AG1K + AG1V + 64 * TOK * h:
                                   AG1K + AG1V + 64 * TOK * (h + 1)].rearrange(
                            "(a b) -> a b", a=64, b=TOK)
                    dma2("sync", fmA[0:64, QOF + 2048 * s_:QOF + 2048 * (s_ + 1)], qsrc(u),
                         fmA[0:64, QOF + 2048 * s_:QOF + 2048 * (s_ + 1)], qsrc(u + 8),
                         reads=[ag1o], writes=[qhat])
                vop(lambda e: e.memset(fmA[:, VOF:VOF + 2080], 1.0), writes=[vhat])
                for kt in range(32):
                    s_ = kt // 16
                    tloc = kt % 16

                    def vsrc(h, s_=s_, tloc=tloc):
                        return go1[s_, AG1K + 1024 * 128 * tloc:
                                   AG1K + 1024 * 128 * (tloc + 1)].rearrange(
                            "(p d) -> p d", p=128, d=1024)[:, 64 * h:64 * (h + 1)]
                    dma2("sync", fmA[:, VOF + 65 * kt:VOF + 65 * kt + 64], vsrc(u),
                         fmA[:, VOF + 65 * kt:VOF + 65 * kt + 64], vsrc(u + 8),
                         reads=[ag1o], writes=[vhat])
                for qt in range(32):
                    ncc = qt // 4 + 1
                    for cc in range(ncc):
                        rb = 512 * (cc % 3)
                        reg = psA[0:128, rb:rb + 512]
                        diag = (cc == qt // 4)
                        top(lambda e, qt=qt, cc=cc, reg=reg, diag=diag: e.matmul(
                            reg, fmA[0:64, QOF + 128 * qt:QOF + 128 * (qt + 1)],
                            fmA[0:64, KOF + 512 * cc:KOF + 512 * (cc + 1)],
                            start=True, stop=not diag),
                            reads=[qhat, khat], writes=[psAb[rb // 512]])
                        if diag:
                            top(lambda e, qt=qt, reg=reg: e.matmul(
                                reg, idb[:, :], m1s[:, 512 * (qt % 4):512 * (qt % 4 + 1)],
                                start=False, stop=True),
                                reads=[idb, m1s], writes=[psAb[rb // 512]])
                        if cc == 0:
                            vop(lambda e, qt=qt, reg=reg: e.tensor_reduce(
                                out=fs1[:, qt:qt + 1], in_=reg, axis=AXX, op=ALU.max),
                                reads=[psAb[rb // 512]], writes=[fs1])
                        else:
                            vop(lambda e, reg=reg: e.tensor_reduce(
                                out=fs2[:, 0:1], in_=reg, axis=AXX, op=ALU.max),
                                reads=[psAb[rb // 512]], writes=[fs2])
                            vop(lambda e, qt=qt: e.tensor_tensor(
                                out=fs1[:, qt:qt + 1], in0=fs1[:, qt:qt + 1],
                                in1=fs2[:, 0:1], op=ALU.max),
                                reads=[fs2, fs1], writes=[fs1])
                    regp = psA[0:128, 1536:1540]
                    top(lambda e, qt=qt, u=u, regp=regp: e.matmul(
                        regp, fmA[0:64, QOF + 128 * qt:QOF + 128 * (qt + 1)],
                        pmk_s[0:64, 4 * u:4 * (u + 1)], start=True, stop=True),
                        reads=[qhat, pmk_s], writes=[psAb[3]])
                    vop(lambda e, regp=regp: e.tensor_reduce(
                        out=fs2[:, 0:1], in_=regp, axis=AXX, op=ALU.max),
                        reads=[psAb[3]], writes=[fs2])
                    vop(lambda e, qt=qt: e.tensor_tensor(
                        out=fs1[:, qt:qt + 1], in0=fs1[:, qt:qt + 1], in1=fs2[:, 0:1],
                        op=ALU.max),
                        reads=[fs2, fs1], writes=[fs1])
                vop(lambda e: e.tensor_scalar_mul(out=fs1[:, 0:32], in0=fs1[:, 0:32],
                                                  scalar1=-1.0),
                    reads=[fs1], writes=[fs1])
                if "mall" in DEBUG:
                    dma("sync", dbg["mall"][u], fs1[:, 0:32], reads=[fs1],
                        writes=[dbg["mall"]])
                top(lambda e: e.transpose(psB[0:32, 0:128], fs1[:, 0:32], idf[:, :]),
                    reads=[fs1, idf], writes=[psBb[0]])
                sop(lambda e: e.copy(out=sbf1[0:32, 0:128], in_=psB[0:32, 0:128]),
                    reads=[psBb[0]], writes=[sbf1])
                dma("scalar", fmA[64:65, QOF:QOF + 4096], sbf1[0:32, 0:128],
                    reads=[sbf1], writes=[qhat])
                for (q0, q1) in ((0, 12), (12, 24), (24, 32)):
                    W = 128 * (q1 - q0)
                    nch = (W + 511) // 512
                    pbase = QOF + 128 * q0
                    for j in range(nch):
                        lo, hi = 512 * j, min(512 * (j + 1), W)
                        top(lambda e, lo=lo, hi=hi, u=u, pbase=pbase: e.matmul(
                            psA[0:4, lo:hi],
                            pmk_s[0:65, 4 * u:4 * (u + 1)],
                            fmA[0:65, pbase + lo:pbase + hi],
                            start=True, stop=True),
                            reads=[pmk_s, qhat], writes=bA(lo, hi))
                    sop(lambda e, W=W: e.activation(out=sbf0[0:4, 0:W], in_=psA[0:4, 0:W],
                                               func=AF.Exp),
                        reads=bA(0, W), writes=[sbf0])
                    for j in range(nch):
                        lo, hi = 512 * j, min(512 * (j + 1), W)
                        top(lambda e, lo=lo, hi=hi, u=u: e.matmul(
                            psB[0:65, lo:hi],
                            pmv_s[0:4, 65 * u:65 * (u + 1)],
                            sbf0[0:4, lo:hi], start=True, stop=False),
                            reads=[pmv_s, sbf0], writes=bB(lo, hi))
                    for kt in range(q1):
                        a = max(128 * (kt - q0), 0)
                        pt = pT[kt % 2]
                        PTO = PT0 if kt % 2 == 0 else PT1
                        j0 = a // 512
                        for j in range(j0, nch):
                            lo = max(a, 512 * j)
                            hi = min(512 * (j + 1), W)
                            dloc = 128 * (kt - q0)
                            diag = (kt >= q0) and (lo <= dloc < hi)
                            top(lambda e, kt=kt, lo=lo, hi=hi, pbase=pbase, diag=diag: e.matmul(
                                psA[0:128, lo:hi],
                                fmA[0:65, KOF + 128 * kt:KOF + 128 * (kt + 1)],
                                fmA[0:65, pbase + lo:pbase + hi],
                                start=True, stop=not diag),
                                reads=[khat, qhat], writes=bA(lo, hi))
                            if diag:
                                top(lambda e, dloc=dloc: e.matmul(
                                    psA[0:128, dloc:dloc + 128], idb[:, :], m2s[:, :],
                                    start=False, stop=True),
                                    reads=[idb, m2s], writes=bA(dloc, dloc + 128))
                        sop(lambda e, a=a, PTO=PTO, W=W: e.activation(
                            out=fmA[:, PTO + a:PTO + W], in_=psA[0:128, a:W],
                            func=AF.Exp),
                            reads=bA(a, W), writes=[pt])
                        for j in range(j0, nch):
                            lo = max(a, 512 * j)
                            hi = min(512 * (j + 1), W)
                            stop = (kt == min(q0 + 4 * (j + 1), q1) - 1)
                            top(lambda e, kt=kt, lo=lo, hi=hi, PTO=PTO, stop=stop: e.matmul(
                                psB[0:65, lo:hi],
                                fmA[:, VOF + 65 * kt:VOF + 65 * (kt + 1)],
                                fmA[:, PTO + lo:PTO + hi], start=False, stop=stop),
                                reads=[vhat, pt], writes=bB(lo, hi))
                    sop(lambda e, W=W: e.copy(out=fs0[0:65, 0:W], in_=psB[0:65, 0:W]),
                        reads=bB(0, W), writes=[fs0])
                    for g in range(q1 - q0):
                        rb = 1024 + 128 * (g % 4)
                        reg = psA[0:128, rb:rb + 65]
                        top(lambda e, g=g, reg=reg: e.transpose(
                            reg, fs0[0:65, 128 * g:128 * (g + 1)], idf[0:65, 0:65]),
                            reads=[fs0, idf], writes=[psAb[2]])
                        vop(lambda e, reg=reg: e.reciprocal(out=fs1[:, 32:33],
                                                            in_=reg[:, 64:65]),
                            reads=[psAb[2]], writes=[fs1])
                        sop(lambda e, reg=reg: e.activation(
                            out=fmA[:, YOF:YOF + 64], in_=reg[:, 0:64],
                            func=AF.Identity, scale=fs1[:, 32:33], bias=cst[:, 0:1]),
                            reads=[psAb[2], fs1, cst], writes=[yst])
                        dma("scalar",
                            ag2i[u, 128 * (q0 + g):128 * (q0 + g + 1), :],
                            fmA[:, YOF:YOF + 64], reads=[yst], writes=[ag2i])
            if "ag2i" in DEBUG:
                dma("sync", dbg["ag2i"][:], ag2i[:, :, :].rearrange("a b c -> (a b c)"),
                    reads=[ag2i], writes=[dbg["ag2i"]])
            K.emit("gpsimd", lambda e: e.collective_compute(
                "AllGather", ALU.bypass, replica_groups=PAIRS,
                ins=[ag2i[:, :, :]], outs=[ag2o[:, :, :, :]]),
                reads=[ag2i], writes=[ag2o], asyncamt=1)

            K.barrier()
            wor = sbt("wor", (128, 8192), BF16, p0)
            for kk in range(8):
                dma("gpsimd", wor[:, 1024 * kk:1024 * (kk + 1)], dr["w_out_r"][kk],
                    reads=[dr["w_out_r"]], writes=[wor])
            for t in range(NT):
                for h in range(16):
                    s_, u_ = h // 8, h % 8

                    def ysrc(par, s_=s_, u_=u_, t=t):
                        return ag2o[s_, u_, 2048 * par + 128 * t:2048 * par + 128 * (t + 1), :]
                    dma2("sync", tmA[:, 1024 * t + 64 * h:1024 * t + 64 * (h + 1)], ysrc(0),
                         tmA[:, 1024 * t + 64 * h:1024 * t + 64 * (h + 1)], ysrc(1),
                         reads=[ag2o], writes=[tmA])
            tp_tm_to_fm(tmA, fmA)
            for t in range(NT):
                for nh in range(2):
                    rb = 512 * ((2 * t + nh) % 3)
                    reg = psB[0:128, rb:rb + 512]
                    for kk in range(8):
                        top(lambda e, kk=kk, t=t, nh=nh, reg=reg: e.matmul(
                            reg, fmA[:, 2048 * kk + 128 * t:2048 * kk + 128 * (t + 1)],
                            wor[:, 1024 * kk + 512 * nh:1024 * kk + 512 * (nh + 1)],
                            start=(kk == 0), stop=(kk == 7)),
                            reads=[fmA, wor], writes=[psBb[rb // 512]])
                    sop(lambda e, nh=nh, reg=reg: e.copy(
                        out=fs0[:, 512 * nh:512 * (nh + 1)], in_=reg),
                        reads=[psBb[rb // 512]], writes=[fs0])
                ln_add_resid(fs0, fs0[:, 0:1024], t)
            if "x1" in TAPS:
                for t in range(NT):
                    dma("sync", taps["x1"][128 * t:128 * (t + 1), :], resid[t][:, :],
                        reads=[resid[t]], writes=[taps["x1"]])
            K.barrier()
            p0.close()

        if "moe" in PHASES:
            pm_ = contextlib.ExitStack()
            impv = sbt("impv", (128, 16), F32, pm_)
            absc = sbt("absc", (128, 16), F32, pm_)
            muc = sbt("muc", (128, 16), F32, pm_)
            impg = sbt("impg", (128, 128), F32, pm_)
            sc1 = sbt("sc1", (1, 16), F32, pm_)
            sc2 = sbt("sc2", (1, 16), F32, pm_)
            scb = sbt("scb", (128, 8), F32, pm_)
            wfx = sbt("wfx", (128, 8), F32, pm_)
            wfxs = sbt("wfxs", (128, 8), F32, pm_)
            noi = sbt("noi", (128, 128), F32, pm_)
            ptm = sbt("ptm", (128, 128), F32, pm_)
            p8 = sbt("p8", (8, 2048), BF16, pm_)
            sumc = sbt("sumc", (128, 8), F32, pm_)
            affws = sbt("affws", (128, 64), F32, pm_)
            b3f = sbt("b3f", (1, 8), F32, pm_)
            us8 = sbt("us8", (1, 8), F32, pm_)
            u8b = sbt("u8b", (1, 8), F32, pm_)
            gb2b = sbt("gb2b", (128, 4), F32, pm_)
            lnob = sbt("lnob", (128, 8), BF16, pm_)
            for _b in (wfx, noi):
                _b.dsem = "gpm"
                _b.shared = True
            dma("sync", wfx[:, :], dr["wfix"][:, :], reads=[dr["wfix"]], writes=[wfx])
            for t in range(NT):
                dma("sync", noi[:, 8 * t:8 * (t + 1)],
                    dr["noise"][128 * t:128 * (t + 1), :],
                    reads=[dr["noise"]], writes=[noi])
            for t in range(NT):
                negm, rr = ln_stats(resid[t], resid[t][:, :], 4 * t)
                bu = mk_bias(negm, rr, 4 * t + 2)
                sop(lambda e, t=t, rr=rr, bu=bu: e.activation(
                    out=dump[:, :], in_=resid[t][:, :], func=AF.Square,
                    scale=rr, bias=bu, accum_out=impv[:, t:t + 1]),
                    reads=[resid[t], stat], writes=[dump, impv])
                sop(lambda e, t=t, rr=rr, bu=bu: e.activation(
                    out=fs0[:, 0:1024], in_=resid[t][:, :], func=AF.Identity,
                    scale=rr, bias=bu),
                    reads=[resid[t], stat], writes=[fs0])
                vop(lambda e, t=t: e.tensor_reduce(
                    out=absc[:, t:t + 1], in_=fs0[:, 0:1024], axis=AXX, op=ALU.add,
                    apply_absolute_value=True),
                    reads=[fs0], writes=[absc])
            sop(lambda e: e.activation(out=impv[:, :], in_=impv[:, :], func=AF.Sqrt),
                reads=[impv], writes=[impv])
            for t in range(NT):
                dma("sync", ag3i[0:1, 128 * t:128 * (t + 1)], impv[:, t:t + 1],
                    reads=[impv], writes=[ag3i])
            K.emit("gpsimd", lambda e: e.collective_compute(
                "AllGather", ALU.bypass, replica_groups=FULLG,
                ins=[ag3i[:, :]], outs=[ag3o[:, :]]),
                reads=[ag3i], writes=[ag3o], asyncamt=1)
            dma("sync", impg[:, :],
                ag3o[:, :].rearrange("a (b c) -> (a b) c", b=16, c=128),
                reads=[ag3o], writes=[impg])
            vop(lambda e: e.tensor_reduce(out=stat[:, 72:73], in_=impg[:, :],
                                          axis=AXX, op=ALU.add),
                reads=[impg], writes=[stat])
            reg = psB[0:1, 1494:1495]
            top(lambda e, reg=reg: e.matmul(reg, stat[:, 72:73], ones_f[:, :],
                                            start=True, stop=True),
                reads=[stat, ones_f], writes=[psBb[2]])
            sop(lambda e, reg=reg: e.activation(out=sc1[0:1, 0:1], in_=reg,
                                                func=AF.Identity, scale=-1.0 / 16384.0,
                                                bias=cst[0:1, 0:1]),
                reads=[psBb[2], cst], writes=[sc1])
            bcast(sc1[0:1, 0:1], stat[:, 73:74], [sc1], [stat])
            sop(lambda e: e.activation(out=dump[:, 0:128], in_=impg[:, :], func=AF.Square,
                                       scale=1.0, bias=stat[:, 73:74],
                                       accum_out=stat[:, 72:73]),
                reads=[impg, stat], writes=[dump, stat])
            top(lambda e, reg=reg: e.matmul(reg, stat[:, 72:73], ones_f[:, :],
                                            start=True, stop=True),
                reads=[stat, ones_f], writes=[psBb[2]])
            sop(lambda e, reg=reg: e.activation(out=sc1[0:1, 1:2], in_=reg, func=AF.Sqrt,
                                                scale=1.0 / 16384.0, bias=cst[0:1, 0:1]),
                reads=[psBb[2], cst], writes=[sc1])
            vop(lambda e: e.tensor_scalar(out=sc1[0:1, 2:3], in0=sc1[0:1, 1:2],
                                          scalar1=0.1, scalar2=1.0, op0=ALU.mult,
                                          op1=ALU.add),
                reads=[sc1], writes=[sc1])
            bcast(sc1[0:1, 2:3], scb[:, 0:1], [sc1], [scb])
            vop(lambda e: e.memset(sc2[0:1, 0:1], 0.0), writes=[sc2])
            vop(lambda e: e.memset(sc2[0:1, 1:2], 32.0), writes=[sc2])
            for it in range(60):
                vop(lambda e: e.tensor_tensor(out=sc2[0:1, 2:3], in0=sc2[0:1, 0:1],
                                              in1=sc2[0:1, 1:2], op=ALU.add),
                    reads=[sc2], writes=[sc2])
                vop(lambda e: e.tensor_scalar_mul(out=sc2[0:1, 2:3], in0=sc2[0:1, 2:3],
                                                  scalar1=0.5),
                    reads=[sc2], writes=[sc2])
                bcast(sc2[0:1, 2:3], stat[:, 73:74], [sc2], [stat])
                vop(lambda e: e.tensor_scalar(out=fs1[:, 0:128], in0=impg[:, :],
                                              scalar1=stat[:, 73:74], scalar2=None,
                                              op0=ALU.is_ge),
                    reads=[impg, stat], writes=[fs1])
                vop(lambda e: e.tensor_reduce(out=stat[:, 72:73], in_=fs1[:, 0:128],
                                              axis=AXX, op=ALU.add),
                    reads=[fs1], writes=[stat])
                top(lambda e, reg=reg: e.matmul(reg, stat[:, 72:73], ones_f[:, :],
                                                start=True, stop=True),
                    reads=[stat, ones_f], writes=[psBb[2]])
                sop(lambda e, reg=reg: e.copy(out=sc2[0:1, 3:4], in_=reg),
                    reads=[psBb[2]], writes=[sc2])
                vop(lambda e: e.tensor_scalar(out=sc2[0:1, 4:5], in0=sc2[0:1, 3:4],
                                              scalar1=1640.0, scalar2=None,
                                              op0=ALU.is_ge),
                    reads=[sc2], writes=[sc2])
                vop(lambda e: e.tensor_tensor(out=sc2[0:1, 5:6], in0=sc2[0:1, 2:3],
                                              in1=sc2[0:1, 0:1], op=ALU.subtract),
                    reads=[sc2], writes=[sc2])
                vop(lambda e: e.tensor_tensor(out=sc2[0:1, 5:6], in0=sc2[0:1, 5:6],
                                              in1=sc2[0:1, 4:5], op=ALU.mult),
                    reads=[sc2], writes=[sc2])
                vop(lambda e: e.tensor_tensor(out=sc2[0:1, 0:1], in0=sc2[0:1, 0:1],
                                              in1=sc2[0:1, 5:6], op=ALU.add),
                    reads=[sc2], writes=[sc2])
                vop(lambda e: e.tensor_scalar(out=sc2[0:1, 6:7], in0=sc2[0:1, 4:5],
                                              scalar1=-1.0, scalar2=1.0, op0=ALU.mult,
                                              op1=ALU.add),
                    reads=[sc2], writes=[sc2])
                vop(lambda e: e.tensor_tensor(out=sc2[0:1, 5:6], in0=sc2[0:1, 2:3],
                                              in1=sc2[0:1, 1:2], op=ALU.subtract),
                    reads=[sc2], writes=[sc2])
                vop(lambda e: e.tensor_tensor(out=sc2[0:1, 5:6], in0=sc2[0:1, 5:6],
                                              in1=sc2[0:1, 6:7], op=ALU.mult),
                    reads=[sc2], writes=[sc2])
                vop(lambda e: e.tensor_tensor(out=sc2[0:1, 1:2], in0=sc2[0:1, 1:2],
                                              in1=sc2[0:1, 5:6], op=ALU.add),
                    reads=[sc2], writes=[sc2])
            bcast(sc2[0:1, 1:2], scb[:, 1:2], [sc2], [scb])
            for t in range(NT):
                vop(lambda e, t=t: e.tensor_scalar(out=muc[:, t:t + 1],
                                                   in0=impv[:, t:t + 1],
                                                   scalar1=scb[:, 1:2], scalar2=None,
                                                   op0=ALU.is_ge),
                    reads=[impv, scb], writes=[muc])
            vop(lambda e: e.tensor_scalar(out=absc[:, :], in0=absc[:, :], scalar1=1e-8,
                                          scalar2=None, op0=ALU.add),
                reads=[absc], writes=[absc])
            for t in range(NT):
                vop(lambda e, t=t: e.tensor_tensor(out=absc[:, t:t + 1],
                                                   in0=absc[:, t:t + 1], in1=scb[:, 0:1],
                                                   op=ALU.mult),
                    reads=[absc, scb], writes=[absc])
            vop(lambda e: e.reciprocal(out=absc[:, :], in_=absc[:, :]),
                reads=[absc], writes=[absc])
            vop(lambda e: e.tensor_tensor(out=muc[:, :], in0=muc[:, :], in1=absc[:, :],
                                          op=ALU.mult),
                reads=[absc, muc], writes=[muc])
            for t in range(NT):
                vop(lambda e, t=t: e.tensor_tensor(out=stat[:, 4 * t + 3:4 * t + 4],
                                                   in0=stat[:, 4 * t:4 * t + 1],
                                                   in1=stat[:, 4 * t + 1:4 * t + 2],
                                                   op=ALU.mult),
                    reads=[stat], writes=[stat])
                vop(lambda e, t=t: e.tensor_tensor(out=stat[:, 4 * t + 3:4 * t + 4],
                                                   in0=stat[:, 4 * t + 3:4 * t + 4],
                                                   in1=muc[:, t:t + 1], op=ALU.mult),
                    reads=[stat, muc], writes=[stat])
                vop(lambda e, t=t: e.tensor_tensor(out=stat[:, 4 * t + 2:4 * t + 3],
                                                   in0=stat[:, 4 * t + 1:4 * t + 2],
                                                   in1=muc[:, t:t + 1], op=ALU.mult),
                    reads=[stat, muc], writes=[stat])
                sop(lambda e, t=t: e.activation(
                    out=tmA[:, 1024 * t:1024 * (t + 1)], in_=resid[t][:, :],
                    func=AF.Identity, scale=stat[:, 4 * t + 2:4 * t + 3],
                    bias=stat[:, 4 * t + 3:4 * t + 4]),
                    reads=[resid[t], stat], writes=[tmA])
            K.barrier()
            tp_tm_to_fm(tmA, fmA)
            for p in range(8):
                vop(lambda e, p=p: e.tensor_reduce(
                    out=sumc[:, p:p + 1], in_=fmA[:, 2048 * p:2048 * (p + 1)],
                    axis=AXX, op=ALU.add),
                    reads=[fmA], writes=[sumc])
            dma("sync", ag4i[:, :], sumc[:, :], reads=[sumc], writes=[ag4i])
            K.emit("gpsimd", lambda e: e.collective_compute(
                "AllReduce", ALU.add, replica_groups=PAIRS,
                ins=[ag4i[:, :]], outs=[ag4o[:, :]]),
                reads=[ag4i], writes=[ag4o], asyncamt=1)
            dma("sync", sumc[:, :], ag4o[:, :], reads=[ag4o], writes=[sumc])
            for p in range(8):
                dma("sync", affws[:, 8 * p:8 * (p + 1)], dr["aff_w"][p],
                    reads=[dr["aff_w"]], writes=[affws])
            rega = psB[0:1, 1480:1488]
            for p in range(8):
                top(lambda e, p=p, rega=rega: e.matmul(
                    rega, sumc[:, p:p + 1], affws[:, 8 * p:8 * (p + 1)],
                    start=(p == 0), stop=(p == 7)),
                    reads=[sumc, affws], writes=[psBb[2]])
            sop(lambda e, rega=rega: e.activation(out=b3f[0:1, :], in_=rega,
                                                  func=AF.Identity, scale=0.3 / 4096.0,
                                                  bias=cst[0:1, 0:1]),
                reads=[psBb[2], cst], writes=[b3f])
            dma("sync", us8[0:1, :], dr["aff_b"][:, :], reads=[dr["aff_b"]], writes=[us8])
            vop(lambda e: e.tensor_scalar_mul(out=us8[0:1, :], in0=us8[0:1, :], scalar1=0.3),
                reads=[us8], writes=[us8])
            vop(lambda e: e.tensor_tensor(out=b3f[0:1, :], in0=b3f[0:1, :], in1=us8[0:1, :],
                                          op=ALU.add),
                reads=[us8, b3f], writes=[b3f])
            dma("sync", us8[0:1, :], dr["gb3"][:, :], reads=[dr["gb3"]], writes=[us8])
            vop(lambda e: e.tensor_tensor(out=b3f[0:1, :], in0=b3f[0:1, :], in1=us8[0:1, :],
                                          op=ALU.add),
                reads=[us8, b3f], writes=[b3f])
            dma("sync", us8[0:1, :], dr["usage"][:, :], reads=[dr["usage"]], writes=[us8])
            vop(lambda e: e.tensor_scalar(out=u8b[0:1, :], in0=us8[0:1, :], scalar1=1e-8,
                                          scalar2=None, op0=ALU.add),
                reads=[us8], writes=[u8b])
            vop(lambda e: e.tensor_reduce(out=sc1[0:1, 3:4], in_=u8b[0:1, :], axis=AXX,
                                          op=ALU.min),
                reads=[u8b], writes=[sc1])
            vop(lambda e: e.tensor_scalar(out=sc1[0:1, 4:5], in0=sc1[0:1, 3:4],
                                          scalar1=0.0, scalar2=None, op0=ALU.is_lt),
                reads=[sc1], writes=[sc1])
            vop(lambda e: e.tensor_scalar(out=u8b[0:1, :], in0=u8b[0:1, :], scalar1=1e-30,
                                          scalar2=None, op0=ALU.max),
                reads=[u8b], writes=[u8b])
            sop(lambda e: e.activation(out=u8b[0:1, :], in_=u8b[0:1, :], func=AF.Ln),
                reads=[u8b], writes=[u8b])
            vop(lambda e: e.tensor_tensor(out=u8b[0:1, :], in0=u8b[0:1, :], in1=us8[0:1, :],
                                          op=ALU.mult),
                reads=[us8, u8b], writes=[u8b])
            vop(lambda e: e.tensor_reduce(out=sc1[0:1, 5:6], in_=u8b[0:1, :], axis=AXX,
                                          op=ALU.add, negate=True),
                reads=[u8b], writes=[sc1])
            vop(lambda e: e.tensor_scalar(out=sc1[0:1, 6:7], in0=sc1[0:1, 5:6],
                                          scalar1=-0.01, scalar2=0.02, op0=ALU.mult,
                                          op1=ALU.add),
                reads=[sc1], writes=[sc1])
            vop(lambda e: e.tensor_scalar(out=sc1[0:1, 6:7], in0=sc1[0:1, 6:7],
                                          scalar1=1e-4, scalar2=None, op0=ALU.max),
                reads=[sc1], writes=[sc1])
            bcast(sc1[0:1, 6:7], scb[:, 2:3], [sc1], [scb])
            bcast(sc1[0:1, 4:5], scb[:, 3:4], [sc1], [scb])
            vop(lambda e: e.tensor_scalar(out=scb[:, 4:5], in0=scb[:, 3:4], scalar1=-1.0,
                                          scalar2=1.0, op0=ALU.mult, op1=ALU.add),
                reads=[scb], writes=[scb])
            vop(lambda e: e.tensor_scalar(out=wfxs[:, :], in0=wfx[:, :],
                                          scalar1=scb[:, 3:4], scalar2=None,
                                          op0=ALU.mult),
                reads=[wfx, scb], writes=[wfxs])
            pm_g = contextlib.ExitStack()
            gw1s = sbt("gw1s", (128, 8192), BF16, pm_g)
            gw2s = sbt("gw2s", (128, 4096), BF16, pm_g)
            gw3s = sbt("gw3s", (128, 32), BF16, pm_g)
            gb1b = sbt("gb1b", (1, 1024), BF16, pm_g)
            b3b = sbt("b3b", (1, 8), BF16, pm_g)
            lnx = sbt("lnx", (128, 16), F32, pm_g)
            bias2 = sbt("bias2", (128, 4), F32, pm_g)
            for _b in (gw1s, gw2s, gw3s, gb1b, lnx, gb2b):
                _b.dsem = "gpg"
                _b.shared = True
            for kk in range(8):
                dma("gpsimd", gw1s[:, 1024 * kk:1024 * (kk + 1)], dr["gw1"][kk],
                    reads=[dr["gw1"]], writes=[gw1s])
            for m in range(4):
                dma("gpsimd", gw2s[:, 1024 * m:1024 * (m + 1)], dr["gw2"][m],
                    reads=[dr["gw2"]], writes=[gw2s])
            for cc4 in range(4):
                dma("gpsimd", gw3s[:, 8 * cc4:8 * (cc4 + 1)], dr["gw3"][cc4],
                    reads=[dr["gw3"]], writes=[gw3s])
            dma("gpsimd", gb1b[0:1, :], dr["gb1r"][:, :], reads=[dr["gb1r"]], writes=[gb1b])
            dma("sync", gb2b[:, :], dr["gb2c"][:, :], reads=[dr["gb2c"]], writes=[gb2b])
            dma("sync", lnx[:, 0:8], dr["lnoc"][:, :], reads=[dr["lnoc"]], writes=[lnx])
            dma("sync", lnx[:, 8:16], dr["lnsc"][:, :], reads=[dr["lnsc"]], writes=[lnx])
            sop(lambda e: e.copy(out=lnob[:, :], in_=lnx[:, 0:8]),
                reads=[lnx], writes=[lnob])
            regc = psB[0:128, 1490:1491]
            for m in range(4):
                for cc8 in range(8):
                    top(lambda e, m=m, cc8=cc8: e.matmul(
                        regc, gw2s[:, 1024 * m + 128 * cc8:1024 * m + 128 * (cc8 + 1)],
                        lnob[:, cc8:cc8 + 1], start=(cc8 == 0), stop=(cc8 == 7)),
                        reads=[gw2s, lnob], writes=[psBb[2]])
                sop(lambda e, m=m: e.copy(out=bias2[:, m:m + 1], in_=regc),
                    reads=[psBb[2]], writes=[bias2])
            vop(lambda e: e.tensor_tensor(out=bias2[:, :], in0=bias2[:, :],
                                          in1=gb2b[:, :], op=ALU.add),
                reads=[gb2b, bias2], writes=[bias2])
            for m in range(4):
                for kk in range(8):
                    vop(lambda e, m=m, kk=kk: e.tensor_scalar(
                        out=gw2s[:, 1024 * m + 128 * kk:1024 * m + 128 * (kk + 1)],
                        in0=gw2s[:, 1024 * m + 128 * kk:1024 * m + 128 * (kk + 1)],
                        scalar1=lnx[:, 8 + kk:9 + kk], scalar2=None, op0=ALU.mult),
                        reads=[gw2s, lnx], writes=[gw2s])
            sop(lambda e: e.copy(out=b3b[0:1, :], in_=b3f[0:1, :]),
                reads=[b3f], writes=[b3b])
            for blk in range(4):
                for ti in range(4):
                    t = 4 * blk + ti
                    reg = psB[0:128, 0:1024]
                    for kk in range(8):
                        for nh in range(2):
                            top(lambda e, kk=kk, t=t, nh=nh, reg=reg: e.matmul(
                                reg[:, 512 * nh:512 * (nh + 1)],
                                fmA[:, 2048 * kk + 128 * t:2048 * kk + 128 * (t + 1)],
                                gw1s[:, 1024 * kk + 512 * nh:1024 * kk + 512 * (nh + 1)],
                                start=(kk == 0), stop=False),
                                reads=[fmA, gw1s], writes=[psBb[0], psBb[1]])
                    for nh in range(2):
                        top(lambda e, nh=nh, reg=reg: e.matmul(
                            reg[:, 512 * nh:512 * (nh + 1)], ones_b[0:1, :],
                            gb1b[0:1, 512 * nh:512 * (nh + 1)], start=False, stop=True),
                            reads=[ones_b, gb1b], writes=[psBb[0], psBb[1]])
                    sop(lambda e, reg=reg: e.activation(out=fs0[:, 0:1024], in_=reg,
                                                        func=AF.Silu),
                        reads=[psBb[0], psBb[1]], writes=[fs0])
                    negm, rr = ln_stats(fs0, fs0[:, 0:1024], 68)
                    bu = mk_bias(negm, rr, 70)
                    sop(lambda e, rr=rr, bu=bu: e.activation(
                        out=sbf1[:, 0:1024], in_=fs0[:, 0:1024], func=AF.Identity,
                        scale=rr, bias=bu),
                        reads=[fs0, stat], writes=[sbf1])
                    for p in range(8):
                        rb = 128 * (p % 8)
                        regt = psT[0:128, rb:rb + 128]
                        top(lambda e, p=p, regt=regt: e.transpose(
                            regt, sbf1[:, 128 * p:128 * (p + 1)], idb[:, :]),
                            reads=[sbf1, idb], writes=[psTb[(p % 8) // 4]])
                        sop(lambda e, p=p, ti=ti, regt=regt: e.copy(
                            out=sbf0[:, 512 * p + 128 * ti:512 * p + 128 * (ti + 1)],
                            in_=regt),
                            reads=[psTb[(p % 8) // 4]], writes=[sbf0])
                for m in range(4):
                    rb = 1024 * (m % 2)
                    reg2 = psA[0:128, rb:rb + 512]
                    for kk in range(8):
                        top(lambda e, m=m, kk=kk, reg2=reg2: e.matmul(
                            reg2, gw2s[:, 1024 * m + 128 * kk:1024 * m + 128 * (kk + 1)],
                            sbf0[:, 512 * kk:512 * (kk + 1)],
                            start=(kk == 0), stop=(kk == 7)),
                            reads=[gw2s, sbf0], writes=[psAb[rb // 512]])
                    sop(lambda e, m=m, reg2=reg2: e.activation(
                        out=sbf1[:, 512 * m:512 * (m + 1)], in_=reg2, func=AF.Silu,
                        scale=1.0, bias=bias2[:, m:m + 1]),
                        reads=[psAb[rb // 512]], writes=[sbf1])
                for ti in range(4):
                    t = 4 * blk + ti
                    regl = psB[0:128, 1024 + 8 * (ti % 2):1024 + 8 * (ti % 2) + 8]
                    for cc4 in range(4):
                        top(lambda e, cc4=cc4, ti=ti, regl=regl: e.matmul(
                            regl, sbf1[:, 512 * cc4 + 128 * ti:512 * cc4 + 128 * (ti + 1)],
                            gw3s[:, 8 * cc4:8 * (cc4 + 1)], start=(cc4 == 0), stop=False),
                            reads=[sbf1, gw3s], writes=[psBb[2]])
                    top(lambda e, regl=regl: e.matmul(
                        regl, ones_fr[0:1, :], b3f[0:1, :], start=False, stop=True),
                        reads=[ones_fr, b3f], writes=[psBb[2]])
                    vop(lambda e, t=t, regl=regl: e.scalar_tensor_tensor(
                        out=fs1[:, 0:8], in0=noi[:, 8 * t:8 * (t + 1)],
                        scalar=scb[:, 2:3], in1=regl, op0=ALU.mult, op1=ALU.add),
                        reads=[noi, scb, psBb[2]], writes=[fs1])
                    vop(lambda e: e.tensor_reduce(out=fs1[:, 8:9], in_=fs1[:, 0:8],
                                                  axis=AXX, op=ALU.max),
                        reads=[fs1], writes=[fs1])
                    vop(lambda e: e.tensor_scalar(out=fs1[:, 16:24], in0=fs1[:, 0:8],
                                                  scalar1=fs1[:, 8:9], scalar2=None,
                                                  op0=ALU.is_ge),
                        reads=[fs1], writes=[fs1])
                    vop(lambda e: e.scalar_tensor_tensor(
                        out=fs1[:, 16:24], in0=fs1[:, 16:24], scalar=-1e30,
                        in1=fs1[:, 0:8], op0=ALU.mult, op1=ALU.add),
                        reads=[fs1], writes=[fs1])
                    vop(lambda e: e.tensor_reduce(out=fs1[:, 9:10], in_=fs1[:, 16:24],
                                                  axis=AXX, op=ALU.max),
                        reads=[fs1], writes=[fs1])
                    vop(lambda e: e.tensor_scalar(out=fs1[:, 16:24], in0=fs1[:, 0:8],
                                                  scalar1=fs1[:, 9:10], scalar2=None,
                                                  op0=ALU.is_ge),
                        reads=[fs1], writes=[fs1])
                    vop(lambda e: e.tensor_scalar_mul(out=fs1[:, 10:11], in0=fs1[:, 8:9],
                                                      scalar1=-1.0 / 1.5),
                        reads=[fs1], writes=[fs1])
                    sop(lambda e: e.activation(out=fs1[:, 24:32], in_=fs1[:, 0:8],
                                               func=AF.Exp, scale=1.0 / 1.5,
                                               bias=fs1[:, 10:11]),
                        reads=[fs1], writes=[fs1])
                    vop(lambda e: e.tensor_tensor(out=fs1[:, 24:32], in0=fs1[:, 24:32],
                                                  in1=fs1[:, 16:24], op=ALU.mult),
                        reads=[fs1], writes=[fs1])
                    vop(lambda e: e.tensor_reduce(out=fs1[:, 11:12], in_=fs1[:, 24:32],
                                                  axis=AXX, op=ALU.add),
                        reads=[fs1], writes=[fs1])
                    vop(lambda e: e.reciprocal(out=fs1[:, 11:12], in_=fs1[:, 11:12]),
                        reads=[fs1], writes=[fs1])
                    vop(lambda e: e.tensor_scalar(out=fs1[:, 24:32], in0=fs1[:, 24:32],
                                                  scalar1=fs1[:, 11:12], scalar2=None,
                                                  op0=ALU.mult),
                        reads=[fs1], writes=[fs1])
                    vop(lambda e, t=t: e.tensor_scalar(out=ptm[:, 8 * t:8 * (t + 1)],
                                                       in0=fs1[:, 24:32],
                                                       scalar1=scb[:, 4:5], scalar2=None,
                                                       op0=ALU.mult),
                        reads=[fs1, scb], writes=[ptm])
                    vop(lambda e, t=t: e.tensor_tensor(out=ptm[:, 8 * t:8 * (t + 1)],
                                                       in0=ptm[:, 8 * t:8 * (t + 1)],
                                                       in1=wfxs[:, :], op=ALU.add),
                        reads=[wfxs, ptm], writes=[ptm])
                    sop(lambda e, t=t: e.copy(out=sbf1[:, 1920:1928],
                                              in_=ptm[:, 8 * t:8 * (t + 1)]),
                        reads=[ptm], writes=[sbf1])
                    regt = psT[0:8, 896:1024]
                    top(lambda e, regt=regt: e.transpose(
                        regt, sbf1[:, 1920:1928], idb[:, :]),
                        reads=[sbf1, idb], writes=[psTb[1]])
                    sop(lambda e, t=t, regt=regt: e.copy(
                        out=p8[0:8, 128 * t:128 * (t + 1)], in_=regt),
                        reads=[psTb[1]], writes=[p8])
            K.barrier()
            pm_g.close()
            b2as = sbt("b2as", (8, 1024), BF16, pm_)
            acc = [sbt(f"acc{i}", (128, 1024), F32, pm_) for i in range(4)]
            esl = [sbt(f"esl{i}", (128, 2048), BF16, pm_) for i in range(2)]
            ewo = [sbt(f"ewo{i}", (128, 1024), BF16, pm_) for i in range(2)]
            sgb = sbt("sgb", (128, 512), BF16, pm_)
            b2as.dsem = "geb"
            b2as.shared = True
            dma("gpsimd", b2as[0:8, :], dr["b2all"][:, :], reads=[dr["b2all"]], writes=[b2as])
            eb1 = {}
            for s in range(8):
                eb1[s] = sbt(f"eb1_{s}", (128, SLOT_H[s] // 128), F32, pm_, g="geb")
                dma("sync", eb1[s][:, :], dr[f"s{s}_b1"][:, :],
                    reads=[dr[f"s{s}_b1"]], writes=[eb1[s]])
                if SLOT_G[s] == "e1":
                    eb1[(s, 2)] = sbt(f"eb2_{s}", (128, 16), F32, pm_, g="geb")
                    dma("sync", eb1[(s, 2)][:, :], dr[f"s{s}_b2"][:, :],
                        reads=[dr[f"s{s}_b2"]], writes=[eb1[(s, 2)]])
            for blk in range(4):
                for st in range(4):
                    t = 4 * blk + st
                    rb = 1024 * (st % 2)
                    rega2 = psA[0:128, rb:rb + 1024]
                    for nh in range(2):
                        top(lambda e, t=t, nh=nh, rega2=rega2: e.matmul(
                            rega2[:, 512 * nh:512 * (nh + 1)],
                            p8[0:8, 128 * t:128 * (t + 1)],
                            b2as[0:8, 512 * nh:512 * (nh + 1)], start=True, stop=True),
                            reads=[p8, b2as], writes=bA(rb, rb + 1024))
                    vop(lambda e, t=t: e.tensor_scalar_mul(out=stat[:, 72:73],
                                                           in0=stat[:, 4 * t + 2:4 * t + 3],
                                                           scalar1=0.1),
                        reads=[stat], writes=[stat])
                    vop(lambda e, t=t: e.tensor_scalar_mul(out=stat[:, 73:74],
                                                           in0=stat[:, 4 * t + 3:4 * t + 4],
                                                           scalar1=0.1),
                        reads=[stat], writes=[stat])
                    vop(lambda e, t=t: e.tensor_scalar(out=fs0[:, 0:1024],
                                                       in0=resid[t][:, :],
                                                       scalar1=stat[:, 72:73],
                                                       scalar2=stat[:, 73:74],
                                                       op0=ALU.mult, op1=ALU.add),
                        reads=[resid[t], stat], writes=[fs0])
                    vop(lambda e, st=st, rega2=rega2: e.tensor_tensor(
                        out=acc[st][:, :], in0=fs0[:, 0:1024], in1=rega2, op=ALU.add),
                        reads=[fs0] + bA(rb, rb + 1024), writes=[acc[st]])
                for s in range(8):
                    g = SLOT_G[s]
                    nH = SLOT_H[s] // 128
                    w1src = dr[f"s{s}_w1"] if blk == 0 else stash[f"s{s}_w1"]
                    w1q = "gpsimd" if blk == 0 else "sync"
                    for m in range(nH):
                        sl = esl[m % 2]
                        dma(w1q, sl[:, 0:1024], w1src[m], reads=[w1src], writes=[sl])
                        if blk == 0:
                            dma("scalar", stash[f"s{s}_w1"][m], sl[:, 0:1024],
                                reads=[sl], writes=[stash[f"s{s}_w1"]])
                        rb = 512 * (m % 3)
                        regs1 = psB[0:128, rb:rb + 512]
                        for kk in range(8):
                            top(lambda e, sl=sl, kk=kk, blk=blk, regs1=regs1: e.matmul(
                                regs1, sl[:, 128 * kk:128 * (kk + 1)],
                                fmA[:, 2048 * kk + 512 * blk:2048 * kk + 512 * (blk + 1)],
                                start=(kk == 0), stop=(kk == 7)),
                                reads=[sl, fmA], writes=[psBb[rb // 512]])
                        if g == "e0":
                            sop(lambda e, s=s, m=m, regs1=regs1: e.activation(
                                out=tmA[:, 512 * m:512 * (m + 1)], in_=regs1,
                                func=AF.Gelu_apprx_tanh, scale=1.0,
                                bias=eb1[s][:, m:m + 1]),
                                reads=[psBb[rb // 512], eb1[s]], writes=[tmA])
                        elif g == "e1":
                            sop(lambda e, s=s, m=m, regs1=regs1: e.activation(
                                out=tmA[:, 512 * m:512 * (m + 1)], in_=regs1,
                                func=AF.Silu, scale=1.0, bias=eb1[s][:, m:m + 1]),
                                reads=[psBb[rb // 512], eb1[s]], writes=[tmA])
                        else:
                            sop(lambda e, s=s, m=m, regs1=regs1: e.activation(
                                out=fs1[:, 0:512], in_=regs1, func=AF.Silu, scale=1.0,
                                bias=eb1[s][:, m:m + 1]),
                                reads=[psBb[rb // 512], eb1[s]], writes=[fs1])
                            vop(lambda e, s=s, m=m: e.tensor_scalar_mul(
                                out=fs0[:, 0:1], in0=eb1[s][:, m:m + 1], scalar1=-1.0),
                                reads=[eb1[s]], writes=[fs0])
                            sop(lambda e, regs1=regs1: e.activation(
                                out=fs0[:, 512:1024], in_=regs1, func=AF.Exp,
                                scale=-1.0, bias=fs0[:, 0:1]),
                                reads=[psBb[rb // 512], fs0], writes=[fs0])
                            vop(lambda e: e.tensor_scalar(
                                out=fs0[:, 512:1024], in0=fs0[:, 512:1024],
                                scalar1=1.0, scalar2=None, op0=ALU.add),
                                reads=[fs0], writes=[fs0])
                            vop(lambda e: e.reciprocal(out=fs0[:, 512:1024],
                                                       in_=fs0[:, 512:1024]),
                                reads=[fs0], writes=[fs0])
                            vop(lambda e, m=m: e.tensor_tensor(
                                out=tmA[:, 512 * m:512 * (m + 1)], in0=fs1[:, 0:512],
                                in1=fs0[:, 512:1024], op=ALU.mult),
                                reads=[fs1, fs0], writes=[tmA])
                    hbase = 0
                    if g == "e1":
                        w2src = dr[f"s{s}_w2"] if blk == 0 else stash[f"s{s}_w2"]
                        for mm in range(16):
                            sl = esl[mm % 2]
                            dma(w1q, sl[:, :], w2src[mm], reads=[w2src], writes=[sl])
                            if blk == 0:
                                dma("scalar", stash[f"s{s}_w2"][mm], sl[:, :],
                                    reads=[sl], writes=[stash[f"s{s}_w2"]])
                            rb = 512 * (mm % 4)
                            regs2 = psA[0:128, rb:rb + 512]
                            for kk in range(16):
                                top(lambda e, sl=sl, kk=kk, regs2=regs2: e.matmul(
                                    regs2, sl[:, 128 * kk:128 * (kk + 1)],
                                    tmA[:, 512 * kk:512 * (kk + 1)],
                                    start=(kk == 0), stop=(kk == 15)),
                                    reads=[sl, tmA], writes=[psAb[rb // 512]])
                            sop(lambda e, s=s, mm=mm, regs2=regs2: e.activation(
                                out=tmA[:, 8192 + 512 * mm:8192 + 512 * (mm + 1)],
                                in_=regs2, func=AF.Silu, scale=1.0,
                                bias=eb1[(s, 2)][:, mm:mm + 1]),
                                reads=[psAb[rb // 512], eb1[(s, 2)]], writes=[tmA])
                        hbase = 8192
                        nK2 = 16
                    else:
                        nK2 = nH
                    wosrc = dr[f"s{s}_wo"] if blk == 0 else stash[f"s{s}_wo"]
                    for wave in range(2):
                        for kk in range(nK2):
                            wo = ewo[kk % 2]
                            dma(w1q if wave == 0 else "sync", wo[:, :],
                                (wosrc if wave == 0 or blk > 0 else stash[f"s{s}_wo"])[kk],
                                reads=[wosrc if wave == 0 or blk > 0 else stash[f"s{s}_wo"]],
                                writes=[wo])
                            if blk == 0 and wave == 0:
                                dma("scalar", stash[f"s{s}_wo"][kk], wo[:, :],
                                    reads=[wo], writes=[stash[f"s{s}_wo"]])
                            for sti in range(2):
                                st = 2 * wave + sti
                                rb = 1024 * sti
                                for nh in range(2):
                                    top(lambda e, wo=wo, kk=kk, st=st, nh=nh, rb=rb,
                                        hbase=hbase, nK2=nK2: e.matmul(
                                            psA[0:128, rb + 512 * nh:rb + 512 * (nh + 1)],
                                            tmA[:, hbase + 512 * kk + 128 * st:
                                                hbase + 512 * kk + 128 * (st + 1)],
                                            wo[:, 512 * nh:512 * (nh + 1)],
                                            start=(kk == 0), stop=(kk == nK2 - 1)),
                                        reads=[wo, tmA], writes=[psAb[(rb + 512 * nh) // 512]])
                        for sti in range(2):
                            st = 2 * wave + sti
                            t = 4 * blk + st
                            rb = 1024 * sti
                            for nh in range(2):
                                vop(lambda e, s=s, st=st, t=t, nh=nh, rb=rb: e.scalar_tensor_tensor(
                                    out=acc[st][:, 512 * nh:512 * (nh + 1)],
                                    in0=psA[0:128, rb + 512 * nh:rb + 512 * (nh + 1)],
                                    scalar=ptm[:, 8 * t + s:8 * t + s + 1],
                                    in1=acc[st][:, 512 * nh:512 * (nh + 1)],
                                    op0=ALU.mult, op1=ALU.add),
                                    reads=[psAb[(rb + 512 * nh) // 512], ptm, acc[st]],
                                    writes=[acc[st]])
                for st in range(4):
                    t = 4 * blk + st
                    ln_add_resid(acc[st], acc[st][:, :], t)
            if "x2" in TAPS:
                for t in range(NT):
                    dma("sync", taps["x2"][128 * t:128 * (t + 1), :], resid[t][:, :],
                        reads=[resid[t]], writes=[taps["x2"]])
            K.barrier()
            pm_.close()

        if "l1a" in PHASES:
            pa = contextlib.ExitStack()
            m01 = sbt("m01", (128, 128), F32, pa)
            pa_h = contextlib.ExitStack()
            kqs = sbt("kqs", (128, 1024), BF16, pa_h)
            kks = sbt("kks", (128, 1024), BF16, pa_h)
            wsl2 = [sbt(f"wsl2{i}", (128, 1024), BF16, pa_h) for i in range(2)]
            kfh = sbt("kfh", (128, 2048), BF16, pa_h)
            kfm = sbt("kfm", (128, 2048), BF16, pa_h)
            qfm = sbt("qfm", (128, 2048), BF16, pa_h)
            sts = sbt("sts", (128, 1040), F32, pa_h)
            prf = sbt("prf", (128, 1040), F32, pa_h)
            Sb = sbt("Sb", (128, 65), BF16, pa_h)
            for _b in (kqs, kks, m01):
                _b.dsem = "gpa"
                _b.shared = True
            dma("gpsimd", kqs[0:64, :], dr["l1_kq"][:, :], reads=[dr["l1_kq"]], writes=[kqs])
            dma("gpsimd", kqs[64:128, :], dr["l1_kq"][:, :], reads=[dr["l1_kq"]], writes=[kqs])
            dma("gpsimd", kks[0:64, :], dr["l1_kk"][:, :], reads=[dr["l1_kk"]], writes=[kks])
            dma("gpsimd", kks[64:128, :], dr["l1_kk"][:, :], reads=[dr["l1_kk"]], writes=[kks])
            dma("sync", m01[:, :], dr["mask01"][:, :], reads=[dr["mask01"]], writes=[m01])
            for t in range(NT):
                negm, rr = ln_stats(resid[t], resid[t][:, :], 4 * t)
                bu = mk_bias(negm, rr, 4 * t + 2)
                sop(lambda e, t=t, rr=rr, bu=bu: e.activation(
                    out=tmA[:, 1024 * t:1024 * (t + 1)], in_=resid[t][:, :],
                    func=AF.Identity, scale=rr, bias=bu),
                    reads=[resid[t], stat], writes=[tmA])
            tp_tm_to_fm(tmA, fmA)
            K.barrier()

            def ufm2(p):
                return fmA[:, 2048 * p:2048 * (p + 1)]

            for m in range(8):
                sl = wsl2[m % 2]
                dma("gpsimd", sl[:, :], dr["l1_wv"][m], reads=[dr["l1_wv"]], writes=[sl])
                for kk in range(8):
                    for tb in range(4):
                        top(lambda e, sl=sl, kk=kk, tb=tb: e.matmul(
                            psA[0:128, 512 * tb:512 * (tb + 1)],
                            sl[:, 128 * kk:128 * (kk + 1)],
                            ufm2(kk)[:, 512 * tb:512 * (tb + 1)],
                            start=(kk == 0), stop=(kk == 7)),
                            reads=[sl, fmA], writes=[psAb[tb]])
                for tb in range(4):
                    sop(lambda e, tb=tb: e.copy(
                        out=sbf1[:, 512 * tb:512 * (tb + 1)],
                        in_=psA[0:128, 512 * tb:512 * (tb + 1)]),
                        reads=[psAb[tb]], writes=[sbf1])
                for g0 in range(0, 16, 4):
                    rb = 512 * ((g0 // 4) % 2)
                    reg = psT[0:128, rb:rb + 512]
                    for j in range(4):
                        t = g0 + j
                        top(lambda e, t=t, j=j, reg=reg: e.transpose(
                            reg[:, 128 * j:128 * (j + 1)],
                            sbf1[:, 128 * t:128 * (t + 1)], idb[:, :]),
                            reads=[sbf1, idb], writes=[psTb[rb // 512]])
                    sop(lambda e, m=m, g0=g0, reg=reg: e.copy(
                        out=tmA[:, 2048 * m + 128 * g0:2048 * m + 128 * (g0 + 4)],
                        in_=reg),
                        reads=[psTb[rb // 512]], writes=[tmA])

            def vtm(h, t):
                c0 = 2048 * (h // 2) + 128 * t + 64 * (h % 2)
                return tmA[:, c0:c0 + 64]

            def feat_maps(h, km_ap, kq_or_kk, dst_buf, dst2=None):
                hb = h % 2
                for g0 in range(0, 16, 2):
                    reg = psA[0:128, 1024 + 256 * ((g0 // 2) % 2):
                              1024 + 256 * ((g0 // 2) % 2) + 256]
                    for j in range(2):
                        t = g0 + j
                        top(lambda e, t=t, j=j, hb=hb, reg=reg, km_ap=km_ap,
                            kq_or_kk=kq_or_kk, h=h: e.matmul(
                            reg[:, 128 * j:128 * j + 64],
                            km_ap[64 * hb:64 * (hb + 1), 128 * t:128 * (t + 1)],
                            kq_or_kk[64 * hb:64 * (hb + 1), 64 * h:64 * (h + 1)],
                            start=True, stop=True),
                            reads=[sbf0, kq_or_kk], writes=[psAb[2]])
                    for j in range(2):
                        t = g0 + j
                        fb = reg[:, 128 * j:128 * j + 64]
                        vop(lambda e, fb=fb: e.tensor_reduce(
                            out=stat[:, 72:73], in_=fb, axis=AXX, op=ALU.max,
                            negate=True),
                            reads=[psAb[2]], writes=[stat])
                        vop(lambda e, fb=fb: e.tensor_reduce(
                            out=stat[:, 73:74], in_=fb, axis=AXX, op=ALU.min),
                            reads=[psAb[2]], writes=[stat])
                        sop(lambda e, fb=fb: e.activation(
                            out=dump[:, 0:64], in_=fb, func=AF.Exp, scale=1.0,
                            bias=stat[:, 72:73], accum_out=stat[:, 74:75]),
                            reads=[psAb[2], stat], writes=[dump, stat])
                        sop(lambda e, fb=fb: e.activation(
                            out=dump[:, 64:128], in_=fb, func=AF.Exp, scale=-1.0,
                            bias=stat[:, 73:74], accum_out=stat[:, 75:76]),
                            reads=[psAb[2], stat], writes=[dump, stat])
                        vop(lambda e: e.reciprocal(out=stat[:, 74:76], in_=stat[:, 74:76]),
                            reads=[stat], writes=[stat])
                        vop(lambda e, t=t: e.tensor_scalar(
                            out=dst_buf[:, 128 * t:128 * t + 64], in0=dump[:, 0:64],
                            scalar1=stat[:, 74:75], scalar2=1e-12, op0=ALU.mult,
                            op1=ALU.max),
                            reads=[dump, stat], writes=[dst_buf])
                        vop(lambda e, t=t: e.tensor_scalar(
                            out=dst_buf[:, 128 * t + 64:128 * (t + 1)],
                            in0=dump[:, 64:128],
                            scalar1=stat[:, 75:76], scalar2=1e-12, op0=ALU.mult,
                            op1=ALU.max),
                            reads=[dump, stat], writes=[dst_buf])

            for m in range(8):
                sl = wsl2[m % 2]
                dma("gpsimd", sl[:, :], dr["l1_wk"][m], reads=[dr["l1_wk"]], writes=[sl])
                for kk in range(8):
                    for tb in range(4):
                        top(lambda e, sl=sl, kk=kk, tb=tb: e.matmul(
                            psA[0:128, 512 * tb:512 * (tb + 1)],
                            sl[:, 128 * kk:128 * (kk + 1)],
                            ufm2(kk)[:, 512 * tb:512 * (tb + 1)],
                            start=(kk == 0), stop=(kk == 7)),
                            reads=[sl, fmA], writes=[psAb[tb]])
                for tb in range(4):
                    sop(lambda e, tb=tb: e.copy(
                        out=sbf0[:, 512 * tb:512 * (tb + 1)],
                        in_=psA[0:128, 512 * tb:512 * (tb + 1)]),
                        reads=[psAb[tb]], writes=[sbf0])
                for h in (2 * m, 2 * m + 1):
                    feat_maps(h, sbf0[:, 0:2048], kks, kfh)
                    for ch in range(8):
                        regS = psA[0:128, 512 * (ch % 2):512 * (ch % 2) + 65]
                        for j in range(2):
                            t = 2 * ch + j
                            top(lambda e, t=t, h=h, j=j, regS=regS: e.matmul(
                                regS[:, 0:64], kfh[:, 128 * t:128 * (t + 1)],
                                vtm(h, t), start=(j == 0), stop=False),
                                reads=[kfh, tmA], writes=[psAb[ch % 2]])
                            top(lambda e, t=t, j=j, regS=regS: e.matmul(
                                regS[:, 64:65], kfh[:, 128 * t:128 * (t + 1)],
                                ones_bc[:, :], start=False, stop=(j == 1)),
                                reads=[kfh, ones_bc], writes=[psAb[ch % 2]])
                        sop(lambda e, regS=regS: e.copy(out=fs0[:, 0:65], in_=regS),
                            reads=[psAb[ch % 2]], writes=[fs0])
                        dma("sync", ag5i[h, ch], fs0[:, 0:65], reads=[fs0], writes=[ag5i])
            K.emit("gpsimd", lambda e: e.collective_compute(
                "AllGather", ALU.bypass, replica_groups=PAIRS,
                ins=[ag5i[:, :, :, :]], outs=[ag5o[:, :, :, :, :]]),
                reads=[ag5i], writes=[ag5o], asyncamt=1)

            for m in range(8):
                slk, slq = wsl2[0], wsl2[1]
                dma("gpsimd", slk[:, :], dr["l1_wk"][m], reads=[dr["l1_wk"]], writes=[slk])
                dma("gpsimd", slq[:, :], dr["l1_wq"][m], reads=[dr["l1_wq"]], writes=[slq])
                for par_i, sl in ((0, slk), (1, slq)):
                    for kk in range(8):
                        for tb in range(4):
                            top(lambda e, sl=sl, kk=kk, tb=tb: e.matmul(
                                psA[0:128, 512 * tb:512 * (tb + 1)],
                                sl[:, 128 * kk:128 * (kk + 1)],
                                ufm2(kk)[:, 512 * tb:512 * (tb + 1)],
                                start=(kk == 0), stop=(kk == 7)),
                                reads=[sl, fmA], writes=[psAb[tb]])
                    for tb in range(4):
                        sop(lambda e, par_i=par_i, tb=tb: e.copy(
                            out=sbf0[:, 2048 * par_i + 512 * tb:2048 * par_i + 512 * (tb + 1)],
                            in_=psA[0:128, 512 * tb:512 * (tb + 1)]),
                            reads=[psAb[tb]], writes=[sbf0])
                for h in (2 * m, 2 * m + 1):
                    feat_maps(h, sbf0[:, 0:2048], kks, kfh)
                    feat_maps(h, sbf0[:, 2048:4096], kqs, sbf1)
                    for src, dst in ((kfh, kfm), (sbf1, qfm)):
                        for g0 in range(0, 16, 4):
                            rb = 512 * ((g0 // 4) % 2)
                            reg = psT[0:128, rb:rb + 512]
                            for j in range(4):
                                t = g0 + j
                                top(lambda e, t=t, j=j, src=src, reg=reg: e.transpose(
                                    reg[:, 128 * j:128 * (j + 1)],
                                    src[:, 128 * t:128 * (t + 1)], idb[:, :]),
                                    reads=[src, idb], writes=[psTb[rb // 512]])
                            sop(lambda e, g0=g0, dst=dst, reg=reg: e.copy(
                                out=dst[:, 128 * g0:128 * (g0 + 4)], in_=reg),
                                reads=[psTb[rb // 512]], writes=[dst])
                    for s_ in range(2):
                        for ch in range(8):
                            dma("sync", sts[:, 65 * (8 * s_ + ch):65 * (8 * s_ + ch + 1)],
                                ag5o[s_, h, ch], reads=[ag5o], writes=[sts])
                    vop(lambda e: e.memset(prf[:, 0:65], 0.0), writes=[prf])
                    for gch in range(1, 16):
                        vop(lambda e, gch=gch: e.tensor_tensor(
                            out=prf[:, 65 * gch:65 * (gch + 1)],
                            in0=prf[:, 65 * (gch - 1):65 * gch],
                            in1=sts[:, 65 * (gch - 1):65 * gch], op=ALU.add),
                            reads=[sts, prf], writes=[prf])
                    for j in range(8):
                        K.emit2("scalar",
                                (lambda e, j=j: e.copy(out=Sb[:, :], in_=prf[:, 65 * j:65 * (j + 1)])),
                                (lambda e, j=j: e.copy(out=Sb[:, :], in_=prf[:, 65 * (8 + j):65 * (9 + j)])),
                                reads=[prf], writes=[Sb])
                        t0 = 2 * j
                        for bi, (qi, kj) in enumerate(((0, 0), (1, 0), (1, 1))):
                            regA = psA[0:128, 1024 + 128 * (bi % 2):1024 + 128 * (bi % 2) + 128]
                            top(lambda e, qi=qi, kj=kj, t0=t0, regA=regA: e.matmul(
                                regA, kfm[:, 128 * (t0 + kj):128 * (t0 + kj + 1)],
                                qfm[:, 128 * (t0 + qi):128 * (t0 + qi + 1)],
                                start=True, stop=True),
                                reads=[kfm, qfm], writes=[psAb[2]])
                            if qi == kj:
                                vop(lambda e, bi=bi, regA=regA: e.tensor_tensor(
                                    out=kfh[:, 128 * bi:128 * (bi + 1)], in0=regA,
                                    in1=m01[:, :], op=ALU.mult),
                                    reads=[psAb[2], m01], writes=[kfh])
                            else:
                                sop(lambda e, bi=bi, regA=regA: e.copy(
                                    out=kfh[:, 128 * bi:128 * (bi + 1)], in_=regA),
                                    reads=[psAb[2]], writes=[kfh])
                        for qi in range(2):
                            t = t0 + qi
                            rb = 1024 + 128 * qi
                            rego = psB[0:128, rb:rb + 65]
                            top(lambda e, t=t, rego=rego: e.matmul(
                                rego, qfm[:, 128 * t:128 * (t + 1)], Sb[:, :],
                                start=True, stop=False),
                                reads=[qfm, Sb], writes=[psBb[2]])
                            blocks = [(0, 0)] if qi == 0 else [(0, 1), (1, 2)]
                            nb = len(blocks)
                            for ib, (kjrel, bi) in enumerate(blocks):
                                kt_ = t0 + kjrel
                                top(lambda e, bi=bi, kt_=kt_, h=h, rego=rego: e.matmul(
                                    rego[:, 0:64], kfh[:, 128 * bi:128 * (bi + 1)],
                                    vtm(h, kt_), start=False, stop=False),
                                    reads=[kfh, tmA], writes=[psBb[2]])
                                top(lambda e, bi=bi, rego=rego, last=(ib == nb - 1): e.matmul(
                                    rego[:, 64:65], kfh[:, 128 * bi:128 * (bi + 1)],
                                    ones_bc[:, :], start=False, stop=last),
                                    reads=[kfh, ones_bc], writes=[psBb[2]])
                            vop(lambda e, rego=rego: e.tensor_scalar(
                                out=stat[:, 76:77], in0=rego[:, 64:65],
                                scalar1=1e-12, scalar2=None, op0=ALU.add),
                                reads=[psBb[2]], writes=[stat])
                            vop(lambda e: e.reciprocal(out=stat[:, 76:77],
                                                       in_=stat[:, 76:77]),
                                reads=[stat], writes=[stat])
                            sop(lambda e, rego=rego: e.activation(
                                out=dump[:, 0:64], in_=rego[:, 0:64], func=AF.Identity,
                                scale=stat[:, 76:77], bias=cst[:, 0:1]),
                                reads=[psBb[2], stat, cst], writes=[dump])
                            dma("sync", yspill.t[t, :, 64 * h:64 * (h + 1)],
                                dump[:, 0:64], reads=[dump], writes=[yspill])
            K.barrier()
            pa_h.close()
            wos = sbt("wos", (128, 8192), BF16, pa)
            for kk in range(8):
                dma("gpsimd", wos[:, 1024 * kk:1024 * (kk + 1)], dr["l1_wo"][kk],
                    reads=[dr["l1_wo"]], writes=[wos])
            for t in range(NT):
                dma("sync", tmA[:, 1024 * t:1024 * (t + 1)], yspill.t[t],
                    reads=[yspill], writes=[tmA])
            tp_tm_to_fm(tmA, fmA)
            for t in range(NT):
                for nh in range(2):
                    rb = 512 * ((2 * t + nh) % 3)
                    reg = psB[0:128, rb:rb + 512]
                    for kk in range(8):
                        top(lambda e, kk=kk, t=t, nh=nh, reg=reg: e.matmul(
                            reg, fmA[:, 2048 * kk + 128 * t:2048 * kk + 128 * (t + 1)],
                            wos[:, 1024 * kk + 512 * nh:1024 * kk + 512 * (nh + 1)],
                            start=(kk == 0), stop=(kk == 7)),
                            reads=[fmA, wos], writes=[psBb[rb // 512]])
                    sop(lambda e, nh=nh, reg=reg: e.copy(
                        out=fs0[:, 512 * nh:512 * (nh + 1)], in_=reg),
                        reads=[psBb[rb // 512]], writes=[fs0])
                ln_add_resid(fs0, fs0[:, 0:1024], t)
            if "x3" in TAPS:
                for t in range(NT):
                    dma("sync", taps["x3"][128 * t:128 * (t + 1), :], resid[t][:, :],
                        reads=[resid[t]], writes=[taps["x3"]])
            K.barrier()
            pa.close()

        if "l1f" in PHASES:
            pf_ = contextlib.ExitStack()
            fsl = [sbt(f"fsl{i}", (128, 4096), BF16, pf_) for i in range(2)]
            ftm = sbt("ftm", (128, 4096), F32, pf_)
            sgb2 = sbt("sgb2", (128, 512), BF16, pf_)
            fscl = sbt("fscl", (128, 136), F32, pf_)
            fscl.dsem = "gpf"
            for nm, c0 in (("f_s1", 0), ("f_s2", 32), ("f_sh1", 64), ("f_sh2", 96)):
                dma("sync", fscl[:, c0:c0 + 32], dr[nm][:, :], reads=[dr[nm]], writes=[fscl])
            dma("sync", fscl[:, 128:136], dr["f_s3"][:, :], reads=[dr["f_s3"]], writes=[fscl])
            for blk in range(4):
                for ti in range(4):
                    t = 4 * blk + ti
                    negm, rr = ln_stats(resid[t], resid[t][:, :], 64)
                    bu = mk_bias(negm, rr, 66)
                    sop(lambda e, t=t, rr=rr, bu=bu: e.activation(
                        out=sbf1[:, 0:1024], in_=resid[t][:, :], func=AF.Identity,
                        scale=rr, bias=bu),
                        reads=[resid[t], stat], writes=[sbf1])
                    for p in range(8):
                        rb = 128 * (p % 8)
                        regt = psT[0:128, rb:rb + 128]
                        top(lambda e, p=p, regt=regt: e.transpose(
                            regt, sbf1[:, 128 * p:128 * (p + 1)], idb[:, :]),
                            reads=[sbf1, idb], writes=[psTb[(p % 8) // 4]])
                        sop(lambda e, p=p, ti=ti, regt=regt: e.copy(
                            out=fmA[:, 512 * p + 128 * ti:512 * p + 128 * (ti + 1)],
                            in_=regt),
                            reads=[psTb[(p % 8) // 4]], writes=[fmA])
                for stage, wname, scol, dsti in (("w1", "f_w1", 0, 0), ("w2", "f_w2", 32, 1)):
                    for m in range(32):
                        sl = fsl[m % 2]
                        src = dr[wname] if blk == 0 else stash[wname]
                        dma("gpsimd" if blk == 0 else "sync", sl[:, 0:1024], src[m],
                            reads=[src], writes=[sl])
                        if blk == 0:
                            dma("scalar", stash[wname][m], sl[:, 0:1024],
                                reads=[sl], writes=[stash[wname]])
                        rb = 512 * (m % 3)
                        reg = psB[0:128, rb:rb + 512]
                        for kk in range(8):
                            top(lambda e, sl=sl, kk=kk, reg=reg: e.matmul(
                                reg, sl[:, 128 * kk:128 * (kk + 1)],
                                fmA[:, 512 * kk:512 * (kk + 1)],
                                start=(kk == 0), stop=(kk == 7)),
                                reads=[sl, fmA], writes=[psBb[rb // 512]])
                        if stage == "w1":
                            sop(lambda e, m=m, reg=reg: e.activation(
                                out=tmA[:, 512 * m:512 * (m + 1)], in_=reg,
                                func=AF.Silu, scale=fscl[:, m:m + 1], bias=cst[:, 0:1]),
                                reads=[psBb[rb // 512], fscl, cst], writes=[tmA])
                        else:
                            sop(lambda e, m=m, reg=reg: e.activation(
                                out=sgb2[:, 0:512], in_=reg, func=AF.Identity,
                                scale=fscl[:, 32 + m:33 + m], bias=cst[:, 0:1]),
                                reads=[psBb[rb // 512], fscl, cst], writes=[sgb2])
                            vop(lambda e, m=m: e.tensor_tensor(
                                out=tmA[:, 512 * m:512 * (m + 1)],
                                in0=tmA[:, 512 * m:512 * (m + 1)], in1=sgb2[:, 0:512],
                                op=ALU.mult),
                                reads=[sgb2, tmA], writes=[tmA])
                for stage, wname, scol, srcb, dstb in (
                        ("h1", "f_h1", 64, tmA, fmA), ("h2", "f_h2", 96, fmA, tmA)):
                    for m in range(32):
                        sl = fsl[m % 2]
                        src = dr[wname] if blk == 0 else stash[wname]
                        dma("gpsimd" if blk == 0 else "sync", sl[:, :], src[m],
                            reads=[src], writes=[sl])
                        if blk == 0:
                            dma("scalar", stash[wname][m], sl[:, :],
                                reads=[sl], writes=[stash[wname]])
                        rb = 512 * (m % 3)
                        reg = psB[0:128, rb:rb + 512]
                        for kk in range(32):
                            top(lambda e, sl=sl, kk=kk, reg=reg, srcb=srcb: e.matmul(
                                reg, sl[:, 128 * kk:128 * (kk + 1)],
                                srcb[:, 512 * kk:512 * (kk + 1)],
                                start=(kk == 0), stop=(kk == 31)),
                                reads=[sl, srcb], writes=[psBb[rb // 512]])
                        sop(lambda e, m=m, reg=reg, scol=scol, dstb=dstb: e.activation(
                            out=dstb[:, 512 * m:512 * (m + 1)], in_=reg, func=AF.Gelu,
                            scale=fscl[:, scol + m:scol + m + 1], bias=cst[:, 0:1]),
                            reads=[psBb[rb // 512], fscl, cst], writes=[dstb])
                for m in range(8):
                    sl = fsl[m % 2]
                    src = dr["f_w3"] if blk == 0 else stash["f_w3"]
                    dma("gpsimd" if blk == 0 else "sync", sl[:, :], src[m],
                        reads=[src], writes=[sl])
                    if blk == 0:
                        dma("scalar", stash["f_w3"][m], sl[:, :],
                            reads=[sl], writes=[stash["f_w3"]])
                    rb = 512 * (m % 2)
                    reg = psB[0:128, rb:rb + 512]
                    for kk in range(32):
                        top(lambda e, sl=sl, kk=kk, reg=reg: e.matmul(
                            reg, sl[:, 128 * kk:128 * (kk + 1)],
                            tmA[:, 512 * kk:512 * (kk + 1)],
                            start=(kk == 0), stop=(kk == 31)),
                            reads=[sl, tmA], writes=[psBb[rb // 512]])
                    sop(lambda e, m=m, reg=reg: e.activation(
                        out=fs0[:, 0:512], in_=reg, func=AF.Identity,
                        scale=fscl[:, 128 + m:129 + m], bias=cst[:, 0:1]),
                        reads=[psBb[rb // 512], fscl, cst], writes=[fs0])
                    for ti in range(4):
                        rb2 = 1024 + 128 * (ti % 2)
                        regt = psA[0:128, rb2:rb2 + 128]
                        top(lambda e, ti=ti, regt=regt: e.transpose(
                            regt, fs0[:, 128 * ti:128 * (ti + 1)], idf[:, :]),
                            reads=[fs0, idf], writes=[psAb[2]])
                        sop(lambda e, m=m, ti=ti, regt=regt: e.copy(
                            out=ftm[:, 1024 * ti + 128 * m:1024 * ti + 128 * (m + 1)],
                            in_=regt),
                            reads=[psAb[2]], writes=[ftm])
                for ti in range(4):
                    t = 4 * blk + ti
                    ln_add_resid(ftm, ftm[:, 1024 * ti:1024 * (ti + 1)], t)
            K.barrier()
            pf_.close()


        K.barrier()
        for t in range(NT):
            dma("sync", OUT[128 * t:128 * (t + 1), :], resid[t][:, :],
                reads=[resid[t]], writes=[OUT])
        K.barrier()

        sems = {s: es.enter_context(nc.semaphore(s)) for s in K.cnt}
        block = es.enter_context(nc.Block())
        K.finalize_into(block, sems)
    return nc


def kernel(**inputs):
    shared = build_host_inputs(inputs)
    noise = moe_noise()
    x = inputs["L0_attn_x"]
    in_maps = []
    for c in range(NCORE):
        b, hf = c >> 1, c & 1
        m = dict(shared)
        m["pmk"] = np.ascontiguousarray(shared["_pmk"][8 * hf:8 * (hf + 1)])
        m["pmv"] = np.ascontiguousarray(shared["_pmv"][8 * hf:8 * (hf + 1)])
        m.pop("_pmk"); m.pop("_pmv")
        m["x"] = np.ascontiguousarray(x[b, 2048 * hf:2048 * (hf + 1), :])
        m["noise"] = np.ascontiguousarray(noise[b, 2048 * hf:2048 * (hf + 1), :])
        cos, sin = rope_tables(2048 * hf)
        m["ropecos"] = cos
        m["ropesin"] = sin
        m["pflag"] = np.array([[hf]], dtype=np.int32)
        in_maps.append(m)
    nc = build_nc()
    res = run_bass_kernel_spmd(nc, in_maps, core_ids=list(range(NCORE)))
    out = np.zeros((4, 4096, 1024), np.float32)
    for c in range(NCORE):
        b, hf = c >> 1, c & 1
        out[b, 2048 * hf:2048 * (hf + 1), :] = np.asarray(res.results[c]["out"],
                                                          np.float32)
    return out


# version 08
# speedup vs baseline: 1.1701x; 1.0026x over previous
import contextlib
import numpy as np
import concourse.bass as bass
import concourse.mybir as mybir
from concourse.bass_utils import run_bass_kernel_spmd

F32 = mybir.dt.float32
BF16 = mybir.dt.bfloat16
I32 = mybir.dt.int32
AF = mybir.ActivationFunctionType
ALU = mybir.AluOpType
AXX = mybir.AxisListType.X

NCORE = 8
D = 1024
NT = 16
TOK = NT * 128
EPS32 = float(np.finfo(np.float32).eps)
PAIRS = [[0, 1], [2, 3], [4, 5], [6, 7]]
FULLG = [list(range(NCORE))]

PHASES = ("l0", "moe", "l1a", "l1f")
TAPS = ()
DEBUG = ()
BANKED = "AB"

ENGS = ("tensor", "vector", "scalar", "gpsimd", "sync")
ESEM = {"tensor": "T", "vector": "V", "scalar": "S", "gpsimd": "G", "sync": "Y"}
ENG_SEMS = set(ESEM.values())


class Buf:
    __slots__ = ("t", "w", "readers", "name", "dsem", "shared")

    def __init__(self, t, name="", g=None):
        self.t = t
        self.name = name
        self.dsem = g or ("d_" + name)
        self.shared = g is not None
        self.w = None
        self.readers = []

    def __getitem__(self, idx):
        return self.t[idx]


class Emitter:
    def __init__(self):
        self.streams = {e: [] for e in ENGS}
        self.cnt = {s: 0 for s in ESEM.values()}
        self.hwm = {}

    def _prep(self, eng, reads, writes, dma, asyncamt=None):
        if dma or asyncamt:
            sem = writes[0].dsem + "@" + eng
            amt = asyncamt or 16
            self.cnt.setdefault(sem, 0)
        else:
            sem = ESEM[eng]
            amt = 1
        deps = []
        for b in reads:
            if b.w is not None:
                deps.append(b.w)
        for b in writes:
            if b.w is not None:
                if dma and not b.readers and b.w[0] == sem:
                    pass
                else:
                    deps.append(b.w)
            deps.extend(b.readers)
        waits = {}
        for (s, v) in deps:
            if s.startswith("#"):
                self.frozen = getattr(self, "frozen", set())
                self.frozen.add(s[1:])
                for q in ENGS:
                    nm = s[1:] + "@" + q
                    vv = self.cnt.get(nm, 0)
                    if vv and waits.get(nm, 0) < vv and self.hwm.get((eng, nm), 0) < vv:
                        waits[nm] = vv
                continue
            if self.hwm.get((eng, s), 0) >= v:
                continue
            if waits.get(s, 0) < v:
                waits[s] = v
        for s, v in waits.items():
            self.hwm[(eng, s)] = v
        self.dirty = True
        self.cnt[sem] += amt
        if dma or asyncamt:
            if writes[0].shared:
                after = ("#" + writes[0].dsem, 0)
            else:
                after = (sem, self.cnt[sem])
        else:
            after = (sem, self.cnt[sem])
        for b in writes:
            b.w = after
            b.readers = []
        for b in reads:
            b.readers.append(after)
        return list(waits.items()), sem, amt

    def emit(self, eng, fn, reads=(), writes=(), dma=False, asyncamt=None):
        waits, sem, amt = self._prep(eng, reads, writes, dma, asyncamt)
        self.streams[eng].append(("op", waits, fn, None, sem, amt))

    def emit2(self, eng, fnE, fnO, reads=(), writes=(), dma=False):
        waits, sem, amt = self._prep(eng, reads, writes, dma)
        self.streams[eng].append(("par", waits, fnE, fnO, sem, amt))

    def regload(self, eng, pfs_buf, ap):
        waits, sem, amt = self._prep(eng, (pfs_buf,), (), False)
        self.streams[eng].append(("reg", waits, ap, None, sem, amt))

    def barrier(self):
        if not getattr(self, "dirty", True):
            return
        self.dirty = False
        self.nbar = getattr(self, "nbar", 0) + 1
        snap = {s: v for s, v in self.cnt.items() if s != "B"}
        self.cnt.setdefault("B", 0)
        for e in ENGS:
            waits = []
            for s, v in snap.items():
                if s == ESEM[e] or v == 0:
                    continue
                if self.hwm.get((e, s), 0) < v:
                    waits.append((s, v))
                    self.hwm[(e, s)] = v
            self.streams[e].append(("op", waits, (lambda eng: eng.nop()), None, "B", 1))
        self.cnt["B"] += 5
        tgt = self.cnt["B"]
        for e in ENGS:
            self.streams[e].append(("op", [("B", tgt)], (lambda eng: eng.nop()), None,
                                    "B", 1))
            self.hwm[(e, "B")] = tgt
        self.cnt["B"] += 5

    def finalize_into(self, block, sems):
        for e in ENGS:
            dec = getattr(block, e)

            def body(eng, _ops=tuple(self.streams[e])):
                preg = None
                for kind, waits, a, b, sem, amt in _ops:
                    for (s, v) in waits:
                        eng.wait_ge(sems[s], v)
                    if kind == "reg":
                        preg = eng.alloc_register("par")
                        eng.reg_load(preg, a)
                        eng.nop().then_inc(sems[sem], amt)
                    elif kind == "op":
                        instr = a(eng)
                        if instr is None:
                            instr = eng.nop()
                        instr.then_inc(sems[sem], amt)
                    else:
                        with eng.If_eq(preg, 0):
                            a(eng).then_inc(sems[sem], amt)
                        with eng.If_eq(preg, 1):
                            b(eng).then_inc(sems[sem], amt)

            dec(body)


def lhsT_tiles(w):
    Kd, Md = w.shape
    nK, nM = Kd // 128, Md // 128
    return np.ascontiguousarray(
        w.reshape(nK, 128, nM, 128).transpose(2, 1, 0, 3).reshape(nM, 128, nK * 128))


def col_tiles(v):
    return np.ascontiguousarray(v.reshape(-1, 128).T)


SLOT_H = [4096, 2048, 3072, 4096, 2048, 3072, 4096, 2048]
SLOT_G = ["e0", "e1", "e2", "e0", "e1", "e2", "e0", "e1"]
SLOT_IDX = [0, 0, 0, 1, 1, 1, 2, 2]


def build_host_inputs(ins):
    d = {}
    wqkv = ins["L0_attn_w_qkv"]
    wq, wk, wv = wqkv[0:1024], wqkv[1024:2048], wqkv[2048:3072]
    perm = []
    for h in range(16):
        perm.extend(64 * h + np.arange(0, 64, 2))
    for h in range(16):
        perm.extend(64 * h + np.arange(1, 64, 2))
    perm = np.array(perm)
    d["w_q"] = lhsT_tiles(np.ascontiguousarray(wq[perm].T))
    d["w_k"] = lhsT_tiles(np.ascontiguousarray(wk[perm].T))
    d["w_v"] = lhsT_tiles(np.ascontiguousarray(wv.T))
    d["w_out_r"] = np.ascontiguousarray(ins["L0_attn_w_out"].T.reshape(8, 128, 1024))
    d["gam"] = col_tiles(ins["L0_attn_norm_weight"])
    pm = ins["L0_attn_persist_mem"]
    hperm = np.concatenate([np.arange(0, 64, 2), np.arange(1, 64, 2)])
    pmk = np.zeros((16, 65, 4), np.float32)
    pmv = np.zeros((16, 4, 65), np.float32)
    for h in range(16):
        pmk[h, 0:64] = pm[h][:, hperm].T
        pmk[h, 64] = 1.0
        pmv[h, :, 0:64] = pm[h]
        pmv[h, :, 64] = 1.0
    d["_pmk"] = pmk
    d["_pmv"] = pmv
    d["gw1"] = np.ascontiguousarray(ins["L0_ffn_gate_w1"].reshape(8, 128, 1024))
    d["gb1r"] = ins["L0_ffn_gate_b1"].reshape(1, 1024)
    d["lnsc"] = col_tiles(ins["L0_ffn_ln_scale"])
    d["lnoc"] = col_tiles(ins["L0_ffn_ln_offset"])
    d["gw2"] = lhsT_tiles(ins["L0_ffn_gate_w2"])
    d["gb2c"] = col_tiles(ins["L0_ffn_gate_b2"])
    d["gw3"] = np.ascontiguousarray(ins["L0_ffn_gate_w3"].reshape(4, 128, 8))
    d["gb3"] = ins["L0_ffn_gate_b3"].reshape(1, 8)
    d["aff_w"] = np.ascontiguousarray(ins["L0_ffn_aff_w"].reshape(8, 128, 8))
    d["aff_b"] = ins["L0_ffn_aff_b"].reshape(1, 8)
    d["usage"] = ins["L0_ffn_expert_usage"].reshape(1, 8)
    b2all = np.zeros((8, 1024), np.float32)
    for s in range(8):
        g, i = SLOT_G[s], SLOT_IDX[s]
        if g == "e0":
            d[f"s{s}_w1"] = lhsT_tiles(ins["L0_ffn_e0_w1"][i])
            d[f"s{s}_b1"] = col_tiles(ins["L0_ffn_e0_b1"][i])
            d[f"s{s}_wo"] = np.ascontiguousarray(ins["L0_ffn_e0_w2"][i].reshape(32, 128, 1024))
            b2all[s] = ins["L0_ffn_e0_b2"][i]
        elif g == "e1":
            d[f"s{s}_w1"] = lhsT_tiles(ins["L0_ffn_e1_w1"][i])
            d[f"s{s}_b1"] = col_tiles(ins["L0_ffn_e1_b1"][i])
            d[f"s{s}_w2"] = lhsT_tiles(ins["L0_ffn_e1_w2"][i])
            d[f"s{s}_b2"] = col_tiles(ins["L0_ffn_e1_b2"][i])
            d[f"s{s}_wo"] = np.ascontiguousarray(ins["L0_ffn_e1_w3"][i].reshape(16, 128, 1024))
            b2all[s] = ins["L0_ffn_e1_b3"][i]
        else:
            d[f"s{s}_w1"] = lhsT_tiles(ins["L0_ffn_e2_w1"][i])
            d[f"s{s}_b1"] = col_tiles(ins["L0_ffn_e2_b1"][i])
            d[f"s{s}_wo"] = np.ascontiguousarray(ins["L0_ffn_e2_w2"][i].reshape(24, 128, 1024))
            b2all[s] = ins["L0_ffn_e2_b2"][i]
    d["b2all"] = b2all
    d["l1_wq"] = lhsT_tiles(ins["L1_attn_Wq"])
    d["l1_wk"] = lhsT_tiles(ins["L1_attn_Wk"])
    d["l1_wv"] = lhsT_tiles(ins["L1_attn_Wv"])
    d["l1_wo"] = np.ascontiguousarray(ins["L1_attn_Wo"].reshape(8, 128, 1024))
    kqs = np.zeros((64, 16 * 64), np.float32)
    kks = np.zeros((64, 16 * 64), np.float32)
    for h in range(16):
        kqs[:, 64 * h:64 * (h + 1)] = ins["L1_attn_Kq"][h]
        kks[:, 64 * h:64 * (h + 1)] = ins["L1_attn_Kk"][h]
    d["l1_kq"] = kqs
    d["l1_kk"] = kks
    d["f_w1"] = lhsT_tiles(np.ascontiguousarray(ins["L1_ffn_w1_w"].T))
    d["f_w2"] = lhsT_tiles(np.ascontiguousarray(ins["L1_ffn_w2_w"].T))
    d["f_h1"] = lhsT_tiles(np.ascontiguousarray(ins["L1_ffn_h1_w"].T))
    d["f_h2"] = lhsT_tiles(np.ascontiguousarray(ins["L1_ffn_h2_w"].T))
    d["f_w3"] = lhsT_tiles(np.ascontiguousarray(ins["L1_ffn_w3_w"].T))
    d["f_s1"] = col_tiles(ins["L1_ffn_w1_s"])
    d["f_s2"] = col_tiles(ins["L1_ffn_w2_s"])
    d["f_sh1"] = col_tiles(ins["L1_ffn_h1_s"])
    d["f_sh2"] = col_tiles(ins["L1_ffn_h2_s"])
    d["f_s3"] = col_tiles(ins["L1_ffn_w3_s"])
    d["ident"] = np.eye(128, dtype=np.float32)
    m1 = np.zeros((4, 128, 512), np.float32)
    q = np.arange(128)[:, None]
    k = np.arange(512)[None, :]
    for dlt in range(4):
        m1[dlt] = np.where(k <= q + 128 * dlt, 0.0, -1e30)
    d["mask1"] = m1
    kk = np.arange(128)[:, None]
    qq = np.arange(128)[None, :]
    d["mask2"] = np.where(kk <= qq, 0.0, -1e30).astype(np.float32)
    d["mask01"] = np.where(kk <= qq, 1.0, 0.0).astype(np.float32)
    wfix = np.zeros((128, 8), np.float32)
    wfix[:, 0] = 0.125
    wfix[:, 1] = 0.125
    d["wfix"] = wfix
    return d


_NOISE = None


def moe_noise():
    global _NOISE
    if _NOISE is None:
        import jax
        import jax.numpy as jnp
        with jax.default_device(jax.devices("cpu")[0]):
            _NOISE = np.asarray(jax.random.normal(jax.random.key(1), (4, 4096, 8),
                                                  dtype=jnp.float32))
    return _NOISE


def rope_tables(pos0):
    inv = (1.0 / 10000.0 ** (np.arange(0, 64, 2, dtype=np.float32) / 64.0))
    pos = np.arange(pos0, pos0 + TOK, dtype=np.float32)
    fr = inv[:, None] * pos[None, :]
    return (np.tile(np.cos(fr), (4, 1)).astype(np.float32),
            np.tile(np.sin(fr), (4, 1)).astype(np.float32))


def build_nc():
    nc = bass.Bass("TRN2", target_bir_lowering=False)
    K = Emitter()

    X = Buf(nc.dram_tensor("x", [TOK, D], F32, kind="ExternalInput").ap(), "x")
    OUT = Buf(nc.dram_tensor("out", [TOK, D], F32, kind="ExternalOutput").ap(), "out",
              g="gout")
    PF = Buf(nc.dram_tensor("pflag", [1, 1], I32, kind="ExternalInput").ap(), "pflag")
    dr = {}

    def din(name, shape):
        dr[name] = Buf(nc.dram_tensor(name, list(shape), F32,
                                      kind="ExternalInput").ap(), name)

    din("w_q", (8, 128, 1024)); din("w_k", (8, 128, 1024)); din("w_v", (8, 128, 1024))
    din("w_out_r", (8, 128, 1024)); din("gam", (128, 8))
    din("pmk", (8, 65, 4)); din("pmv", (8, 4, 65))
    din("gw1", (8, 128, 1024)); din("gb1r", (1, 1024)); din("lnsc", (128, 8)); din("lnoc", (128, 8))
    din("gw2", (4, 128, 1024)); din("gb2c", (128, 4)); din("gw3", (4, 128, 8)); din("gb3", (1, 8))
    din("aff_w", (8, 128, 8)); din("aff_b", (1, 8)); din("usage", (1, 8))
    for s in range(8):
        H = SLOT_H[s]
        din(f"s{s}_w1", (H // 128, 128, 1024))
        din(f"s{s}_b1", (128, H // 128))
        if SLOT_G[s] == "e1":
            din(f"s{s}_w2", (16, 128, 2048)); din(f"s{s}_b2", (128, 16))
        din(f"s{s}_wo", (H // 128, 128, 1024))
    din("b2all", (8, 1024))
    din("l1_wq", (8, 128, 1024)); din("l1_wk", (8, 128, 1024)); din("l1_wv", (8, 128, 1024))
    din("l1_wo", (8, 128, 1024)); din("l1_kq", (64, 1024)); din("l1_kk", (64, 1024))
    din("f_w1", (32, 128, 1024)); din("f_w2", (32, 128, 1024))
    din("f_h1", (32, 128, 4096)); din("f_h2", (32, 128, 4096)); din("f_w3", (8, 128, 4096))
    din("f_s1", (128, 32)); din("f_s2", (128, 32)); din("f_sh1", (128, 32))
    din("f_sh2", (128, 32)); din("f_s3", (128, 8))
    din("ident", (128, 128)); din("mask1", (4, 128, 512)); din("mask2", (128, 128))
    din("mask01", (128, 128)); din("wfix", (128, 8)); din("noise", (TOK, 8))
    din("ropecos", (128, TOK)); din("ropesin", (128, TOK))

    def dint(name, shape, dt=F32, g=None):
        return Buf(nc.dram_tensor(name, list(shape), dt).ap(), name, g=g)

    AG1K = 16 * 65 * TOK
    AG1V = NT * 128 * 1024
    AG1Q = 16 * 64 * TOK
    AG1 = AG1K + AG1V + AG1Q
    ag1i = dint("ag1i", [AG1], BF16)
    ag1o = dint("ag1o", [2 * AG1], BF16)
    ag2i = dint("ag2i", [8, 4096, 64], BF16)
    ag2o = dint("ag2o", [2, 8, 4096, 64], BF16)
    ag3i = dint("ag3i", [1, TOK])
    ag3o = dint("ag3o", [NCORE, TOK])
    ag4i = dint("ag4i", [128, 8])
    ag4o = dint("ag4o", [128, 8])
    ag5i = dint("ag5i", [16, 8, 128, 65])
    ag5o = dint("ag5o", [2, 16, 8, 128, 65])
    yspill = dint("yspill", [NT, 128, 1024], BF16)
    stash = {}
    for s in range(8):
        H = SLOT_H[s]
        stash[f"s{s}_w1"] = dint(f"st_s{s}w1", [H // 128, 128, 1024], BF16, g="gstash")
        if SLOT_G[s] == "e1":
            stash[f"s{s}_w2"] = dint(f"st_s{s}w2", [16, 128, 2048], BF16, g="gstash")
        stash[f"s{s}_wo"] = dint(f"st_s{s}wo", [H // 128, 128, 1024], BF16, g="gstash")
    for nm, shp in (("f_w1", (32, 128, 1024)), ("f_w2", (32, 128, 1024)),
                    ("f_h1", (32, 128, 4096)), ("f_h2", (32, 128, 4096)),
                    ("f_w3", (8, 128, 4096))):
        stash[nm] = dint("st_" + nm, list(shp), BF16, g="gstash")
    taps = {}
    for t in TAPS:
        taps[t] = Buf(nc.dram_tensor("tap_" + t, [TOK, D], F32,
                                     kind="ExternalOutput").ap(), "tap" + t, g="gout")
    dbg = {}
    for nm, shp, dt in (("ag1i", [AG1], BF16), ("ag2i", [8 * 4096 * 64], BF16),
                        ("mall", [8, 128, 32], F32)):
        if nm in DEBUG:
            dbg[nm] = Buf(nc.dram_tensor("dbg_" + nm, shp, dt,
                                         kind="ExternalOutput").ap(), "dbg" + nm)

    es = contextlib.ExitStack()
    with es:
        def sbt(name, shape, dt, stack=None, g=None):
            return Buf((stack or es).enter_context(
                nc.sbuf_tensor(name, list(shape), dt)), name, g=g)

        resid = [sbt(f"res{t}", (128, D), F32, g="gx") for t in range(NT)]
        fmA = sbt("fmA", (128, 16384), BF16)
        tmA = sbt("tmA", (128, 16384), BF16)
        fs0 = sbt("fs0", (128, 2048), F32)
        fs1 = sbt("fs1", (128, 1024), F32)
        fs2 = sbt("fs2", (128, 1024), F32)
        fs3 = sbt("fs3", (128, 1024), F32)
        sbf0 = sbt("sbf0", (128, 4096), BF16)
        sbf1 = sbt("sbf1", (128, 2048), BF16)
        dump = sbt("dump", (128, 1024), BF16)
        stat = sbt("stat", (128, 80), F32)
        cst = sbt("cst", (128, 8), F32)
        idb = sbt("idb", (128, 128), BF16)
        idf = sbt("idf", (128, 128), F32)
        ones_b = sbt("ones_b", (1, 128), BF16)
        ones_bc = sbt("ones_bc", (128, 1), BF16)
        ones_f = sbt("ones_f", (128, 1), F32)
        ones_fr = sbt("ones_fr", (1, 128), F32)
        gamm = sbt("gamm", (128, 8), F32)
        pfs = sbt("pfs", (1, 1), I32)
        for _b in (idb, idf, gamm, pfs):
            _b.dsem = "gx"
            _b.shared = True
        psA = Buf(es.enter_context(nc.psum_tensor("psA", [128, 2048], F32)), "psA")
        psB = Buf(es.enter_context(nc.psum_tensor("psB", [128, 1536], F32)), "psB")
        psT = Buf(es.enter_context(nc.psum_tensor("psT", [128, 1024], BF16)), "psT")
        psAb = ([Buf(psA.t, f"psA_{i}") for i in range(4)]
                if "A" in BANKED else [psA] * 4)
        psBb = ([Buf(psB.t, f"psB_{i}") for i in range(3)]
                if "B" in BANKED else [psB] * 3)
        psTb = ([Buf(psT.t, f"psT_{i}") for i in range(2)]
                if "T" in BANKED else [psT] * 2)

        def bA(lo, hi):
            return psAb[lo // 512:(hi + 511) // 512]

        def bB(lo, hi):
            return psBb[lo // 512:(hi + 511) // 512]

        def bT(lo, hi):
            return psTb[lo // 512:(hi + 511) // 512]

        def dma(q, out, in_, reads=(), writes=()):
            K.emit(q, (lambda e, o=out, i=in_: e.dma_start(out=o, in_=i)),
                   reads=reads, writes=writes, dma=True)

        def dma2(q, outE, inE, outO, inO, reads=(), writes=()):
            K.emit2(q, (lambda e, o=outE, i=inE: e.dma_start(out=o, in_=i)),
                    (lambda e, o=outO, i=inO: e.dma_start(out=o, in_=i)),
                    reads=reads, writes=writes, dma=True)

        vop = lambda fn, reads=(), writes=(): K.emit("vector", fn, reads, writes)
        sop = lambda fn, reads=(), writes=(): K.emit("scalar", fn, reads, writes)
        top = lambda fn, reads=(), writes=(): K.emit("tensor", fn, reads, writes)
        gop = lambda fn, reads=(), writes=(): K.emit("gpsimd", fn, reads, writes)

        dma("sync", pfs[:, :], PF[:, :], reads=[PF], writes=[pfs])
        for e in ENGS:
            K.regload(e, pfs, pfs[0:1, 0:1])

        dma("gpsimd", idb[:, :], dr["ident"][:, :], reads=[dr["ident"]], writes=[idb])
        dma("sync", idf[:, :], dr["ident"][:, :], reads=[dr["ident"]], writes=[idf])
        dma("sync", gamm[:, :], dr["gam"][:, :], reads=[dr["gam"]], writes=[gamm])
        vop(lambda e: e.memset(ones_b[:, :], 1.0), writes=[ones_b])
        vop(lambda e: e.memset(ones_bc[:, :], 1.0), writes=[ones_bc])
        vop(lambda e: e.memset(ones_f[:, :], 1.0), writes=[ones_f])
        vop(lambda e: e.memset(ones_fr[:, :], 1.0), writes=[ones_fr])
        vop(lambda e: e.memset(cst[:, :], 0.0), writes=[cst])
        vop(lambda e: e.memset(cst[:, 1:2], 1e-5), writes=[cst])
        vop(lambda e: e.memset(cst[:, 2:3], EPS32 * 1e-5), writes=[cst])
        for t in range(NT):
            dma("sync", resid[t][:, :], X[128 * t:128 * (t + 1), :],
                reads=[X], writes=[resid[t]])

        def ln_stats(src_buf, src_ap, c0, rms_fuse=False):
            negm = stat[:, c0:c0 + 1]
            rr = stat[:, c0 + 1:c0 + 2]
            vop(lambda e: e.tensor_reduce(out=negm, in_=src_ap, axis=AXX, op=ALU.add,
                                          negate=True),
                reads=[src_buf], writes=[stat])
            vop(lambda e: e.tensor_scalar_mul(out=negm, in0=negm, scalar1=1.0 / D),
                reads=[stat], writes=[stat])
            sop(lambda e: e.activation(out=dump[:, :], in_=src_ap, func=AF.Square,
                                       scale=1.0, bias=negm, accum_out=rr),
                reads=[src_buf, stat], writes=[dump, stat])
            if rms_fuse:
                sc, bc = (1.0 + EPS32) / D, cst[:, 2:3]
            else:
                sc, bc = 1.0 / D, cst[:, 1:2]
            sop(lambda e, sc=sc, bc=bc: e.activation(out=rr, in_=rr, func=AF.Sqrt,
                                                     scale=sc, bias=bc),
                reads=[stat, cst], writes=[stat])
            vop(lambda e: e.reciprocal(out=rr, in_=rr), reads=[stat], writes=[stat])
            return negm, rr

        def mk_bias(negm, rr, c2):
            bu = stat[:, c2:c2 + 1]
            vop(lambda e: e.tensor_tensor(out=bu, in0=negm, in1=rr, op=ALU.mult),
                reads=[stat], writes=[stat])
            return bu

        def ln_add_resid(src_buf, src_ap, t):
            negm, rr = ln_stats(src_buf, src_ap, 64)
            bu = mk_bias(negm, rr, 66)
            sop(lambda e: e.activation(out=fs1[:, :], in_=src_ap, func=AF.Identity,
                                       scale=rr, bias=bu),
                reads=[src_buf, stat], writes=[fs1])
            vop(lambda e, t=t: e.tensor_tensor(out=resid[t][:, :], in0=resid[t][:, :],
                                               in1=fs1[:, :], op=ALU.add),
                reads=[fs1, resid[t]], writes=[resid[t]])

        def tp_tm_to_fm(src_buf, dst_buf, n_tiles=NT, scale_buf=None, dst_fm_stride=2048,
                        src_col=0, dst_col=0, reg_base=1024):
            for p in range(8):
                for g0 in range(0, n_tiles, 4):
                    n = min(4, n_tiles - g0)
                    rb = 512 * ((g0 // 4) % 2)
                    reg = psT[0:128, rb:rb + 512]
                    for j in range(n):
                        t = g0 + j
                        top(lambda e, t=t, p=p, j=j, reg=reg: e.transpose(
                            reg[:, 128 * j:128 * (j + 1)],
                            src_buf[:, src_col + 1024 * t + 128 * p:
                                    src_col + 1024 * t + 128 * (p + 1)], idb[:, :]),
                            reads=[src_buf, idb], writes=[psTb[rb // 512]])
                    c0 = dst_col + dst_fm_stride * p + 128 * g0
                    if scale_buf is not None:
                        sop(lambda e, c0=c0, n=n, reg=reg, p=p: e.activation(
                            out=dst_buf[:, c0:c0 + 128 * n], in_=reg[:, 0:128 * n],
                            func=AF.Identity, scale=scale_buf[:, p:p + 1],
                            bias=cst[:, 0:1]),
                            reads=[psTb[rb // 512], scale_buf, cst], writes=[dst_buf])
                    else:
                        sop(lambda e, c0=c0, n=n, reg=reg: e.copy(
                            out=dst_buf[:, c0:c0 + 128 * n], in_=reg[:, 0:128 * n]),
                            reads=[psTb[rb // 512]], writes=[dst_buf])

        def bcast(src_ap, dst_ap, reads, writes, scale=1.0):
            reg = psB[0:128, 1500:1501]
            top(lambda e: e.matmul(reg, ones_fr[0:1, :], src_ap, start=True, stop=True),
                reads=list(reads) + [ones_fr], writes=[psBb[2]])
            sop(lambda e: e.activation(out=dst_ap, in_=reg, func=AF.Identity,
                                       scale=scale, bias=cst[:, 0:1]),
                reads=[psBb[2], cst], writes=list(writes))

        if "l0" in PHASES:
            p0 = contextlib.ExitStack()
            pmk_s = sbt("pmk_s", (65, 32), BF16, p0)
            pmv_s = sbt("pmv_s", (4, 520), BF16, p0)
            m1s = sbt("m1s", (128, 2048), BF16, p0)
            m2s = sbt("m2s", (128, 128), BF16, p0)
            p0a = contextlib.ExitStack()
            ropc = sbt("ropc", (128, TOK), F32, p0a)
            rops = sbt("rops", (128, TOK), F32, p0a)
            wsl = [sbt(f"wsl{i}", (128, 1024), BF16, p0a) for i in range(4)]
            for _b in (ropc, rops, pmk_s, pmv_s, m1s, m2s):
                _b.dsem = "gp0"
                _b.shared = True
            dma("sync", ropc[:, :], dr["ropecos"][:, :], reads=[dr["ropecos"]], writes=[ropc])
            dma("sync", rops[:, :], dr["ropesin"][:, :], reads=[dr["ropesin"]], writes=[rops])
            for u in range(8):
                dma("gpsimd", pmk_s[0:65, 4 * u:4 * (u + 1)], dr["pmk"][u],
                    reads=[dr["pmk"]], writes=[pmk_s])
                dma("gpsimd", pmv_s[0:4, 65 * u:65 * (u + 1)], dr["pmv"][u],
                    reads=[dr["pmv"]], writes=[pmv_s])
            for dlt in range(4):
                dma("gpsimd", m1s[:, 512 * dlt:512 * (dlt + 1)], dr["mask1"][dlt],
                    reads=[dr["mask1"]], writes=[m1s])
            dma("gpsimd", m2s[:, :], dr["mask2"][:, :], reads=[dr["mask2"]], writes=[m2s])

            for t in range(NT):
                negm, rr = ln_stats(resid[t], resid[t][:, :], 4 * t, rms_fuse=True)
                bu = mk_bias(negm, rr, 4 * t + 2)
                sop(lambda e, t=t, rr=rr, bu=bu: e.activation(
                    out=tmA[:, 1024 * t:1024 * (t + 1)], in_=resid[t][:, :],
                    func=AF.Identity, scale=rr, bias=bu),
                    reads=[resid[t], stat], writes=[tmA])
            tp_tm_to_fm(tmA, fmA, scale_buf=gamm)

            def ufm(p):
                return fmA[:, 2048 * p:2048 * (p + 1)]

            vop(lambda e: e.memset(sbf0[0:1, 0:2048], 1.0), writes=[sbf0])
            ag1k = ag1i[0:AG1K].rearrange("(h r c) -> h r c", h=16, r=65, c=TOK)
            ag1v = ag1i[AG1K:AG1K + AG1V].rearrange("(t p d) -> t p d", t=NT, p=128, d=1024)
            ag1q = ag1i[AG1K + AG1V:AG1].rearrange("(h r c) -> h r c", h=16, r=64, c=TOK)
            for h in range(16):
                dma("sync", ag1k[h, 64:65, :], sbf0[0:1, 0:TOK],
                    reads=[sbf0], writes=[ag1i])

            for m in range(8):
                sl = wsl[m % 2]
                dma("gpsimd", sl[:, :], dr["w_v"][m], reads=[dr["w_v"]], writes=[sl])
                for kk in range(8):
                    for tb in range(4):
                        top(lambda e, sl=sl, kk=kk, tb=tb: e.matmul(
                            psA[0:128, 512 * tb:512 * (tb + 1)],
                            sl[:, 128 * kk:128 * (kk + 1)],
                            ufm(kk)[:, 512 * tb:512 * (tb + 1)],
                            start=(kk == 0), stop=(kk == 7)),
                            reads=[sl, fmA], writes=[psAb[tb]])
                for tb in range(4):
                    sop(lambda e, tb=tb: e.copy(
                        out=sbf1[:, 512 * tb:512 * (tb + 1)],
                        in_=psA[0:128, 512 * tb:512 * (tb + 1)]),
                        reads=[psAb[tb]], writes=[sbf1])
                for g0 in range(0, 16, 4):
                    rb = 512 * ((g0 // 4) % 2)
                    reg = psT[0:128, rb:rb + 512]
                    for j in range(4):
                        t = g0 + j
                        top(lambda e, t=t, j=j, reg=reg: e.transpose(
                            reg[:, 128 * j:128 * (j + 1)],
                            sbf1[:, 128 * t:128 * (t + 1)], idb[:, :]),
                            reads=[sbf1, idb], writes=[psTb[rb // 512]])
                    sop(lambda e, g0=g0, reg=reg: e.copy(
                        out=sbf0[:, 2048 + 512 * (g0 // 4):2048 + 512 * (g0 // 4) + 512],
                        in_=reg),
                        reads=[psTb[rb // 512]], writes=[sbf0])
                for t in range(NT):
                    dma("sync", ag1v[t, :, 128 * m:128 * (m + 1)],
                        sbf0[:, 2048 + 128 * t:2048 + 128 * (t + 1)],
                        reads=[sbf0], writes=[ag1i])

            for qk, wname, agsec, scl in (("q", "w_q", ag1q, 0.125), ("k", "w_k", ag1k, 1.0)):
                for pr in range(4):
                    sle, slo = wsl[2], wsl[3]
                    dma("gpsimd", sle[:, :], dr[wname][pr], reads=[dr[wname]], writes=[sle])
                    dma("gpsimd", slo[:, :], dr[wname][pr + 4], reads=[dr[wname]], writes=[slo])
                    for hh in range(2):
                        for par_i, sl, dstf in ((0, sle, fs2), (1, slo, fs3)):
                            for tb in range(2):
                                rb = 512 * ((2 * par_i + tb) % 3)
                                reg = psB[0:128, rb:rb + 512]
                                for kk in range(8):
                                    top(lambda e, sl=sl, kk=kk, reg=reg, hh=hh, tb=tb: e.matmul(
                                        reg, sl[:, 128 * kk:128 * (kk + 1)],
                                        ufm(kk)[:, 1024 * hh + 512 * tb:1024 * hh + 512 * (tb + 1)],
                                        start=(kk == 0), stop=(kk == 7)),
                                        reads=[sl, fmA], writes=[psBb[rb // 512]])
                                sop(lambda e, reg=reg, dstf=dstf, tb=tb, scl=scl: e.activation(
                                    out=dstf[:, 512 * tb:512 * (tb + 1)], in_=reg,
                                    func=AF.Identity, scale=scl, bias=cst[:, 0:1]),
                                    reads=[psBb[rb // 512], cst], writes=[dstf])
                        cs_ = ropc[:, 1024 * hh:1024 * (hh + 1)]
                        sn_ = rops[:, 1024 * hh:1024 * (hh + 1)]
                        vop(lambda e, cs_=cs_: e.tensor_tensor(out=fs0[:, 0:1024], in0=fs2[:, :], in1=cs_, op=ALU.mult),
                            reads=[fs2, ropc], writes=[fs0])
                        vop(lambda e, sn_=sn_: e.tensor_tensor(out=fs1[:, :], in0=fs3[:, :], in1=sn_, op=ALU.mult),
                            reads=[fs3, rops], writes=[fs1])
                        vop(lambda e: e.tensor_tensor(out=sbf1[:, 0:1024], in0=fs0[:, 0:1024], in1=fs1[:, :], op=ALU.subtract),
                            reads=[fs0, fs1], writes=[sbf1])
                        vop(lambda e, cs_=cs_: e.tensor_tensor(out=fs0[:, 1024:2048], in0=fs3[:, :], in1=cs_, op=ALU.mult),
                            reads=[fs3, ropc], writes=[fs0])
                        vop(lambda e, sn_=sn_: e.tensor_tensor(out=fs1[:, :], in0=fs2[:, :], in1=sn_, op=ALU.mult),
                            reads=[fs2, rops], writes=[fs1])
                        vop(lambda e: e.tensor_tensor(out=sbf1[:, 1024:2048], in0=fs0[:, 1024:2048], in1=fs1[:, :], op=ALU.add),
                            reads=[fs0, fs1], writes=[sbf1])
                        for i in range(4):
                            h = 4 * pr + i
                            dma("sync", agsec[h, 0:32, 1024 * hh:1024 * (hh + 1)],
                                sbf1[32 * i:32 * (i + 1), 0:1024],
                                reads=[sbf1], writes=[ag1i])
                            dma("sync", agsec[h, 32:64, 1024 * hh:1024 * (hh + 1)],
                                sbf1[32 * i:32 * (i + 1), 1024:2048],
                                reads=[sbf1], writes=[ag1i])

            if "ag1i" in DEBUG:
                dma("sync", dbg["ag1i"][:], ag1i[:], reads=[ag1i], writes=[dbg["ag1i"]])
            K.emit("gpsimd", lambda e: e.collective_compute(
                "AllGather", ALU.bypass, replica_groups=PAIRS,
                ins=[ag1i[:]], outs=[ag1o[:]]),
                reads=[ag1i], writes=[ag1o], asyncamt=1)

            K.barrier()
            p0a.close()
            go1 = ag1o[:].rearrange("(s f) -> s f", s=2, f=AG1)
            KOF, QOF, VOF, PT0, PT1, YOF = 0, 4096, 8192, 10272, 12320, 14368
            khat = Buf(fmA.t, "khat")
            qhat = Buf(fmA.t, "qhat")
            vhat = Buf(fmA.t, "vhat")
            pT = [Buf(fmA.t, "pT0"), Buf(fmA.t, "pT1")]
            yst = Buf(fmA.t, "yst")
            for u in range(8):
                for s_ in range(2):
                    def ksrc(h, s_=s_):
                        return go1[s_, 65 * TOK * h:65 * TOK * (h + 1)].rearrange(
                            "(a b) -> a b", a=65, b=TOK)
                    dma2("sync", fmA[0:65, KOF + 2048 * s_:KOF + 2048 * (s_ + 1)], ksrc(u),
                         fmA[0:65, KOF + 2048 * s_:KOF + 2048 * (s_ + 1)], ksrc(u + 8),
                         reads=[ag1o], writes=[khat])

                    def qsrc(h, s_=s_):
                        return go1[s_, AG1K + AG1V + 64 * TOK * h:
                                   AG1K + AG1V + 64 * TOK * (h + 1)].rearrange(
                            "(a b) -> a b", a=64, b=TOK)
                    dma2("sync", fmA[0:64, QOF + 2048 * s_:QOF + 2048 * (s_ + 1)], qsrc(u),
                         fmA[0:64, QOF + 2048 * s_:QOF + 2048 * (s_ + 1)], qsrc(u + 8),
                         reads=[ag1o], writes=[qhat])
                vop(lambda e: e.memset(fmA[:, VOF:VOF + 2080], 1.0), writes=[vhat])
                for kt in range(32):
                    s_ = kt // 16
                    tloc = kt % 16

                    def vsrc(h, s_=s_, tloc=tloc):
                        return go1[s_, AG1K + 1024 * 128 * tloc:
                                   AG1K + 1024 * 128 * (tloc + 1)].rearrange(
                            "(p d) -> p d", p=128, d=1024)[:, 64 * h:64 * (h + 1)]
                    dma2("sync", fmA[:, VOF + 65 * kt:VOF + 65 * kt + 64], vsrc(u),
                         fmA[:, VOF + 65 * kt:VOF + 65 * kt + 64], vsrc(u + 8),
                         reads=[ag1o], writes=[vhat])
                for qt in range(32):
                    ncc = qt // 4 + 1
                    for cc in range(ncc):
                        rb = 512 * (cc % 3)
                        reg = psA[0:128, rb:rb + 512]
                        diag = (cc == qt // 4)
                        top(lambda e, qt=qt, cc=cc, reg=reg, diag=diag: e.matmul(
                            reg, fmA[0:64, QOF + 128 * qt:QOF + 128 * (qt + 1)],
                            fmA[0:64, KOF + 512 * cc:KOF + 512 * (cc + 1)],
                            start=True, stop=not diag),
                            reads=[qhat, khat], writes=[psAb[rb // 512]])
                        if diag:
                            top(lambda e, qt=qt, reg=reg: e.matmul(
                                reg, idb[:, :], m1s[:, 512 * (qt % 4):512 * (qt % 4 + 1)],
                                start=False, stop=True),
                                reads=[idb, m1s], writes=[psAb[rb // 512]])
                        if cc == 0:
                            vop(lambda e, qt=qt, reg=reg: e.tensor_reduce(
                                out=fs1[:, qt:qt + 1], in_=reg, axis=AXX, op=ALU.max),
                                reads=[psAb[rb // 512]], writes=[fs1])
                        else:
                            vop(lambda e, reg=reg: e.tensor_reduce(
                                out=fs2[:, 0:1], in_=reg, axis=AXX, op=ALU.max),
                                reads=[psAb[rb // 512]], writes=[fs2])
                            vop(lambda e, qt=qt: e.tensor_tensor(
                                out=fs1[:, qt:qt + 1], in0=fs1[:, qt:qt + 1],
                                in1=fs2[:, 0:1], op=ALU.max),
                                reads=[fs2, fs1], writes=[fs1])
                    regp = psA[0:128, 1536:1540]
                    top(lambda e, qt=qt, u=u, regp=regp: e.matmul(
                        regp, fmA[0:64, QOF + 128 * qt:QOF + 128 * (qt + 1)],
                        pmk_s[0:64, 4 * u:4 * (u + 1)], start=True, stop=True),
                        reads=[qhat, pmk_s], writes=[psAb[3]])
                    vop(lambda e, regp=regp: e.tensor_reduce(
                        out=fs2[:, 0:1], in_=regp, axis=AXX, op=ALU.max),
                        reads=[psAb[3]], writes=[fs2])
                    vop(lambda e, qt=qt: e.tensor_tensor(
                        out=fs1[:, qt:qt + 1], in0=fs1[:, qt:qt + 1], in1=fs2[:, 0:1],
                        op=ALU.max),
                        reads=[fs2, fs1], writes=[fs1])
                vop(lambda e: e.tensor_scalar_mul(out=fs1[:, 0:32], in0=fs1[:, 0:32],
                                                  scalar1=-1.0),
                    reads=[fs1], writes=[fs1])
                if "mall" in DEBUG:
                    dma("sync", dbg["mall"][u], fs1[:, 0:32], reads=[fs1],
                        writes=[dbg["mall"]])
                top(lambda e: e.transpose(psB[0:32, 0:128], fs1[:, 0:32], idf[:, :]),
                    reads=[fs1, idf], writes=[psBb[0]])
                sop(lambda e: e.copy(out=sbf1[0:32, 0:128], in_=psB[0:32, 0:128]),
                    reads=[psBb[0]], writes=[sbf1])
                dma("scalar", fmA[64:65, QOF:QOF + 4096], sbf1[0:32, 0:128],
                    reads=[sbf1], writes=[qhat])
                for (q0, q1) in ((0, 12), (12, 24), (24, 32)):
                    W = 128 * (q1 - q0)
                    nch = (W + 511) // 512
                    pbase = QOF + 128 * q0
                    for j in range(nch):
                        lo, hi = 512 * j, min(512 * (j + 1), W)
                        top(lambda e, lo=lo, hi=hi, u=u, pbase=pbase: e.matmul(
                            psA[0:4, lo:hi],
                            pmk_s[0:65, 4 * u:4 * (u + 1)],
                            fmA[0:65, pbase + lo:pbase + hi],
                            start=True, stop=True),
                            reads=[pmk_s, qhat], writes=bA(lo, hi))
                    sop(lambda e, W=W: e.activation(out=sbf0[0:4, 0:W], in_=psA[0:4, 0:W],
                                               func=AF.Exp),
                        reads=bA(0, W), writes=[sbf0])
                    for j in range(nch):
                        lo, hi = 512 * j, min(512 * (j + 1), W)
                        top(lambda e, lo=lo, hi=hi, u=u: e.matmul(
                            psB[0:65, lo:hi],
                            pmv_s[0:4, 65 * u:65 * (u + 1)],
                            sbf0[0:4, lo:hi], start=True, stop=False),
                            reads=[pmv_s, sbf0], writes=bB(lo, hi))
                    for kt in range(q1):
                        a = max(128 * (kt - q0), 0)
                        pt = pT[kt % 2]
                        PTO = PT0 if kt % 2 == 0 else PT1
                        j0 = a // 512
                        for j in range(j0, nch):
                            lo = max(a, 512 * j)
                            hi = min(512 * (j + 1), W)
                            dloc = 128 * (kt - q0)
                            diag = (kt >= q0) and (lo <= dloc < hi)
                            top(lambda e, kt=kt, lo=lo, hi=hi, pbase=pbase, diag=diag: e.matmul(
                                psA[0:128, lo:hi],
                                fmA[0:65, KOF + 128 * kt:KOF + 128 * (kt + 1)],
                                fmA[0:65, pbase + lo:pbase + hi],
                                start=True, stop=not diag),
                                reads=[khat, qhat], writes=bA(lo, hi))
                            if diag:
                                top(lambda e, dloc=dloc: e.matmul(
                                    psA[0:128, dloc:dloc + 128], idb[:, :], m2s[:, :],
                                    start=False, stop=True),
                                    reads=[idb, m2s], writes=bA(dloc, dloc + 128))
                        sop(lambda e, a=a, PTO=PTO, W=W: e.activation(
                            out=fmA[:, PTO + a:PTO + W], in_=psA[0:128, a:W],
                            func=AF.Exp),
                            reads=bA(a, W), writes=[pt])
                        for j in range(j0, nch):
                            lo = max(a, 512 * j)
                            hi = min(512 * (j + 1), W)
                            stop = (kt == min(q0 + 4 * (j + 1), q1) - 1)
                            top(lambda e, kt=kt, lo=lo, hi=hi, PTO=PTO, stop=stop: e.matmul(
                                psB[0:65, lo:hi],
                                fmA[:, VOF + 65 * kt:VOF + 65 * (kt + 1)],
                                fmA[:, PTO + lo:PTO + hi], start=False, stop=stop),
                                reads=[vhat, pt], writes=bB(lo, hi))
                    sop(lambda e, W=W: e.copy(out=fs0[0:65, 0:W], in_=psB[0:65, 0:W]),
                        reads=bB(0, W), writes=[fs0])
                    for g in range(q1 - q0):
                        rb = 1024 + 128 * (g % 4)
                        reg = psA[0:128, rb:rb + 65]
                        top(lambda e, g=g, reg=reg: e.transpose(
                            reg, fs0[0:65, 128 * g:128 * (g + 1)], idf[0:65, 0:65]),
                            reads=[fs0, idf], writes=[psAb[2]])
                        vop(lambda e, reg=reg: e.reciprocal(out=fs1[:, 32:33],
                                                            in_=reg[:, 64:65]),
                            reads=[psAb[2]], writes=[fs1])
                        sop(lambda e, reg=reg: e.activation(
                            out=fmA[:, YOF:YOF + 64], in_=reg[:, 0:64],
                            func=AF.Identity, scale=fs1[:, 32:33], bias=cst[:, 0:1]),
                            reads=[psAb[2], fs1, cst], writes=[yst])
                        dma("scalar",
                            ag2i[u, 128 * (q0 + g):128 * (q0 + g + 1), :],
                            fmA[:, YOF:YOF + 64], reads=[yst], writes=[ag2i])
            if "ag2i" in DEBUG:
                dma("sync", dbg["ag2i"][:], ag2i[:, :, :].rearrange("a b c -> (a b c)"),
                    reads=[ag2i], writes=[dbg["ag2i"]])
            K.emit("gpsimd", lambda e: e.collective_compute(
                "AllGather", ALU.bypass, replica_groups=PAIRS,
                ins=[ag2i[:, :, :]], outs=[ag2o[:, :, :, :]]),
                reads=[ag2i], writes=[ag2o], asyncamt=1)

            K.barrier()
            wor = sbt("wor", (128, 8192), BF16, p0)
            for kk in range(8):
                dma("gpsimd", wor[:, 1024 * kk:1024 * (kk + 1)], dr["w_out_r"][kk],
                    reads=[dr["w_out_r"]], writes=[wor])
            for t in range(NT):
                for h in range(16):
                    s_, u_ = h // 8, h % 8

                    def ysrc(par, s_=s_, u_=u_, t=t):
                        return ag2o[s_, u_, 2048 * par + 128 * t:2048 * par + 128 * (t + 1), :]
                    dma2("sync", tmA[:, 1024 * t + 64 * h:1024 * t + 64 * (h + 1)], ysrc(0),
                         tmA[:, 1024 * t + 64 * h:1024 * t + 64 * (h + 1)], ysrc(1),
                         reads=[ag2o], writes=[tmA])
            tp_tm_to_fm(tmA, fmA)
            for t in range(NT):
                for nh in range(2):
                    rb = 512 * ((2 * t + nh) % 3)
                    reg = psB[0:128, rb:rb + 512]
                    for kk in range(8):
                        top(lambda e, kk=kk, t=t, nh=nh, reg=reg: e.matmul(
                            reg, fmA[:, 2048 * kk + 128 * t:2048 * kk + 128 * (t + 1)],
                            wor[:, 1024 * kk + 512 * nh:1024 * kk + 512 * (nh + 1)],
                            start=(kk == 0), stop=(kk == 7)),
                            reads=[fmA, wor], writes=[psBb[rb // 512]])
                    sop(lambda e, nh=nh, reg=reg: e.copy(
                        out=fs0[:, 512 * nh:512 * (nh + 1)], in_=reg),
                        reads=[psBb[rb // 512]], writes=[fs0])
                ln_add_resid(fs0, fs0[:, 0:1024], t)
            if "x1" in TAPS:
                for t in range(NT):
                    dma("sync", taps["x1"][128 * t:128 * (t + 1), :], resid[t][:, :],
                        reads=[resid[t]], writes=[taps["x1"]])
            K.barrier()
            p0.close()

        if "moe" in PHASES:
            pm_ = contextlib.ExitStack()
            impv = sbt("impv", (128, 16), F32, pm_)
            absc = sbt("absc", (128, 16), F32, pm_)
            muc = sbt("muc", (128, 16), F32, pm_)
            impg = sbt("impg", (128, 128), F32, pm_)
            sc1 = sbt("sc1", (1, 16), F32, pm_)
            sc2 = sbt("sc2", (1, 16), F32, pm_)
            scb = sbt("scb", (128, 8), F32, pm_)
            wfx = sbt("wfx", (128, 8), F32, pm_)
            wfxs = sbt("wfxs", (128, 8), F32, pm_)
            noi = sbt("noi", (128, 128), F32, pm_)
            ptm = sbt("ptm", (128, 128), F32, pm_)
            p8 = sbt("p8", (8, 2048), BF16, pm_)
            sumc = sbt("sumc", (128, 8), F32, pm_)
            affws = sbt("affws", (128, 64), F32, pm_)
            b3f = sbt("b3f", (1, 8), F32, pm_)
            us8 = sbt("us8", (1, 8), F32, pm_)
            u8b = sbt("u8b", (1, 8), F32, pm_)
            gb2b = sbt("gb2b", (128, 4), F32, pm_)
            lnob = sbt("lnob", (128, 8), BF16, pm_)
            for _b in (wfx, noi):
                _b.dsem = "gpm"
                _b.shared = True
            dma("sync", wfx[:, :], dr["wfix"][:, :], reads=[dr["wfix"]], writes=[wfx])
            for t in range(NT):
                dma("sync", noi[:, 8 * t:8 * (t + 1)],
                    dr["noise"][128 * t:128 * (t + 1), :],
                    reads=[dr["noise"]], writes=[noi])
            for t in range(NT):
                negm, rr = ln_stats(resid[t], resid[t][:, :], 4 * t)
                bu = mk_bias(negm, rr, 4 * t + 2)
                sop(lambda e, t=t, rr=rr, bu=bu: e.activation(
                    out=dump[:, :], in_=resid[t][:, :], func=AF.Square,
                    scale=rr, bias=bu, accum_out=impv[:, t:t + 1]),
                    reads=[resid[t], stat], writes=[dump, impv])
                sop(lambda e, t=t, rr=rr, bu=bu: e.activation(
                    out=fs0[:, 0:1024], in_=resid[t][:, :], func=AF.Identity,
                    scale=rr, bias=bu),
                    reads=[resid[t], stat], writes=[fs0])
                vop(lambda e, t=t: e.tensor_reduce(
                    out=absc[:, t:t + 1], in_=fs0[:, 0:1024], axis=AXX, op=ALU.add,
                    apply_absolute_value=True),
                    reads=[fs0], writes=[absc])
            sop(lambda e: e.activation(out=impv[:, :], in_=impv[:, :], func=AF.Sqrt),
                reads=[impv], writes=[impv])
            for t in range(NT):
                dma("sync", ag3i[0:1, 128 * t:128 * (t + 1)], impv[:, t:t + 1],
                    reads=[impv], writes=[ag3i])
            K.emit("gpsimd", lambda e: e.collective_compute(
                "AllGather", ALU.bypass, replica_groups=FULLG,
                ins=[ag3i[:, :]], outs=[ag3o[:, :]]),
                reads=[ag3i], writes=[ag3o], asyncamt=1)
            dma("sync", impg[:, :],
                ag3o[:, :].rearrange("a (b c) -> (a b) c", b=16, c=128),
                reads=[ag3o], writes=[impg])
            vop(lambda e: e.tensor_reduce(out=stat[:, 72:73], in_=impg[:, :],
                                          axis=AXX, op=ALU.add),
                reads=[impg], writes=[stat])
            reg = psB[0:1, 1494:1495]
            top(lambda e, reg=reg: e.matmul(reg, stat[:, 72:73], ones_f[:, :],
                                            start=True, stop=True),
                reads=[stat, ones_f], writes=[psBb[2]])
            sop(lambda e, reg=reg: e.activation(out=sc1[0:1, 0:1], in_=reg,
                                                func=AF.Identity, scale=-1.0 / 16384.0,
                                                bias=cst[0:1, 0:1]),
                reads=[psBb[2], cst], writes=[sc1])
            bcast(sc1[0:1, 0:1], stat[:, 73:74], [sc1], [stat])
            sop(lambda e: e.activation(out=dump[:, 0:128], in_=impg[:, :], func=AF.Square,
                                       scale=1.0, bias=stat[:, 73:74],
                                       accum_out=stat[:, 72:73]),
                reads=[impg, stat], writes=[dump, stat])
            top(lambda e, reg=reg: e.matmul(reg, stat[:, 72:73], ones_f[:, :],
                                            start=True, stop=True),
                reads=[stat, ones_f], writes=[psBb[2]])
            sop(lambda e, reg=reg: e.activation(out=sc1[0:1, 1:2], in_=reg, func=AF.Sqrt,
                                                scale=1.0 / 16384.0, bias=cst[0:1, 0:1]),
                reads=[psBb[2], cst], writes=[sc1])
            vop(lambda e: e.tensor_scalar(out=sc1[0:1, 2:3], in0=sc1[0:1, 1:2],
                                          scalar1=0.1, scalar2=1.0, op0=ALU.mult,
                                          op1=ALU.add),
                reads=[sc1], writes=[sc1])
            bcast(sc1[0:1, 2:3], scb[:, 0:1], [sc1], [scb])
            vop(lambda e: e.memset(sc2[0:1, 0:1], 0.0), writes=[sc2])
            vop(lambda e: e.memset(sc2[0:1, 1:2], 32.0), writes=[sc2])
            for it in range(60):
                vop(lambda e: e.tensor_tensor(out=sc2[0:1, 2:3], in0=sc2[0:1, 0:1],
                                              in1=sc2[0:1, 1:2], op=ALU.add),
                    reads=[sc2], writes=[sc2])
                vop(lambda e: e.tensor_scalar_mul(out=sc2[0:1, 2:3], in0=sc2[0:1, 2:3],
                                                  scalar1=0.5),
                    reads=[sc2], writes=[sc2])
                bcast(sc2[0:1, 2:3], stat[:, 73:74], [sc2], [stat])
                vop(lambda e: e.tensor_scalar(out=fs1[:, 0:128], in0=impg[:, :],
                                              scalar1=stat[:, 73:74], scalar2=None,
                                              op0=ALU.is_ge),
                    reads=[impg, stat], writes=[fs1])
                vop(lambda e: e.tensor_reduce(out=stat[:, 72:73], in_=fs1[:, 0:128],
                                              axis=AXX, op=ALU.add),
                    reads=[fs1], writes=[stat])
                top(lambda e, reg=reg: e.matmul(reg, stat[:, 72:73], ones_f[:, :],
                                                start=True, stop=True),
                    reads=[stat, ones_f], writes=[psBb[2]])
                sop(lambda e, reg=reg: e.copy(out=sc2[0:1, 3:4], in_=reg),
                    reads=[psBb[2]], writes=[sc2])
                vop(lambda e: e.tensor_scalar(out=sc2[0:1, 4:5], in0=sc2[0:1, 3:4],
                                              scalar1=1640.0, scalar2=None,
                                              op0=ALU.is_ge),
                    reads=[sc2], writes=[sc2])
                vop(lambda e: e.tensor_tensor(out=sc2[0:1, 5:6], in0=sc2[0:1, 2:3],
                                              in1=sc2[0:1, 0:1], op=ALU.subtract),
                    reads=[sc2], writes=[sc2])
                vop(lambda e: e.tensor_tensor(out=sc2[0:1, 5:6], in0=sc2[0:1, 5:6],
                                              in1=sc2[0:1, 4:5], op=ALU.mult),
                    reads=[sc2], writes=[sc2])
                vop(lambda e: e.tensor_tensor(out=sc2[0:1, 0:1], in0=sc2[0:1, 0:1],
                                              in1=sc2[0:1, 5:6], op=ALU.add),
                    reads=[sc2], writes=[sc2])
                vop(lambda e: e.tensor_scalar(out=sc2[0:1, 6:7], in0=sc2[0:1, 4:5],
                                              scalar1=-1.0, scalar2=1.0, op0=ALU.mult,
                                              op1=ALU.add),
                    reads=[sc2], writes=[sc2])
                vop(lambda e: e.tensor_tensor(out=sc2[0:1, 5:6], in0=sc2[0:1, 2:3],
                                              in1=sc2[0:1, 1:2], op=ALU.subtract),
                    reads=[sc2], writes=[sc2])
                vop(lambda e: e.tensor_tensor(out=sc2[0:1, 5:6], in0=sc2[0:1, 5:6],
                                              in1=sc2[0:1, 6:7], op=ALU.mult),
                    reads=[sc2], writes=[sc2])
                vop(lambda e: e.tensor_tensor(out=sc2[0:1, 1:2], in0=sc2[0:1, 1:2],
                                              in1=sc2[0:1, 5:6], op=ALU.add),
                    reads=[sc2], writes=[sc2])
            bcast(sc2[0:1, 1:2], scb[:, 1:2], [sc2], [scb])
            for t in range(NT):
                vop(lambda e, t=t: e.tensor_scalar(out=muc[:, t:t + 1],
                                                   in0=impv[:, t:t + 1],
                                                   scalar1=scb[:, 1:2], scalar2=None,
                                                   op0=ALU.is_ge),
                    reads=[impv, scb], writes=[muc])
            vop(lambda e: e.tensor_scalar(out=absc[:, :], in0=absc[:, :], scalar1=1e-8,
                                          scalar2=None, op0=ALU.add),
                reads=[absc], writes=[absc])
            for t in range(NT):
                vop(lambda e, t=t: e.tensor_tensor(out=absc[:, t:t + 1],
                                                   in0=absc[:, t:t + 1], in1=scb[:, 0:1],
                                                   op=ALU.mult),
                    reads=[absc, scb], writes=[absc])
            vop(lambda e: e.reciprocal(out=absc[:, :], in_=absc[:, :]),
                reads=[absc], writes=[absc])
            vop(lambda e: e.tensor_tensor(out=muc[:, :], in0=muc[:, :], in1=absc[:, :],
                                          op=ALU.mult),
                reads=[absc, muc], writes=[muc])
            for t in range(NT):
                vop(lambda e, t=t: e.tensor_tensor(out=stat[:, 4 * t + 3:4 * t + 4],
                                                   in0=stat[:, 4 * t:4 * t + 1],
                                                   in1=stat[:, 4 * t + 1:4 * t + 2],
                                                   op=ALU.mult),
                    reads=[stat], writes=[stat])
                vop(lambda e, t=t: e.tensor_tensor(out=stat[:, 4 * t + 3:4 * t + 4],
                                                   in0=stat[:, 4 * t + 3:4 * t + 4],
                                                   in1=muc[:, t:t + 1], op=ALU.mult),
                    reads=[stat, muc], writes=[stat])
                vop(lambda e, t=t: e.tensor_tensor(out=stat[:, 4 * t + 2:4 * t + 3],
                                                   in0=stat[:, 4 * t + 1:4 * t + 2],
                                                   in1=muc[:, t:t + 1], op=ALU.mult),
                    reads=[stat, muc], writes=[stat])
                sop(lambda e, t=t: e.activation(
                    out=tmA[:, 1024 * t:1024 * (t + 1)], in_=resid[t][:, :],
                    func=AF.Identity, scale=stat[:, 4 * t + 2:4 * t + 3],
                    bias=stat[:, 4 * t + 3:4 * t + 4]),
                    reads=[resid[t], stat], writes=[tmA])
            K.barrier()
            tp_tm_to_fm(tmA, fmA)
            for p in range(8):
                vop(lambda e, p=p: e.tensor_reduce(
                    out=sumc[:, p:p + 1], in_=fmA[:, 2048 * p:2048 * (p + 1)],
                    axis=AXX, op=ALU.add),
                    reads=[fmA], writes=[sumc])
            dma("sync", ag4i[:, :], sumc[:, :], reads=[sumc], writes=[ag4i])
            K.emit("gpsimd", lambda e: e.collective_compute(
                "AllReduce", ALU.add, replica_groups=PAIRS,
                ins=[ag4i[:, :]], outs=[ag4o[:, :]]),
                reads=[ag4i], writes=[ag4o], asyncamt=1)
            dma("sync", sumc[:, :], ag4o[:, :], reads=[ag4o], writes=[sumc])
            for p in range(8):
                dma("sync", affws[:, 8 * p:8 * (p + 1)], dr["aff_w"][p],
                    reads=[dr["aff_w"]], writes=[affws])
            rega = psB[0:1, 1480:1488]
            for p in range(8):
                top(lambda e, p=p, rega=rega: e.matmul(
                    rega, sumc[:, p:p + 1], affws[:, 8 * p:8 * (p + 1)],
                    start=(p == 0), stop=(p == 7)),
                    reads=[sumc, affws], writes=[psBb[2]])
            sop(lambda e, rega=rega: e.activation(out=b3f[0:1, :], in_=rega,
                                                  func=AF.Identity, scale=0.3 / 4096.0,
                                                  bias=cst[0:1, 0:1]),
                reads=[psBb[2], cst], writes=[b3f])
            dma("sync", us8[0:1, :], dr["aff_b"][:, :], reads=[dr["aff_b"]], writes=[us8])
            vop(lambda e: e.tensor_scalar_mul(out=us8[0:1, :], in0=us8[0:1, :], scalar1=0.3),
                reads=[us8], writes=[us8])
            vop(lambda e: e.tensor_tensor(out=b3f[0:1, :], in0=b3f[0:1, :], in1=us8[0:1, :],
                                          op=ALU.add),
                reads=[us8, b3f], writes=[b3f])
            dma("sync", us8[0:1, :], dr["gb3"][:, :], reads=[dr["gb3"]], writes=[us8])
            vop(lambda e: e.tensor_tensor(out=b3f[0:1, :], in0=b3f[0:1, :], in1=us8[0:1, :],
                                          op=ALU.add),
                reads=[us8, b3f], writes=[b3f])
            dma("sync", us8[0:1, :], dr["usage"][:, :], reads=[dr["usage"]], writes=[us8])
            vop(lambda e: e.tensor_scalar(out=u8b[0:1, :], in0=us8[0:1, :], scalar1=1e-8,
                                          scalar2=None, op0=ALU.add),
                reads=[us8], writes=[u8b])
            vop(lambda e: e.tensor_reduce(out=sc1[0:1, 3:4], in_=u8b[0:1, :], axis=AXX,
                                          op=ALU.min),
                reads=[u8b], writes=[sc1])
            vop(lambda e: e.tensor_scalar(out=sc1[0:1, 4:5], in0=sc1[0:1, 3:4],
                                          scalar1=0.0, scalar2=None, op0=ALU.is_lt),
                reads=[sc1], writes=[sc1])
            vop(lambda e: e.tensor_scalar(out=u8b[0:1, :], in0=u8b[0:1, :], scalar1=1e-30,
                                          scalar2=None, op0=ALU.max),
                reads=[u8b], writes=[u8b])
            sop(lambda e: e.activation(out=u8b[0:1, :], in_=u8b[0:1, :], func=AF.Ln),
                reads=[u8b], writes=[u8b])
            vop(lambda e: e.tensor_tensor(out=u8b[0:1, :], in0=u8b[0:1, :], in1=us8[0:1, :],
                                          op=ALU.mult),
                reads=[us8, u8b], writes=[u8b])
            vop(lambda e: e.tensor_reduce(out=sc1[0:1, 5:6], in_=u8b[0:1, :], axis=AXX,
                                          op=ALU.add, negate=True),
                reads=[u8b], writes=[sc1])
            vop(lambda e: e.tensor_scalar(out=sc1[0:1, 6:7], in0=sc1[0:1, 5:6],
                                          scalar1=-0.01, scalar2=0.02, op0=ALU.mult,
                                          op1=ALU.add),
                reads=[sc1], writes=[sc1])
            vop(lambda e: e.tensor_scalar(out=sc1[0:1, 6:7], in0=sc1[0:1, 6:7],
                                          scalar1=1e-4, scalar2=None, op0=ALU.max),
                reads=[sc1], writes=[sc1])
            bcast(sc1[0:1, 6:7], scb[:, 2:3], [sc1], [scb])
            bcast(sc1[0:1, 4:5], scb[:, 3:4], [sc1], [scb])
            vop(lambda e: e.tensor_scalar(out=scb[:, 4:5], in0=scb[:, 3:4], scalar1=-1.0,
                                          scalar2=1.0, op0=ALU.mult, op1=ALU.add),
                reads=[scb], writes=[scb])
            vop(lambda e: e.tensor_scalar(out=wfxs[:, :], in0=wfx[:, :],
                                          scalar1=scb[:, 3:4], scalar2=None,
                                          op0=ALU.mult),
                reads=[wfx, scb], writes=[wfxs])
            pm_g = contextlib.ExitStack()
            gw1s = sbt("gw1s", (128, 8192), BF16, pm_g)
            gw2s = sbt("gw2s", (128, 4096), BF16, pm_g)
            gw3s = sbt("gw3s", (128, 32), BF16, pm_g)
            gb1b = sbt("gb1b", (1, 1024), BF16, pm_g)
            b3b = sbt("b3b", (1, 8), BF16, pm_g)
            lnx = sbt("lnx", (128, 16), F32, pm_g)
            bias2 = sbt("bias2", (128, 4), F32, pm_g)
            for _b in (gw1s, gw2s, gw3s, gb1b, lnx, gb2b):
                _b.dsem = "gpg"
                _b.shared = True
            for kk in range(8):
                dma("gpsimd", gw1s[:, 1024 * kk:1024 * (kk + 1)], dr["gw1"][kk],
                    reads=[dr["gw1"]], writes=[gw1s])
            for m in range(4):
                dma("gpsimd", gw2s[:, 1024 * m:1024 * (m + 1)], dr["gw2"][m],
                    reads=[dr["gw2"]], writes=[gw2s])
            for cc4 in range(4):
                dma("gpsimd", gw3s[:, 8 * cc4:8 * (cc4 + 1)], dr["gw3"][cc4],
                    reads=[dr["gw3"]], writes=[gw3s])
            dma("gpsimd", gb1b[0:1, :], dr["gb1r"][:, :], reads=[dr["gb1r"]], writes=[gb1b])
            dma("sync", gb2b[:, :], dr["gb2c"][:, :], reads=[dr["gb2c"]], writes=[gb2b])
            dma("sync", lnx[:, 0:8], dr["lnoc"][:, :], reads=[dr["lnoc"]], writes=[lnx])
            dma("sync", lnx[:, 8:16], dr["lnsc"][:, :], reads=[dr["lnsc"]], writes=[lnx])
            sop(lambda e: e.copy(out=lnob[:, :], in_=lnx[:, 0:8]),
                reads=[lnx], writes=[lnob])
            regc = psB[0:128, 1490:1491]
            for m in range(4):
                for cc8 in range(8):
                    top(lambda e, m=m, cc8=cc8: e.matmul(
                        regc, gw2s[:, 1024 * m + 128 * cc8:1024 * m + 128 * (cc8 + 1)],
                        lnob[:, cc8:cc8 + 1], start=(cc8 == 0), stop=(cc8 == 7)),
                        reads=[gw2s, lnob], writes=[psBb[2]])
                sop(lambda e, m=m: e.copy(out=bias2[:, m:m + 1], in_=regc),
                    reads=[psBb[2]], writes=[bias2])
            vop(lambda e: e.tensor_tensor(out=bias2[:, :], in0=bias2[:, :],
                                          in1=gb2b[:, :], op=ALU.add),
                reads=[gb2b, bias2], writes=[bias2])
            for m in range(4):
                for kk in range(8):
                    vop(lambda e, m=m, kk=kk: e.tensor_scalar(
                        out=gw2s[:, 1024 * m + 128 * kk:1024 * m + 128 * (kk + 1)],
                        in0=gw2s[:, 1024 * m + 128 * kk:1024 * m + 128 * (kk + 1)],
                        scalar1=lnx[:, 8 + kk:9 + kk], scalar2=None, op0=ALU.mult),
                        reads=[gw2s, lnx], writes=[gw2s])
            sop(lambda e: e.copy(out=b3b[0:1, :], in_=b3f[0:1, :]),
                reads=[b3f], writes=[b3b])
            for blk in range(4):
                for ti in range(4):
                    t = 4 * blk + ti
                    reg = psB[0:128, 0:1024]
                    for kk in range(8):
                        for nh in range(2):
                            top(lambda e, kk=kk, t=t, nh=nh, reg=reg: e.matmul(
                                reg[:, 512 * nh:512 * (nh + 1)],
                                fmA[:, 2048 * kk + 128 * t:2048 * kk + 128 * (t + 1)],
                                gw1s[:, 1024 * kk + 512 * nh:1024 * kk + 512 * (nh + 1)],
                                start=(kk == 0), stop=False),
                                reads=[fmA, gw1s], writes=[psBb[0], psBb[1]])
                    for nh in range(2):
                        top(lambda e, nh=nh, reg=reg: e.matmul(
                            reg[:, 512 * nh:512 * (nh + 1)], ones_b[0:1, :],
                            gb1b[0:1, 512 * nh:512 * (nh + 1)], start=False, stop=True),
                            reads=[ones_b, gb1b], writes=[psBb[0], psBb[1]])
                    sop(lambda e, reg=reg: e.activation(out=fs0[:, 0:1024], in_=reg,
                                                        func=AF.Silu),
                        reads=[psBb[0], psBb[1]], writes=[fs0])
                    negm, rr = ln_stats(fs0, fs0[:, 0:1024], 68)
                    bu = mk_bias(negm, rr, 70)
                    sop(lambda e, rr=rr, bu=bu: e.activation(
                        out=sbf1[:, 0:1024], in_=fs0[:, 0:1024], func=AF.Identity,
                        scale=rr, bias=bu),
                        reads=[fs0, stat], writes=[sbf1])
                    for p in range(8):
                        rb = 128 * (p % 8)
                        regt = psT[0:128, rb:rb + 128]
                        top(lambda e, p=p, regt=regt: e.transpose(
                            regt, sbf1[:, 128 * p:128 * (p + 1)], idb[:, :]),
                            reads=[sbf1, idb], writes=[psTb[(p % 8) // 4]])
                        sop(lambda e, p=p, ti=ti, regt=regt: e.copy(
                            out=sbf0[:, 512 * p + 128 * ti:512 * p + 128 * (ti + 1)],
                            in_=regt),
                            reads=[psTb[(p % 8) // 4]], writes=[sbf0])
                for m in range(4):
                    rb = 1024 * (m % 2)
                    reg2 = psA[0:128, rb:rb + 512]
                    for kk in range(8):
                        top(lambda e, m=m, kk=kk, reg2=reg2: e.matmul(
                            reg2, gw2s[:, 1024 * m + 128 * kk:1024 * m + 128 * (kk + 1)],
                            sbf0[:, 512 * kk:512 * (kk + 1)],
                            start=(kk == 0), stop=(kk == 7)),
                            reads=[gw2s, sbf0], writes=[psAb[rb // 512]])
                    sop(lambda e, m=m, reg2=reg2: e.activation(
                        out=sbf1[:, 512 * m:512 * (m + 1)], in_=reg2, func=AF.Silu,
                        scale=1.0, bias=bias2[:, m:m + 1]),
                        reads=[psAb[rb // 512]], writes=[sbf1])
                for ti in range(4):
                    t = 4 * blk + ti
                    regl = psB[0:128, 1024 + 8 * (ti % 2):1024 + 8 * (ti % 2) + 8]
                    for cc4 in range(4):
                        top(lambda e, cc4=cc4, ti=ti, regl=regl: e.matmul(
                            regl, sbf1[:, 512 * cc4 + 128 * ti:512 * cc4 + 128 * (ti + 1)],
                            gw3s[:, 8 * cc4:8 * (cc4 + 1)], start=(cc4 == 0), stop=False),
                            reads=[sbf1, gw3s], writes=[psBb[2]])
                    top(lambda e, regl=regl: e.matmul(
                        regl, ones_fr[0:1, :], b3f[0:1, :], start=False, stop=True),
                        reads=[ones_fr, b3f], writes=[psBb[2]])
                    vop(lambda e, t=t, regl=regl: e.scalar_tensor_tensor(
                        out=fs1[:, 0:8], in0=noi[:, 8 * t:8 * (t + 1)],
                        scalar=scb[:, 2:3], in1=regl, op0=ALU.mult, op1=ALU.add),
                        reads=[noi, scb, psBb[2]], writes=[fs1])
                    vop(lambda e: e.tensor_reduce(out=fs1[:, 8:9], in_=fs1[:, 0:8],
                                                  axis=AXX, op=ALU.max),
                        reads=[fs1], writes=[fs1])
                    vop(lambda e: e.tensor_scalar(out=fs1[:, 16:24], in0=fs1[:, 0:8],
                                                  scalar1=fs1[:, 8:9], scalar2=None,
                                                  op0=ALU.is_ge),
                        reads=[fs1], writes=[fs1])
                    vop(lambda e: e.scalar_tensor_tensor(
                        out=fs1[:, 16:24], in0=fs1[:, 16:24], scalar=-1e30,
                        in1=fs1[:, 0:8], op0=ALU.mult, op1=ALU.add),
                        reads=[fs1], writes=[fs1])
                    vop(lambda e: e.tensor_reduce(out=fs1[:, 9:10], in_=fs1[:, 16:24],
                                                  axis=AXX, op=ALU.max),
                        reads=[fs1], writes=[fs1])
                    vop(lambda e: e.tensor_scalar(out=fs1[:, 16:24], in0=fs1[:, 0:8],
                                                  scalar1=fs1[:, 9:10], scalar2=None,
                                                  op0=ALU.is_ge),
                        reads=[fs1], writes=[fs1])
                    vop(lambda e: e.tensor_scalar_mul(out=fs1[:, 10:11], in0=fs1[:, 8:9],
                                                      scalar1=-1.0 / 1.5),
                        reads=[fs1], writes=[fs1])
                    sop(lambda e: e.activation(out=fs1[:, 24:32], in_=fs1[:, 0:8],
                                               func=AF.Exp, scale=1.0 / 1.5,
                                               bias=fs1[:, 10:11]),
                        reads=[fs1], writes=[fs1])
                    vop(lambda e: e.tensor_tensor(out=fs1[:, 24:32], in0=fs1[:, 24:32],
                                                  in1=fs1[:, 16:24], op=ALU.mult),
                        reads=[fs1], writes=[fs1])
                    vop(lambda e: e.tensor_reduce(out=fs1[:, 11:12], in_=fs1[:, 24:32],
                                                  axis=AXX, op=ALU.add),
                        reads=[fs1], writes=[fs1])
                    vop(lambda e: e.reciprocal(out=fs1[:, 11:12], in_=fs1[:, 11:12]),
                        reads=[fs1], writes=[fs1])
                    vop(lambda e: e.tensor_scalar(out=fs1[:, 24:32], in0=fs1[:, 24:32],
                                                  scalar1=fs1[:, 11:12], scalar2=None,
                                                  op0=ALU.mult),
                        reads=[fs1], writes=[fs1])
                    vop(lambda e, t=t: e.tensor_scalar(out=ptm[:, 8 * t:8 * (t + 1)],
                                                       in0=fs1[:, 24:32],
                                                       scalar1=scb[:, 4:5], scalar2=None,
                                                       op0=ALU.mult),
                        reads=[fs1, scb], writes=[ptm])
                    vop(lambda e, t=t: e.tensor_tensor(out=ptm[:, 8 * t:8 * (t + 1)],
                                                       in0=ptm[:, 8 * t:8 * (t + 1)],
                                                       in1=wfxs[:, :], op=ALU.add),
                        reads=[wfxs, ptm], writes=[ptm])
                    sop(lambda e, t=t: e.copy(out=sbf1[:, 1920:1928],
                                              in_=ptm[:, 8 * t:8 * (t + 1)]),
                        reads=[ptm], writes=[sbf1])
                    regt = psT[0:8, 896:1024]
                    top(lambda e, regt=regt: e.transpose(
                        regt, sbf1[:, 1920:1928], idb[:, :]),
                        reads=[sbf1, idb], writes=[psTb[1]])
                    sop(lambda e, t=t, regt=regt: e.copy(
                        out=p8[0:8, 128 * t:128 * (t + 1)], in_=regt),
                        reads=[psTb[1]], writes=[p8])
            K.barrier()
            pm_g.close()
            b2as = sbt("b2as", (8, 1024), BF16, pm_)
            acc = [sbt(f"acc{i}", (128, 1024), F32, pm_) for i in range(4)]
            esl = [sbt(f"esl{i}", (128, 2048), BF16, pm_) for i in range(2)]
            ewo = [sbt(f"ewo{i}", (128, 1024), BF16, pm_) for i in range(2)]
            sgb = sbt("sgb", (128, 512), BF16, pm_)
            b2as.dsem = "geb"
            b2as.shared = True
            dma("gpsimd", b2as[0:8, :], dr["b2all"][:, :], reads=[dr["b2all"]], writes=[b2as])
            eb1 = {}
            for s in range(8):
                eb1[s] = sbt(f"eb1_{s}", (128, SLOT_H[s] // 128), F32, pm_, g="geb")
                dma("sync", eb1[s][:, :], dr[f"s{s}_b1"][:, :],
                    reads=[dr[f"s{s}_b1"]], writes=[eb1[s]])
                if SLOT_G[s] == "e1":
                    eb1[(s, 2)] = sbt(f"eb2_{s}", (128, 16), F32, pm_, g="geb")
                    dma("sync", eb1[(s, 2)][:, :], dr[f"s{s}_b2"][:, :],
                        reads=[dr[f"s{s}_b2"]], writes=[eb1[(s, 2)]])
            for blk in range(4):
                for st in range(4):
                    t = 4 * blk + st
                    rb = 1024 * (st % 2)
                    rega2 = psA[0:128, rb:rb + 1024]
                    for nh in range(2):
                        top(lambda e, t=t, nh=nh, rega2=rega2: e.matmul(
                            rega2[:, 512 * nh:512 * (nh + 1)],
                            p8[0:8, 128 * t:128 * (t + 1)],
                            b2as[0:8, 512 * nh:512 * (nh + 1)], start=True, stop=True),
                            reads=[p8, b2as], writes=bA(rb, rb + 1024))
                    vop(lambda e, t=t: e.tensor_scalar_mul(out=stat[:, 72:73],
                                                           in0=stat[:, 4 * t + 2:4 * t + 3],
                                                           scalar1=0.1),
                        reads=[stat], writes=[stat])
                    vop(lambda e, t=t: e.tensor_scalar_mul(out=stat[:, 73:74],
                                                           in0=stat[:, 4 * t + 3:4 * t + 4],
                                                           scalar1=0.1),
                        reads=[stat], writes=[stat])
                    vop(lambda e, t=t: e.tensor_scalar(out=fs0[:, 0:1024],
                                                       in0=resid[t][:, :],
                                                       scalar1=stat[:, 72:73],
                                                       scalar2=stat[:, 73:74],
                                                       op0=ALU.mult, op1=ALU.add),
                        reads=[resid[t], stat], writes=[fs0])
                    vop(lambda e, st=st, rega2=rega2: e.tensor_tensor(
                        out=acc[st][:, :], in0=fs0[:, 0:1024], in1=rega2, op=ALU.add),
                        reads=[fs0] + bA(rb, rb + 1024), writes=[acc[st]])
                for s in range(8):
                    g = SLOT_G[s]
                    nH = SLOT_H[s] // 128
                    w1src = dr[f"s{s}_w1"] if blk == 0 else stash[f"s{s}_w1"]
                    w1q = "gpsimd" if blk == 0 else "sync"
                    for m in range(nH):
                        sl = esl[m % 2]
                        dma(w1q, sl[:, 0:1024], w1src[m], reads=[w1src], writes=[sl])
                        if blk == 0:
                            dma("scalar", stash[f"s{s}_w1"][m], sl[:, 0:1024],
                                reads=[sl], writes=[stash[f"s{s}_w1"]])
                        rb = 512 * (m % 3)
                        regs1 = psB[0:128, rb:rb + 512]
                        for kk in range(8):
                            top(lambda e, sl=sl, kk=kk, blk=blk, regs1=regs1: e.matmul(
                                regs1, sl[:, 128 * kk:128 * (kk + 1)],
                                fmA[:, 2048 * kk + 512 * blk:2048 * kk + 512 * (blk + 1)],
                                start=(kk == 0), stop=(kk == 7)),
                                reads=[sl, fmA], writes=[psBb[rb // 512]])
                        if g == "e0":
                            sop(lambda e, s=s, m=m, regs1=regs1: e.activation(
                                out=tmA[:, 512 * m:512 * (m + 1)], in_=regs1,
                                func=AF.Gelu_apprx_tanh, scale=1.0,
                                bias=eb1[s][:, m:m + 1]),
                                reads=[psBb[rb // 512], eb1[s]], writes=[tmA])
                        elif g == "e1":
                            sop(lambda e, s=s, m=m, regs1=regs1: e.activation(
                                out=tmA[:, 512 * m:512 * (m + 1)], in_=regs1,
                                func=AF.Silu, scale=1.0, bias=eb1[s][:, m:m + 1]),
                                reads=[psBb[rb // 512], eb1[s]], writes=[tmA])
                        else:
                            sop(lambda e, s=s, m=m, regs1=regs1: e.activation(
                                out=fs1[:, 0:512], in_=regs1, func=AF.Silu, scale=1.0,
                                bias=eb1[s][:, m:m + 1]),
                                reads=[psBb[rb // 512], eb1[s]], writes=[fs1])
                            vop(lambda e, s=s, m=m: e.tensor_scalar_mul(
                                out=fs0[:, 0:1], in0=eb1[s][:, m:m + 1], scalar1=-1.0),
                                reads=[eb1[s]], writes=[fs0])
                            sop(lambda e, regs1=regs1: e.activation(
                                out=fs0[:, 512:1024], in_=regs1, func=AF.Exp,
                                scale=-1.0, bias=fs0[:, 0:1]),
                                reads=[psBb[rb // 512], fs0], writes=[fs0])
                            vop(lambda e: e.tensor_scalar(
                                out=fs0[:, 512:1024], in0=fs0[:, 512:1024],
                                scalar1=1.0, scalar2=None, op0=ALU.add),
                                reads=[fs0], writes=[fs0])
                            vop(lambda e: e.reciprocal(out=fs0[:, 512:1024],
                                                       in_=fs0[:, 512:1024]),
                                reads=[fs0], writes=[fs0])
                            vop(lambda e, m=m: e.tensor_tensor(
                                out=tmA[:, 512 * m:512 * (m + 1)], in0=fs1[:, 0:512],
                                in1=fs0[:, 512:1024], op=ALU.mult),
                                reads=[fs1, fs0], writes=[tmA])
                    hbase = 0
                    if g == "e1":
                        w2src = dr[f"s{s}_w2"] if blk == 0 else stash[f"s{s}_w2"]
                        for mm in range(16):
                            sl = esl[mm % 2]
                            dma(w1q, sl[:, :], w2src[mm], reads=[w2src], writes=[sl])
                            if blk == 0:
                                dma("scalar", stash[f"s{s}_w2"][mm], sl[:, :],
                                    reads=[sl], writes=[stash[f"s{s}_w2"]])
                            rb = 512 * (mm % 4)
                            regs2 = psA[0:128, rb:rb + 512]
                            for kk in range(16):
                                top(lambda e, sl=sl, kk=kk, regs2=regs2: e.matmul(
                                    regs2, sl[:, 128 * kk:128 * (kk + 1)],
                                    tmA[:, 512 * kk:512 * (kk + 1)],
                                    start=(kk == 0), stop=(kk == 15)),
                                    reads=[sl, tmA], writes=[psAb[rb // 512]])
                            sop(lambda e, s=s, mm=mm, regs2=regs2: e.activation(
                                out=tmA[:, 8192 + 512 * mm:8192 + 512 * (mm + 1)],
                                in_=regs2, func=AF.Silu, scale=1.0,
                                bias=eb1[(s, 2)][:, mm:mm + 1]),
                                reads=[psAb[rb // 512], eb1[(s, 2)]], writes=[tmA])
                        hbase = 8192
                        nK2 = 16
                    else:
                        nK2 = nH
                    wosrc = dr[f"s{s}_wo"] if blk == 0 else stash[f"s{s}_wo"]
                    for wave in range(2):
                        for kk in range(nK2):
                            wo = ewo[kk % 2]
                            dma(w1q if wave == 0 else "sync", wo[:, :],
                                (wosrc if wave == 0 or blk > 0 else stash[f"s{s}_wo"])[kk],
                                reads=[wosrc if wave == 0 or blk > 0 else stash[f"s{s}_wo"]],
                                writes=[wo])
                            if blk == 0 and wave == 0:
                                dma("scalar", stash[f"s{s}_wo"][kk], wo[:, :],
                                    reads=[wo], writes=[stash[f"s{s}_wo"]])
                            for sti in range(2):
                                st = 2 * wave + sti
                                rb = 1024 * sti
                                for nh in range(2):
                                    top(lambda e, wo=wo, kk=kk, st=st, nh=nh, rb=rb,
                                        hbase=hbase, nK2=nK2: e.matmul(
                                            psA[0:128, rb + 512 * nh:rb + 512 * (nh + 1)],
                                            tmA[:, hbase + 512 * kk + 128 * st:
                                                hbase + 512 * kk + 128 * (st + 1)],
                                            wo[:, 512 * nh:512 * (nh + 1)],
                                            start=(kk == 0), stop=(kk == nK2 - 1)),
                                        reads=[wo, tmA], writes=[psAb[(rb + 512 * nh) // 512]])
                        for sti in range(2):
                            st = 2 * wave + sti
                            t = 4 * blk + st
                            rb = 1024 * sti
                            for nh in range(2):
                                vop(lambda e, s=s, st=st, t=t, nh=nh, rb=rb: e.scalar_tensor_tensor(
                                    out=acc[st][:, 512 * nh:512 * (nh + 1)],
                                    in0=psA[0:128, rb + 512 * nh:rb + 512 * (nh + 1)],
                                    scalar=ptm[:, 8 * t + s:8 * t + s + 1],
                                    in1=acc[st][:, 512 * nh:512 * (nh + 1)],
                                    op0=ALU.mult, op1=ALU.add),
                                    reads=[psAb[(rb + 512 * nh) // 512], ptm, acc[st]],
                                    writes=[acc[st]])
                for st in range(4):
                    t = 4 * blk + st
                    ln_add_resid(acc[st], acc[st][:, :], t)
            if "x2" in TAPS:
                for t in range(NT):
                    dma("sync", taps["x2"][128 * t:128 * (t + 1), :], resid[t][:, :],
                        reads=[resid[t]], writes=[taps["x2"]])
            K.barrier()
            pm_.close()

        if "l1a" in PHASES:
            pa = contextlib.ExitStack()
            m01 = sbt("m01", (128, 128), F32, pa)
            pa_h = contextlib.ExitStack()
            kqs = sbt("kqs", (128, 1024), BF16, pa_h)
            kks = sbt("kks", (128, 1024), BF16, pa_h)
            wsl2 = [sbt(f"wsl2{i}", (128, 1024), BF16, pa_h) for i in range(2)]
            kfhs = [sbt("kfh", (128, 2048), BF16, pa_h),
                    sbt("kfhB", (128, 2048), BF16, pa_h)]
            kfms = [sbt("kfm", (128, 2048), BF16, pa_h),
                    sbt("kfmB", (128, 2048), BF16, pa_h)]
            qfms = [sbt("qfm", (128, 2048), BF16, pa_h),
                    sbt("qfmB", (128, 2048), BF16, pa_h)]
            sts = sbt("sts", (128, 1040), F32, pa_h)
            prf = sbt("prf", (128, 1040), F32, pa_h)
            Sb = sbt("Sb", (128, 65), BF16, pa_h)
            for _b in (kqs, kks, m01):
                _b.dsem = "gpa"
                _b.shared = True
            dma("gpsimd", kqs[0:64, :], dr["l1_kq"][:, :], reads=[dr["l1_kq"]], writes=[kqs])
            dma("gpsimd", kqs[64:128, :], dr["l1_kq"][:, :], reads=[dr["l1_kq"]], writes=[kqs])
            dma("gpsimd", kks[0:64, :], dr["l1_kk"][:, :], reads=[dr["l1_kk"]], writes=[kks])
            dma("gpsimd", kks[64:128, :], dr["l1_kk"][:, :], reads=[dr["l1_kk"]], writes=[kks])
            dma("sync", m01[:, :], dr["mask01"][:, :], reads=[dr["mask01"]], writes=[m01])
            for t in range(NT):
                negm, rr = ln_stats(resid[t], resid[t][:, :], 4 * t)
                bu = mk_bias(negm, rr, 4 * t + 2)
                sop(lambda e, t=t, rr=rr, bu=bu: e.activation(
                    out=tmA[:, 1024 * t:1024 * (t + 1)], in_=resid[t][:, :],
                    func=AF.Identity, scale=rr, bias=bu),
                    reads=[resid[t], stat], writes=[tmA])
            tp_tm_to_fm(tmA, fmA)
            K.barrier()

            def ufm2(p):
                return fmA[:, 2048 * p:2048 * (p + 1)]

            for m in range(8):
                sl = wsl2[m % 2]
                dma("gpsimd", sl[:, :], dr["l1_wv"][m], reads=[dr["l1_wv"]], writes=[sl])
                for kk in range(8):
                    for tb in range(4):
                        top(lambda e, sl=sl, kk=kk, tb=tb: e.matmul(
                            psA[0:128, 512 * tb:512 * (tb + 1)],
                            sl[:, 128 * kk:128 * (kk + 1)],
                            ufm2(kk)[:, 512 * tb:512 * (tb + 1)],
                            start=(kk == 0), stop=(kk == 7)),
                            reads=[sl, fmA], writes=[psAb[tb]])
                for tb in range(4):
                    sop(lambda e, tb=tb: e.copy(
                        out=sbf1[:, 512 * tb:512 * (tb + 1)],
                        in_=psA[0:128, 512 * tb:512 * (tb + 1)]),
                        reads=[psAb[tb]], writes=[sbf1])
                for g0 in range(0, 16, 4):
                    rb = 512 * ((g0 // 4) % 2)
                    reg = psT[0:128, rb:rb + 512]
                    for j in range(4):
                        t = g0 + j
                        top(lambda e, t=t, j=j, reg=reg: e.transpose(
                            reg[:, 128 * j:128 * (j + 1)],
                            sbf1[:, 128 * t:128 * (t + 1)], idb[:, :]),
                            reads=[sbf1, idb], writes=[psTb[rb // 512]])
                    sop(lambda e, m=m, g0=g0, reg=reg: e.copy(
                        out=tmA[:, 2048 * m + 128 * g0:2048 * m + 128 * (g0 + 4)],
                        in_=reg),
                        reads=[psTb[rb // 512]], writes=[tmA])

            def vtm(h, t):
                c0 = 2048 * (h // 2) + 128 * t + 64 * (h % 2)
                return tmA[:, c0:c0 + 64]

            def feat_maps(h, km_ap, kq_or_kk, dst_buf, dst2=None):
                hb = h % 2
                for g0 in range(0, 16, 2):
                    reg = psA[0:128, 1024 + 256 * ((g0 // 2) % 2):
                              1024 + 256 * ((g0 // 2) % 2) + 256]
                    for j in range(2):
                        t = g0 + j
                        top(lambda e, t=t, j=j, hb=hb, reg=reg, km_ap=km_ap,
                            kq_or_kk=kq_or_kk, h=h: e.matmul(
                            reg[:, 128 * j:128 * j + 64],
                            km_ap[64 * hb:64 * (hb + 1), 128 * t:128 * (t + 1)],
                            kq_or_kk[64 * hb:64 * (hb + 1), 64 * h:64 * (h + 1)],
                            start=True, stop=True),
                            reads=[sbf0, kq_or_kk], writes=[psAb[2]])
                    for j in range(2):
                        t = g0 + j
                        fb = reg[:, 128 * j:128 * j + 64]
                        vop(lambda e, fb=fb: e.tensor_reduce(
                            out=stat[:, 72:73], in_=fb, axis=AXX, op=ALU.max,
                            negate=True),
                            reads=[psAb[2]], writes=[stat])
                        vop(lambda e, fb=fb: e.tensor_reduce(
                            out=stat[:, 73:74], in_=fb, axis=AXX, op=ALU.min),
                            reads=[psAb[2]], writes=[stat])
                        sop(lambda e, fb=fb: e.activation(
                            out=dump[:, 0:64], in_=fb, func=AF.Exp, scale=1.0,
                            bias=stat[:, 72:73], accum_out=stat[:, 74:75]),
                            reads=[psAb[2], stat], writes=[dump, stat])
                        sop(lambda e, fb=fb: e.activation(
                            out=dump[:, 64:128], in_=fb, func=AF.Exp, scale=-1.0,
                            bias=stat[:, 73:74], accum_out=stat[:, 75:76]),
                            reads=[psAb[2], stat], writes=[dump, stat])
                        vop(lambda e: e.reciprocal(out=stat[:, 74:76], in_=stat[:, 74:76]),
                            reads=[stat], writes=[stat])
                        vop(lambda e, t=t: e.tensor_scalar(
                            out=dst_buf[:, 128 * t:128 * t + 64], in0=dump[:, 0:64],
                            scalar1=stat[:, 74:75], scalar2=1e-12, op0=ALU.mult,
                            op1=ALU.max),
                            reads=[dump, stat], writes=[dst_buf])
                        vop(lambda e, t=t: e.tensor_scalar(
                            out=dst_buf[:, 128 * t + 64:128 * (t + 1)],
                            in0=dump[:, 64:128],
                            scalar1=stat[:, 75:76], scalar2=1e-12, op0=ALU.mult,
                            op1=ALU.max),
                            reads=[dump, stat], writes=[dst_buf])

            for m in range(8):
                sl = wsl2[m % 2]
                dma("gpsimd", sl[:, :], dr["l1_wk"][m], reads=[dr["l1_wk"]], writes=[sl])
                for kk in range(8):
                    for tb in range(4):
                        top(lambda e, sl=sl, kk=kk, tb=tb: e.matmul(
                            psA[0:128, 512 * tb:512 * (tb + 1)],
                            sl[:, 128 * kk:128 * (kk + 1)],
                            ufm2(kk)[:, 512 * tb:512 * (tb + 1)],
                            start=(kk == 0), stop=(kk == 7)),
                            reads=[sl, fmA], writes=[psAb[tb]])
                for tb in range(4):
                    sop(lambda e, tb=tb: e.copy(
                        out=sbf0[:, 512 * tb:512 * (tb + 1)],
                        in_=psA[0:128, 512 * tb:512 * (tb + 1)]),
                        reads=[psAb[tb]], writes=[sbf0])
                for h in (2 * m, 2 * m + 1):
                    kfh = kfhs[h % 2]
                    feat_maps(h, sbf0[:, 0:2048], kks, kfh)
                    for ch in range(8):
                        regS = psA[0:128, 512 * (ch % 2):512 * (ch % 2) + 65]
                        for j in range(2):
                            t = 2 * ch + j
                            top(lambda e, t=t, h=h, j=j, regS=regS, kfh=kfh: e.matmul(
                                regS[:, 0:64], kfh[:, 128 * t:128 * (t + 1)],
                                vtm(h, t), start=(j == 0), stop=False),
                                reads=[kfh, tmA], writes=[psAb[ch % 2]])
                            top(lambda e, t=t, j=j, regS=regS, kfh=kfh: e.matmul(
                                regS[:, 64:65], kfh[:, 128 * t:128 * (t + 1)],
                                ones_bc[:, :], start=False, stop=(j == 1)),
                                reads=[kfh, ones_bc], writes=[psAb[ch % 2]])
                        sop(lambda e, regS=regS: e.copy(out=fs0[:, 0:65], in_=regS),
                            reads=[psAb[ch % 2]], writes=[fs0])
                        dma("sync", ag5i[h, ch], fs0[:, 0:65], reads=[fs0], writes=[ag5i])
            K.emit("gpsimd", lambda e: e.collective_compute(
                "AllGather", ALU.bypass, replica_groups=PAIRS,
                ins=[ag5i[:, :, :, :]], outs=[ag5o[:, :, :, :, :]]),
                reads=[ag5i], writes=[ag5o], asyncamt=1)

            for m in range(8):
                slk, slq = wsl2[0], wsl2[1]
                dma("gpsimd", slk[:, :], dr["l1_wk"][m], reads=[dr["l1_wk"]], writes=[slk])
                dma("gpsimd", slq[:, :], dr["l1_wq"][m], reads=[dr["l1_wq"]], writes=[slq])
                for par_i, sl in ((0, slk), (1, slq)):
                    for kk in range(8):
                        for tb in range(4):
                            top(lambda e, sl=sl, kk=kk, tb=tb: e.matmul(
                                psA[0:128, 512 * tb:512 * (tb + 1)],
                                sl[:, 128 * kk:128 * (kk + 1)],
                                ufm2(kk)[:, 512 * tb:512 * (tb + 1)],
                                start=(kk == 0), stop=(kk == 7)),
                                reads=[sl, fmA], writes=[psAb[tb]])
                    for tb in range(4):
                        sop(lambda e, par_i=par_i, tb=tb: e.copy(
                            out=sbf0[:, 2048 * par_i + 512 * tb:2048 * par_i + 512 * (tb + 1)],
                            in_=psA[0:128, 512 * tb:512 * (tb + 1)]),
                            reads=[psAb[tb]], writes=[sbf0])
                for h in (2 * m, 2 * m + 1):
                    kfh = kfhs[h % 2]
                    kfm = kfms[h % 2]
                    qfm = qfms[h % 2]
                    feat_maps(h, sbf0[:, 0:2048], kks, kfh)
                    feat_maps(h, sbf0[:, 2048:4096], kqs, sbf1)
                    for src, dst in ((kfh, kfm), (sbf1, qfm)):
                        for g0 in range(0, 16, 4):
                            rb = 512 * ((g0 // 4) % 2)
                            reg = psT[0:128, rb:rb + 512]
                            for j in range(4):
                                t = g0 + j
                                top(lambda e, t=t, j=j, src=src, reg=reg: e.transpose(
                                    reg[:, 128 * j:128 * (j + 1)],
                                    src[:, 128 * t:128 * (t + 1)], idb[:, :]),
                                    reads=[src, idb], writes=[psTb[rb // 512]])
                            sop(lambda e, g0=g0, dst=dst, reg=reg: e.copy(
                                out=dst[:, 128 * g0:128 * (g0 + 4)], in_=reg),
                                reads=[psTb[rb // 512]], writes=[dst])
                    for s_ in range(2):
                        for ch in range(8):
                            dma("sync", sts[:, 65 * (8 * s_ + ch):65 * (8 * s_ + ch + 1)],
                                ag5o[s_, h, ch], reads=[ag5o], writes=[sts])
                    vop(lambda e: e.memset(prf[:, 0:65], 0.0), writes=[prf])
                    for gch in range(1, 16):
                        vop(lambda e, gch=gch: e.tensor_tensor(
                            out=prf[:, 65 * gch:65 * (gch + 1)],
                            in0=prf[:, 65 * (gch - 1):65 * gch],
                            in1=sts[:, 65 * (gch - 1):65 * gch], op=ALU.add),
                            reads=[sts, prf], writes=[prf])
                    for j in range(8):
                        K.emit2("scalar",
                                (lambda e, j=j: e.copy(out=Sb[:, :], in_=prf[:, 65 * j:65 * (j + 1)])),
                                (lambda e, j=j: e.copy(out=Sb[:, :], in_=prf[:, 65 * (8 + j):65 * (9 + j)])),
                                reads=[prf], writes=[Sb])
                        t0 = 2 * j
                        for bi, (qi, kj) in enumerate(((0, 0), (1, 0), (1, 1))):
                            regA = psA[0:128, 1024 + 128 * (bi % 2):1024 + 128 * (bi % 2) + 128]
                            top(lambda e, qi=qi, kj=kj, t0=t0, regA=regA, kfm=kfm,
                                qfm=qfm: e.matmul(
                                regA, kfm[:, 128 * (t0 + kj):128 * (t0 + kj + 1)],
                                qfm[:, 128 * (t0 + qi):128 * (t0 + qi + 1)],
                                start=True, stop=True),
                                reads=[kfm, qfm], writes=[psAb[2]])
                            if qi == kj:
                                vop(lambda e, bi=bi, regA=regA, kfh=kfh: e.tensor_tensor(
                                    out=kfh[:, 128 * bi:128 * (bi + 1)], in0=regA,
                                    in1=m01[:, :], op=ALU.mult),
                                    reads=[psAb[2], m01], writes=[kfh])
                            else:
                                sop(lambda e, bi=bi, regA=regA, kfh=kfh: e.copy(
                                    out=kfh[:, 128 * bi:128 * (bi + 1)], in_=regA),
                                    reads=[psAb[2]], writes=[kfh])
                        for qi in range(2):
                            t = t0 + qi
                            rb = 1024 + 128 * qi
                            rego = psB[0:128, rb:rb + 65]
                            top(lambda e, t=t, rego=rego, qfm=qfm: e.matmul(
                                rego, qfm[:, 128 * t:128 * (t + 1)], Sb[:, :],
                                start=True, stop=False),
                                reads=[qfm, Sb], writes=[psBb[2]])
                            blocks = [(0, 0)] if qi == 0 else [(0, 1), (1, 2)]
                            nb = len(blocks)
                            for ib, (kjrel, bi) in enumerate(blocks):
                                kt_ = t0 + kjrel
                                top(lambda e, bi=bi, kt_=kt_, h=h, rego=rego,
                                    kfh=kfh: e.matmul(
                                    rego[:, 0:64], kfh[:, 128 * bi:128 * (bi + 1)],
                                    vtm(h, kt_), start=False, stop=False),
                                    reads=[kfh, tmA], writes=[psBb[2]])
                                top(lambda e, bi=bi, rego=rego, last=(ib == nb - 1),
                                    kfh=kfh: e.matmul(
                                    rego[:, 64:65], kfh[:, 128 * bi:128 * (bi + 1)],
                                    ones_bc[:, :], start=False, stop=last),
                                    reads=[kfh, ones_bc], writes=[psBb[2]])
                            vop(lambda e, rego=rego: e.tensor_scalar(
                                out=stat[:, 76:77], in0=rego[:, 64:65],
                                scalar1=1e-12, scalar2=None, op0=ALU.add),
                                reads=[psBb[2]], writes=[stat])
                            vop(lambda e: e.reciprocal(out=stat[:, 76:77],
                                                       in_=stat[:, 76:77]),
                                reads=[stat], writes=[stat])
                            sop(lambda e, rego=rego: e.activation(
                                out=dump[:, 0:64], in_=rego[:, 0:64], func=AF.Identity,
                                scale=stat[:, 76:77], bias=cst[:, 0:1]),
                                reads=[psBb[2], stat, cst], writes=[dump])
                            dma("sync", yspill.t[t, :, 64 * h:64 * (h + 1)],
                                dump[:, 0:64], reads=[dump], writes=[yspill])
            K.barrier()
            pa_h.close()
            wos = sbt("wos", (128, 8192), BF16, pa)
            for kk in range(8):
                dma("gpsimd", wos[:, 1024 * kk:1024 * (kk + 1)], dr["l1_wo"][kk],
                    reads=[dr["l1_wo"]], writes=[wos])
            for t in range(NT):
                dma("sync", tmA[:, 1024 * t:1024 * (t + 1)], yspill.t[t],
                    reads=[yspill], writes=[tmA])
            tp_tm_to_fm(tmA, fmA)
            for t in range(NT):
                for nh in range(2):
                    rb = 512 * ((2 * t + nh) % 3)
                    reg = psB[0:128, rb:rb + 512]
                    for kk in range(8):
                        top(lambda e, kk=kk, t=t, nh=nh, reg=reg: e.matmul(
                            reg, fmA[:, 2048 * kk + 128 * t:2048 * kk + 128 * (t + 1)],
                            wos[:, 1024 * kk + 512 * nh:1024 * kk + 512 * (nh + 1)],
                            start=(kk == 0), stop=(kk == 7)),
                            reads=[fmA, wos], writes=[psBb[rb // 512]])
                    sop(lambda e, nh=nh, reg=reg: e.copy(
                        out=fs0[:, 512 * nh:512 * (nh + 1)], in_=reg),
                        reads=[psBb[rb // 512]], writes=[fs0])
                ln_add_resid(fs0, fs0[:, 0:1024], t)
            if "x3" in TAPS:
                for t in range(NT):
                    dma("sync", taps["x3"][128 * t:128 * (t + 1), :], resid[t][:, :],
                        reads=[resid[t]], writes=[taps["x3"]])
            K.barrier()
            pa.close()

        if "l1f" in PHASES:
            pf_ = contextlib.ExitStack()
            fsl = [sbt(f"fsl{i}", (128, 4096), BF16, pf_) for i in range(2)]
            ftm = sbt("ftm", (128, 4096), F32, pf_)
            sgb2 = sbt("sgb2", (128, 512), BF16, pf_)
            fscl = sbt("fscl", (128, 136), F32, pf_)
            fscl.dsem = "gpf"
            for nm, c0 in (("f_s1", 0), ("f_s2", 32), ("f_sh1", 64), ("f_sh2", 96)):
                dma("sync", fscl[:, c0:c0 + 32], dr[nm][:, :], reads=[dr[nm]], writes=[fscl])
            dma("sync", fscl[:, 128:136], dr["f_s3"][:, :], reads=[dr["f_s3"]], writes=[fscl])
            for blk in range(4):
                for ti in range(4):
                    t = 4 * blk + ti
                    negm, rr = ln_stats(resid[t], resid[t][:, :], 64)
                    bu = mk_bias(negm, rr, 66)
                    sop(lambda e, t=t, rr=rr, bu=bu: e.activation(
                        out=sbf1[:, 0:1024], in_=resid[t][:, :], func=AF.Identity,
                        scale=rr, bias=bu),
                        reads=[resid[t], stat], writes=[sbf1])
                    for p in range(8):
                        rb = 128 * (p % 8)
                        regt = psT[0:128, rb:rb + 128]
                        top(lambda e, p=p, regt=regt: e.transpose(
                            regt, sbf1[:, 128 * p:128 * (p + 1)], idb[:, :]),
                            reads=[sbf1, idb], writes=[psTb[(p % 8) // 4]])
                        sop(lambda e, p=p, ti=ti, regt=regt: e.copy(
                            out=fmA[:, 512 * p + 128 * ti:512 * p + 128 * (ti + 1)],
                            in_=regt),
                            reads=[psTb[(p % 8) // 4]], writes=[fmA])
                for stage, wname, scol, dsti in (("w1", "f_w1", 0, 0), ("w2", "f_w2", 32, 1)):
                    for m in range(32):
                        sl = fsl[m % 2]
                        src = dr[wname] if blk == 0 else stash[wname]
                        dma("gpsimd" if blk == 0 else "sync", sl[:, 0:1024], src[m],
                            reads=[src], writes=[sl])
                        if blk == 0:
                            dma("scalar", stash[wname][m], sl[:, 0:1024],
                                reads=[sl], writes=[stash[wname]])
                        rb = 512 * (m % 3)
                        reg = psB[0:128, rb:rb + 512]
                        for kk in range(8):
                            top(lambda e, sl=sl, kk=kk, reg=reg: e.matmul(
                                reg, sl[:, 128 * kk:128 * (kk + 1)],
                                fmA[:, 512 * kk:512 * (kk + 1)],
                                start=(kk == 0), stop=(kk == 7)),
                                reads=[sl, fmA], writes=[psBb[rb // 512]])
                        if stage == "w1":
                            sop(lambda e, m=m, reg=reg: e.activation(
                                out=tmA[:, 512 * m:512 * (m + 1)], in_=reg,
                                func=AF.Silu, scale=fscl[:, m:m + 1], bias=cst[:, 0:1]),
                                reads=[psBb[rb // 512], fscl, cst], writes=[tmA])
                        else:
                            sop(lambda e, m=m, reg=reg: e.activation(
                                out=sgb2[:, 0:512], in_=reg, func=AF.Identity,
                                scale=fscl[:, 32 + m:33 + m], bias=cst[:, 0:1]),
                                reads=[psBb[rb // 512], fscl, cst], writes=[sgb2])
                            vop(lambda e, m=m: e.tensor_tensor(
                                out=tmA[:, 512 * m:512 * (m + 1)],
                                in0=tmA[:, 512 * m:512 * (m + 1)], in1=sgb2[:, 0:512],
                                op=ALU.mult),
                                reads=[sgb2, tmA], writes=[tmA])
                for stage, wname, scol, srcb, dstb in (
                        ("h1", "f_h1", 64, tmA, fmA), ("h2", "f_h2", 96, fmA, tmA)):
                    for m in range(32):
                        sl = fsl[m % 2]
                        src = dr[wname] if blk == 0 else stash[wname]
                        dma("gpsimd" if blk == 0 else "sync", sl[:, :], src[m],
                            reads=[src], writes=[sl])
                        if blk == 0:
                            dma("scalar", stash[wname][m], sl[:, :],
                                reads=[sl], writes=[stash[wname]])
                        rb = 512 * (m % 3)
                        reg = psB[0:128, rb:rb + 512]
                        for kk in range(32):
                            top(lambda e, sl=sl, kk=kk, reg=reg, srcb=srcb: e.matmul(
                                reg, sl[:, 128 * kk:128 * (kk + 1)],
                                srcb[:, 512 * kk:512 * (kk + 1)],
                                start=(kk == 0), stop=(kk == 31)),
                                reads=[sl, srcb], writes=[psBb[rb // 512]])
                        sop(lambda e, m=m, reg=reg, scol=scol, dstb=dstb: e.activation(
                            out=dstb[:, 512 * m:512 * (m + 1)], in_=reg, func=AF.Gelu,
                            scale=fscl[:, scol + m:scol + m + 1], bias=cst[:, 0:1]),
                            reads=[psBb[rb // 512], fscl, cst], writes=[dstb])
                for m in range(8):
                    sl = fsl[m % 2]
                    src = dr["f_w3"] if blk == 0 else stash["f_w3"]
                    dma("gpsimd" if blk == 0 else "sync", sl[:, :], src[m],
                        reads=[src], writes=[sl])
                    if blk == 0:
                        dma("scalar", stash["f_w3"][m], sl[:, :],
                            reads=[sl], writes=[stash["f_w3"]])
                    rb = 512 * (m % 2)
                    reg = psB[0:128, rb:rb + 512]
                    for kk in range(32):
                        top(lambda e, sl=sl, kk=kk, reg=reg: e.matmul(
                            reg, sl[:, 128 * kk:128 * (kk + 1)],
                            tmA[:, 512 * kk:512 * (kk + 1)],
                            start=(kk == 0), stop=(kk == 31)),
                            reads=[sl, tmA], writes=[psBb[rb // 512]])
                    sop(lambda e, m=m, reg=reg: e.activation(
                        out=fs0[:, 0:512], in_=reg, func=AF.Identity,
                        scale=fscl[:, 128 + m:129 + m], bias=cst[:, 0:1]),
                        reads=[psBb[rb // 512], fscl, cst], writes=[fs0])
                    for ti in range(4):
                        rb2 = 1024 + 128 * (ti % 2)
                        regt = psA[0:128, rb2:rb2 + 128]
                        top(lambda e, ti=ti, regt=regt: e.transpose(
                            regt, fs0[:, 128 * ti:128 * (ti + 1)], idf[:, :]),
                            reads=[fs0, idf], writes=[psAb[2]])
                        sop(lambda e, m=m, ti=ti, regt=regt: e.copy(
                            out=ftm[:, 1024 * ti + 128 * m:1024 * ti + 128 * (m + 1)],
                            in_=regt),
                            reads=[psAb[2]], writes=[ftm])
                for ti in range(4):
                    t = 4 * blk + ti
                    ln_add_resid(ftm, ftm[:, 1024 * ti:1024 * (ti + 1)], t)
            K.barrier()
            pf_.close()


        K.barrier()
        for t in range(NT):
            dma("sync", OUT[128 * t:128 * (t + 1), :], resid[t][:, :],
                reads=[resid[t]], writes=[OUT])
        K.barrier()

        sems = {s: es.enter_context(nc.semaphore(s)) for s in K.cnt}
        block = es.enter_context(nc.Block())
        K.finalize_into(block, sems)
    return nc


def kernel(**inputs):
    shared = build_host_inputs(inputs)
    noise = moe_noise()
    x = inputs["L0_attn_x"]
    in_maps = []
    for c in range(NCORE):
        b, hf = c >> 1, c & 1
        m = dict(shared)
        m["pmk"] = np.ascontiguousarray(shared["_pmk"][8 * hf:8 * (hf + 1)])
        m["pmv"] = np.ascontiguousarray(shared["_pmv"][8 * hf:8 * (hf + 1)])
        m.pop("_pmk"); m.pop("_pmv")
        m["x"] = np.ascontiguousarray(x[b, 2048 * hf:2048 * (hf + 1), :])
        m["noise"] = np.ascontiguousarray(noise[b, 2048 * hf:2048 * (hf + 1), :])
        cos, sin = rope_tables(2048 * hf)
        m["ropecos"] = cos
        m["ropesin"] = sin
        m["pflag"] = np.array([[hf]], dtype=np.int32)
        in_maps.append(m)
    nc = build_nc()
    res = run_bass_kernel_spmd(nc, in_maps, core_ids=list(range(NCORE)))
    out = np.zeros((4, 4096, 1024), np.float32)
    for c in range(NCORE):
        b, hf = c >> 1, c & 1
        out[b, 2048 * hf:2048 * (hf + 1), :] = np.asarray(res.results[c]["out"],
                                                          np.float32)
    return out
